# version 01
# baseline (speedup 1.0000x reference)
import math
import contextlib
import numpy as np
import ml_dtypes
import concourse.bass as bass
import concourse.mybir as mybir
from concourse.bass_utils import run_bass_kernel_spmd

F32 = mybir.dt.float32
BF16 = mybir.dt.bfloat16
AF = mybir.ActivationFunctionType
ALU = mybir.AluOpType
AX = mybir.AxisListType

NP = 128
D = 1024
TOK = 4096
SEQ = 8192
NH = 16
DH = 64
NPAIR = 8
I2 = 2048
NE = 8
PAN = 512
NBLK = 4
BT = 1024
EPS = 1e-5
KB = 1024


class Sem:
    def __init__(self, handle):
        self.h = handle
        self.n = 0


class Sched:
    ENGS = ("sync", "tensor", "vector", "scalar", "gpsimd")

    def __init__(self):
        self.steps = {e: [] for e in self.ENGS}
        self.last_wait = {e: {} for e in self.ENGS}

    def step(self, eng, fns, waits=(), inc=None, inc_k=1):
        if not isinstance(fns, (list, tuple)):
            fns = [fns]
        lw = self.last_wait[eng]
        eff = []
        for (s, v) in waits:
            if s is None or v <= 0:
                continue
            if lw.get(id(s), -1) >= v:
                continue
            lw[id(s)] = v
            eff.append((s, v))
        tok = None
        if inc is not None:
            inc.n += inc_k
            tok = (inc, inc.n)
        self.steps[eng].append((eff, list(fns), inc, inc_k))
        return tok

    def replay(self, eng_name, eng):
        for waits, fns, inc, inc_k in self.steps[eng_name]:
            for (s, v) in waits:
                eng.wait_ge(s.h, v)
            instr = None
            for f in fns:
                instr = f(eng)
            if inc is not None:
                instr.then_inc(inc.h, inc_k)


def build_program(upto="D", debug=False):
    nc = bass.Bass("TRN2", target_bir_lowering=False)

    dt = nc.dram_tensor
    x_own = dt("x_own", [TOK, D], F32, kind="ExternalInput").ap()
    x_oth = dt("x_oth", [TOK, D], F32, kind="ExternalInput").ap()
    wq_d = dt("wq", [D, D], F32, kind="ExternalInput").ap()
    wkv_d = dt("wkv", [D, 2 * D], F32, kind="ExternalInput").ap()
    wo_d = dt("wo", [D, D], F32, kind="ExternalInput").ap()
    bo_d = dt("bo", [1, D], F32, kind="ExternalInput").ap()
    wgT_d = dt("wgT", [D, I2], F32, kind="ExternalInput").ap()
    wuT_d = dt("wuT", [D, I2], F32, kind="ExternalInput").ap()
    wdT_d = dt("wdT", [I2, D], F32, kind="ExternalInput").ap()
    gw_d = dt("gw", [NE, D, I2], F32, kind="ExternalInput").ap()
    uw_d = dt("uw", [NE, D, I2], F32, kind="ExternalInput").ap()
    dw_d = dt("dw", [NE, I2, D], F32, kind="ExternalInput").ap()
    rwT_d = dt("rwT", [D, NE], F32, kind="ExternalInput").ap()
    rb_d = dt("rb", [1, NE], F32, kind="ExternalInput").ap()
    idb_d = dt("idb", [NP, NP], BF16, kind="ExternalInput").ap()
    idf_d = dt("idf", [NP, NP], F32, kind="ExternalInput").ap()

    y_d = dt("y", [TOK, D], F32, kind="ExternalOutput").ap()

    def scratch(name, shape, dtype):
        kind = "ExternalOutput" if debug else "Internal"
        return dt(name, shape, dtype, kind=kind).ap()

    KT_s = scratch("KT_s", [D, SEQ], BF16)
    V_s = scratch("V_s", [NH, SEQ, DH], BF16)
    X1_s = scratch("X1_s", [TOK, D], F32)
    H2T_s = scratch("H2T_s", [D, TOK], BF16)
    if debug:
        QT_dbg = dt("QT_dbg", [NPAIR, NP, TOK], BF16, kind="ExternalOutput").ap()
        AT_dbg = dt("AT_dbg", [NPAIR, NP, TOK], BF16, kind="ExternalOutput").ap()
        MB_dbg = dt("MB_dbg", [1, 8], F32, kind="ExternalOutput").ap()
        W_dbg = dt("W_dbg", [NP, 256], F32, kind="ExternalOutput").ap()

    al = nc.alloc_sbuf_tensor
    ones_col_b = al("ones_col_b", [NP, 1], BF16)
    ones_row_f = al("ones_row_f", [1, NP], F32)
    bias_t = al("bias_t", [NP, 1], F32)
    nacc = al("nacc", [1, 16], F32)
    w_res = al("w_res", [NP, 256], F32)
    bo_row = al("bo_row", [1, D], F32)
    rb_row = al("rb_row", [1, NE], F32)
    rwT_sb = al("rwT_sb", [NP, 8 * NE], F32)
    idb_sb = al("idb_sb", [NP, NP], BF16)
    idf_sb = al("idf_sb", [NP, NP], F32)

    AR_BYTES = 190 * KB
    arena = al("arena", [NP, AR_BYTES], mybir.dt.uint8)
    ar0 = nc.lookup_mloc(arena).addr
    _uid = [0]

    def at(off_kb, shape, dtype, n=None):
        def one(i):
            _uid[0] += 1
            off = ar0 + int(off_kb * KB) + i * shape[1] * mybir.dt.size(dtype)
            off = (off + 31) // 32 * 32
            return nc.alloc_sbuf_tensor_at(f"ar_{_uid[0]}", shape, dtype, offset=off)
        if n is None:
            return one(0)
        return [one(i) for i in range(n)]

    qt = at(0, [NP, TOK], BF16, n=NPAIR)
    a_x = at(64, [NP, D], F32, n=3)
    a_hn = at(76, [NP, D], BF16, n=2)
    a_hnT = at(80, [NP, 4096], BF16, n=2)
    a_kst = at(96, [NP, PAN], BF16, n=2)
    a_vst = at(98, [NP, D], BF16, n=2)
    a_sq = at(102, [NP, PAN], BF16, n=2)
    a_dump = at(104, [NP, D], F32)
    a_rows = at(108, [NP, 64], F32)
    wq_sb = at(108.25, [NP, 8 * KB], BF16)
    wkv_sb = at(124.25, [NP, 16 * KB], BF16)
    attn = at(64, [NP, TOK], BF16, n=NPAIR)
    b_k = at(128, [NP, SEQ], BF16, n=2)
    b_v = at(160, [NP, 64 * 66], BF16, n=2)
    b_pt = at(176.8, [NP, KB], BF16, n=2)
    b_bc = at(180.8, [64, PAN], F32)
    b_rs = at(182.8, [1, PAN], F32)
    wo_sb = at(128, [NP, 8 * KB], BF16)
    c_x = at(144, [NP, D], F32, n=2)
    c_x1 = at(152, [NP, D], F32, n=2)
    c_h2f = at(160, [NP, D], F32, n=2)
    c_h2T32 = at(168, [NP, D], F32, n=2)
    c_h2Tb = at(176, [NP, D], BF16, n=2)
    c_rows = at(180, [NP, 64], F32)
    c_rt = at(180.25, [NP, 64], F32)
    c_dump = at(180.5, [NP, D], F32)
    d_acc = at(0, [NP, 8 * KB], F32)
    d_h2b = at(32, [NP, 8 * KB], BF16)
    d_prod = at(48, [NP, 16 * KB], BF16)
    d_wg = at(80, [NP, 4 * KB], BF16, n=2)
    d_wu = at(96, [NP, 4 * KB], BF16, n=2)
    d_wd = at(112, [NP, 16 * KB], BF16)
    d_g = at(144, [NP, PAN], F32, n=2)

    pM0 = nc.alloc_psum_tensor("pM0", [NP, 1024], F32)
    pM1 = nc.alloc_psum_tensor("pM1", [NP, 1024], F32)
    pM2 = nc.alloc_psum_tensor("pM2", [NP, 1024], F32)
    pT = nc.alloc_psum_tensor("pT", [NP, 512], F32)
    pX = nc.alloc_psum_tensor("pX", [NP, 512], F32)
    PTB0 = pT[:].bitcast(BF16)
    PTB1 = pM2[:, 0:512].bitcast(BF16)

    mk = lambda nm: Sem(nc.alloc_semaphore(name=nm))
    s_pre = mk("s_pre")
    s_xb = [mk(f"s_xb{i}") for i in range(3)]
    s_kst = [mk(f"s_kst{i}") for i in range(2)]
    s_vst = [mk(f"s_vst{i}") for i in range(2)]
    s_kld = [mk(f"s_kld{i}") for i in range(2)]
    s_vld = [mk(f"s_vld{i}") for i in range(2)]
    s_wq, s_wkv, s_wo, s_wd = mk("s_wq"), mk("s_wkv"), mk("s_wo"), mk("s_wd")
    s_cx = [mk(f"s_cx{i}") for i in range(2)]
    s_x1 = [mk(f"s_x1{i}") for i in range(2)]
    s_h2 = [mk(f"s_h2{i}") for i in range(2)]
    s_wg = [mk(f"s_wg{i}") for i in range(2)]
    s_wu = [mk(f"s_wu{i}") for i in range(2)]
    s_acc = [mk(f"s_acc{i}") for i in range(2)]
    s_h2b = [mk(f"s_h2b{i}") for i in range(2)]
    s_y, s_pe, s_v, s_a = mk("s_y"), mk("s_pe"), mk("s_v"), mk("s_a")
    all_dma_sems = ([s_pre] + s_xb + s_kst + s_vst + s_kld + s_vld +
                    [s_wq, s_wkv, s_wo, s_wd] + s_cx + s_x1 + s_h2 +
                    s_wg + s_wu + s_acc + s_h2b + [s_y])
    sems = dict(all_dma=all_dma_sems, s_y=s_y, s_pe=s_pe, s_v=s_v, s_a=s_a)

    sc = Sched()

    sc.step("vector", lambda e: e.memset(ones_col_b[:], 1.0), inc=s_v)
    sc.step("vector", lambda e: e.memset(ones_row_f[:], 1.0), inc=s_v)
    sc.step("vector", lambda e: e.memset(nacc[:], 0.0), inc=s_v)
    ones_ready = (s_v, s_v.n)
    sc.step("sync", lambda e: e.dma_start(out=bo_row[:], in_=bo_d), inc=s_pre, inc_k=16)
    sc.step("sync", lambda e: e.dma_start(out=rb_row[:], in_=rb_d), inc=s_pre, inc_k=16)
    sc.step("sync", lambda e: e.dma_start(
        out=rwT_sb[:].rearrange("p (j e) -> p j e", j=8),
        in_=rwT_d.rearrange("(j p) e -> p j e", p=NP)), inc=s_pre, inc_k=16)
    sc.step("sync", lambda e: e.dma_start(out=idb_sb[:], in_=idb_d), inc=s_pre, inc_k=16)
    sc.step("sync", lambda e: e.dma_start(out=idf_sb[:], in_=idf_d), inc=s_pre, inc_k=16)
    small_ld = (s_pre, s_pre.n)
    sc.step("gpsimd", lambda e: e.dma_start(
        out=wq_sb[:].rearrange("p (r q) -> p r q", r=8),
        in_=wq_d.rearrange("(r p) q -> p r q", p=NP)), inc=s_wq, inc_k=16)
    wq_ld = (s_wq, s_wq.n)
    sc.step("gpsimd", lambda e: e.dma_start(
        out=wkv_sb[:].rearrange("p (r j) -> p r j", r=8),
        in_=wkv_d.rearrange("(r p) j -> p r j", p=NP)), inc=s_wkv, inc_k=16)
    wkv_ld = (s_wkv, s_wkv.n)

    RC = {"sum": 0, "m": 2, "msq": 4, "vep": 6, "var": 8, "ln": 10, "r": 12}

    def rcol(name, t):
        return a_rows[:, RC[name] + (t % 2): RC[name] + (t % 2) + 1]

    norm_tok = [None]
    tile_norm_done = {}
    tile_tp_done = {}
    tile_ev_done = {}
    panel_mm_done = {}
    kst_dma = {}
    kst_sq = {}
    vst_dma = {}
    ps_ring = [pM0[:, 0:512], pM0[:, 512:1024], pM1[:, 0:512], pM1[:, 512:1024]]
    ps_tok = [None, None, None, None]
    ps_i = [0]

    def next_ps():
        i = ps_i[0] % 4
        ps_i[0] += 1
        return i

    a_mm_end = [None]
    ASQ_guard = [None, None]
    pX_guard = [None]

    for t in range(64):
        xb = a_x[t % 3]
        src = (x_own if t < 32 else x_oth)
        tsl = (t % 32)
        tok_x = sc.step("sync", lambda e, s=src, q=tsl, xb=xb: e.dma_start(
            out=xb[:], in_=s[q * NP:(q + 1) * NP, :]),
            waits=[tile_norm_done.get(t - 3, (None, 0))],
            inc=s_xb[t % 3], inc_k=16)
        v1 = sc.step("vector", lambda e, xb=xb, t=t: e.reduce_sum(
            rcol("sum", t), xb[:], axis=AX.X), waits=[tok_x], inc=s_v)
        a1 = sc.step("scalar", lambda e, xb=xb, t=t: e.activation(
            a_dump[:], xb[:], AF.Square, accum_out=rcol("msq", t)),
            waits=[tok_x], inc=s_a)
        v2 = sc.step("vector", lambda e, t=t: e.tensor_scalar_mul(
            rcol("m", t), rcol("sum", t), 1.0 / D), waits=[v1], inc=s_v)
        v4 = sc.step("vector", lambda e, t=t: e.tensor_scalar(
            rcol("vep", t), rcol("msq", t), 1.0 / D, EPS, ALU.mult, ALU.add),
            waits=[a1], inc=s_v)
        v3 = sc.step("vector", lambda e, t=t: e.tensor_tensor(
            rcol("var", t), rcol("m", t), rcol("m", t), ALU.mult),
            waits=[v2], inc=s_v)
        v5 = sc.step("vector", lambda e, t=t: e.tensor_sub(
            rcol("vep", t), rcol("vep", t), rcol("var", t)),
            waits=[v3, v4], inc=s_v)
        a2 = sc.step("scalar", lambda e, t=t: e.activation(
            rcol("ln", t), rcol("vep", t), AF.Ln), waits=[v5], inc=s_a)
        a3 = sc.step("scalar", lambda e, t=t: e.activation(
            rcol("r", t), rcol("ln", t), AF.Exp, scale=-0.5),
            waits=[a2], inc=s_a)
        hb = a_hn[t % 2]
        v6 = sc.step("vector", lambda e, xb=xb, hb=hb, t=t: e.tensor_scalar(
            hb[:], xb[:], rcol("m", t), rcol("r", t), ALU.subtract, ALU.mult),
            waits=[a3, tile_tp_done.get(t - 2, (None, 0))], inc=s_v)
        tile_norm_done[t] = v6
        PTB = (PTB0 if t % 2 == 0 else PTB1)
        fns = [
            (lambda e, hb=hb, c=c, PTB=PTB: e.transpose(
                PTB[:, c * NP:(c + 1) * NP], hb[:, c * NP:(c + 1) * NP],
                idb_sb[:]))
            for c in range(8)
        ]
        jprev = panel_mm_done.get(t // 4 - 2, (None, 0))
        p1 = sc.step("tensor", fns, waits=[v6, tile_ev_done.get(t - 2, (None, 0)),
                                           small_ld], inc=s_pe)
        hT = a_hnT[(t // 4) % 2]
        ev = sc.step("vector", lambda e, hT=hT, t=t, PTB=PTB: e.tensor_copy(
            hT[:].rearrange("p (c u) -> p c u", c=8)[:, :, (t % 4) * NP:(t % 4 + 1) * NP],
            PTB[:].rearrange("p (c u) -> p c u", c=8)),
            waits=[p1, jprev], inc=s_v)
        tile_ev_done[t] = ev
        tile_tp_done[t] = p1

        if t % 4 != 3:
            continue
        j = t // 4
        own = j < 8
        hT = a_hnT[j % 2]
        pan_ready = ev
        last_mm = None

        def mm_group(lhs_fn, rhs_fn, extra_waits, n_k=8):
            nonlocal last_mm
            si = next_ps()
            ps = ps_ring[si]
            fns = [
                (lambda e, k=k, si=si: e.matmul(
                    ps_ring[si], lhs_fn(k), rhs_fn(k),
                    start=(k == 0), stop=(k == n_k - 1)))
                for k in range(n_k)
            ]
            w = [pan_ready] + extra_waits
            if ps_tok[si] is not None:
                w.append(ps_tok[si])
            tokm = sc.step("tensor", fns, waits=w, inc=s_pe)
            last_mm = tokm
            return si, tokm

        if own:
            jq = j
            for m in range(8):
                si, tokm = mm_group(
                    lambda k, m=m: wq_sb[:, k * KB + m * NP: k * KB + (m + 1) * NP],
                    lambda k, hT=hT: hT[:, k * PAN:(k + 1) * PAN], [wq_ld])
                tev = sc.step("vector", lambda e, m=m, jq=jq, si=si: e.tensor_copy(
                    qt[m][:, jq * PAN:(jq + 1) * PAN], ps_ring[si]),
                    waits=[tokm], inc=s_v)
                ps_tok[si] = tev
                sqb = a_sq[m % 2]
                tsq = sc.step("scalar", lambda e, m=m, jq=jq, sqb=sqb: e.activation(
                    sqb[:], qt[m][:, jq * PAN:(jq + 1) * PAN], AF.Square),
                    waits=[tev, ASQ_guard[m % 2] or (None, 0)], inc=s_a)
                tnm = sc.step("tensor", lambda e, sqb=sqb: e.matmul(
                    pX[0:1, 0:PAN], ones_col_b[:], sqb[:], start=True, stop=True),
                    waits=[tsq, ones_ready, pX_guard[0] or (None, 0)], inc=s_pe)
                ASQ_guard[m % 2] = tnm
                trm = sc.step("vector", lambda e: e.reduce_max(
                    nacc[0:1, 8:9], pX[0:1, 0:PAN], axis=AX.X),
                    waits=[tnm], inc=s_v)
                tmx = sc.step("vector", lambda e: e.tensor_max(
                    nacc[0:1, 0:1], nacc[0:1, 0:1], nacc[0:1, 8:9]),
                    waits=[trm], inc=s_v)
                pX_guard[0] = trm
                norm_tok[0] = tmx
        for c in range(8):
            si, tokm = mm_group(
                lambda k, c=c: wkv_sb[:, k * 2048 + c * NP: k * 2048 + (c + 1) * NP],
                lambda k, hT=hT: hT[:, k * PAN:(k + 1) * PAN], [wkv_ld])
            kb = a_kst[c % 2]
            tev = sc.step("vector", lambda e, kb=kb, si=si: e.tensor_copy(
                kb[:], ps_ring[si]),
                waits=[tokm, kst_dma.get((j, c - 2), kst_dma.get((j - 1, c + 6), (None, 0))),
                       kst_sq.get((j, c - 2), kst_sq.get((j - 1, c + 6), (None, 0)))],
                inc=s_v)
            ps_tok[si] = tev
            td = sc.step("sync", lambda e, kb=kb, c=c, j=j: e.dma_start(
                out=KT_s[c * NP:(c + 1) * NP, j * PAN:(j + 1) * PAN], in_=kb[:]),
                waits=[tev], inc=s_kst[c % 2], inc_k=16)
            kst_dma[(j, c)] = td
            sqb = a_sq[c % 2]
            tsq = sc.step("scalar", lambda e, kb=kb, sqb=sqb: e.activation(
                sqb[:], kb[:], AF.Square),
                waits=[tev, ASQ_guard[c % 2] or (None, 0)], inc=s_a)
            tnm = sc.step("tensor", lambda e, sqb=sqb: e.matmul(
                pX[0:1, 0:PAN], ones_col_b[:], sqb[:], start=True, stop=True),
                waits=[tsq, pX_guard[0] or (None, 0)], inc=s_pe)
            ASQ_guard[c % 2] = tnm
            last_mm = tnm
            trm = sc.step("vector", lambda e: e.reduce_max(
                nacc[0:1, 9:10], pX[0:1, 0:PAN], axis=AX.X),
                waits=[tnm], inc=s_v)
            tmx = sc.step("vector", lambda e: e.tensor_max(
                nacc[0:1, 1:2], nacc[0:1, 1:2], nacc[0:1, 9:10]),
                waits=[trm], inc=s_v)
            pX_guard[0] = trm
            kst_sq[(j, c)] = tsq
            norm_tok[0] = tmx
        for u in range(4):
            vb = a_vst[u % 2]
            toks = []
            for n in range(2):
                si, tokm = mm_group(
                    lambda k, u=u, hT=hT: hT[:, k * PAN + u * NP: k * PAN + (u + 1) * NP],
                    lambda k, n=n: wkv_sb[:, k * 2048 + KB + n * PAN: k * 2048 + KB + (n + 1) * PAN],
                    [wkv_ld])
                tev = sc.step("vector", lambda e, vb=vb, n=n, si=si: e.tensor_copy(
                    vb[:, n * PAN:(n + 1) * PAN], ps_ring[si]),
                    waits=[tokm, vst_dma.get((j, u - 2), vst_dma.get((j - 1, u + 2), (None, 0)))],
                    inc=s_v)
                ps_tok[si] = tev
                toks.append(tev)
            tt = j * 4 + u
            td = sc.step("sync", lambda e, vb=vb, tt=tt: e.dma_start(
                out=V_s[:, tt * NP:(tt + 1) * NP, :].rearrange("h p n -> p h n"),
                in_=vb[:].rearrange("p (h n) -> p h n", h=NH)),
                waits=toks, inc=s_vst[u % 2], inc_k=16)
            vst_dma[(j, u)] = td
        panel_mm_done[j] = last_mm

    a_scratch_done = [(s_kst[0], s_kst[0].n), (s_kst[1], s_kst[1].n),
                      (s_vst[0], s_vst[0].n), (s_vst[1], s_vst[1].n)]
    a_mm_end = panel_mm_done[15]

    bv1 = sc.step("vector", lambda e: e.tensor_tensor(
        nacc[0:1, 2:3], nacc[0:1, 0:1], nacc[0:1, 1:2], ALU.mult),
        waits=[norm_tok[0]], inc=s_v)
    ba1 = sc.step("scalar", lambda e: e.activation(
        nacc[0:1, 3:4], nacc[0:1, 2:3], AF.Ln), waits=[bv1], inc=s_a)
    ba2 = sc.step("scalar", lambda e: e.activation(
        nacc[0:1, 4:5], nacc[0:1, 3:4], AF.Exp, scale=0.5), waits=[ba1], inc=s_a)
    bv2 = sc.step("vector", lambda e: e.tensor_scalar_mul(
        nacc[0:1, 5:6], nacc[0:1, 4:5], -0.125), waits=[ba2], inc=s_v)
    bt1 = sc.step("tensor", lambda e: e.matmul(
        pX[:, 0:1], ones_row_f[:], nacc[0:1, 5:6], start=True, stop=True),
        waits=[bv2, pX_guard[0] or (None, 0)], inc=s_pe)
    bt2 = sc.step("vector", lambda e: e.tensor_copy(bias_t[:], pX[:, 0:1]),
                  waits=[bt1], inc=s_v)
    bias_ready = bt2
    pX_guard[0] = bt2

    if upto == "A":
        return _finish(nc, sc, sems, debug and dict(
            QT_dbg=QT_dbg, qt=qt, AT_dbg=None, attn=None,
            MB_dbg=MB_dbg, W_dbg=None, nacc=nacc, bias_t=bias_t, w_res=w_res),
            upto)

    kld = {}
    vld = {}
    for p in range(2):
        kld[p] = sc.step("sync", lambda e, p=p: e.dma_start(
            out=b_k[p][:], in_=KT_s[p * NP:(p + 1) * NP, :]),
            waits=a_scratch_done + [a_mm_end], inc=s_kld[p % 2], inc_k=16)
    for hh in range(2):
        vld[hh] = sc.step("sync", lambda e, hh=hh: e.dma_start(
            out=b_v[hh % 2][:].rearrange("p (c w) -> p c w", w=66)[:, :, 0:64],
            in_=V_s[hh].rearrange("(c p) n -> p c n", p=NP)),
            waits=a_scratch_done + [a_mm_end], inc=s_vld[hh % 2], inc_k=16)
    vmemset = {}
    for sl in range(2):
        vmemset[sl] = sc.step("vector", lambda e, sl=sl: e.memset(
            b_v[sl][:].rearrange("p (c w) -> p c w", w=66)[:, :, 64:65], 1.0),
            waits=[a_mm_end], inc=s_v)

    exp_tok = {}
    pv_tok = {}
    smm_end = {}
    norm_done = {}
    for p in range(NPAIR):
        for head in range(2):
            hh = 2 * p + head
            kbuf = b_k[p % 2]
            vbuf = b_v[hh % 2]
            rbase = 64 * head
            for q in range(8):
                for g in range(32):
                    Sp = (pM0 if g % 2 == 0 else pM1)[:]
                    w = [exp_tok.get((hh, q, g - 2), (None, 0)),
                         kld[p], vld[hh]]
                    if g == 0:
                        w.append(norm_done.get((hh, q - 1), (None, 0)))
                        if q == 0:
                            w.append(vmemset[hh % 2])
                            if hh > 0:
                                w.append(exp_tok[(hh - 1, 7, 31)])
                    fns = [
                        (lambda e, c=c, Sp=Sp, kbuf=kbuf, rbase=rbase, p=p, q=q, cc=cc:
                         e.matmul(Sp[:, cc * PAN:(cc + 1) * PAN],
                                  kbuf[rbase:rbase + 64, c * NP:(c + 1) * NP],
                                  qt[p][rbase:rbase + 64, q * PAN:(q + 1) * PAN],
                                  start=True, stop=True))
                        for cc, c in enumerate((2 * g, 2 * g + 1))
                    ]
                    tokm = sc.step("tensor", fns, waits=w, inc=s_pe)
                    smm_end[(hh, q, g)] = tokm
                    pb = b_pt[g % 2]
                    te = sc.step("scalar", lambda e, Sp=Sp, pb=pb: e.activation(
                        pb[:], Sp, AF.Exp, bias=bias_t[:, 0:1], scale=0.125),
                        waits=[tokm, bias_ready, pv_tok.get((hh, q, g - 2), (None, 0))],
                        inc=s_a)
                    exp_tok[(hh, q, g)] = te
                    pvw = [te]
                    if g == 0:
                        pvw.append(norm_done.get((hh, q - 1),
                                   norm_done.get((hh - 1, 7), (None, 0))))
                    fns = [
                        (lambda e, c=c, pb=pb, vbuf=vbuf, g=g, cc=cc: e.matmul(
                            pM2[0:65, 0:PAN],
                            vbuf[:, c * 66:c * 66 + 65],
                            pb[:, cc * PAN:(cc + 1) * PAN],
                            start=(g == 0 and cc == 0), stop=(g == 31 and cc == 1)))
                        for cc, c in enumerate((2 * g, 2 * g + 1))
                    ]
                    tpv = sc.step("tensor", fns, waits=pvw, inc=s_pe)
                    pv_tok[(hh, q, g)] = tpv
                n1 = sc.step("vector", lambda e: e.reciprocal(
                    b_rs[:], pM2[64:65, 0:PAN]),
                    waits=[pv_tok[(hh, q, 31)]], inc=s_v)
                n2 = sc.step("tensor", lambda e: e.matmul(
                    pX[0:64, 0:PAN], ones_row_f[0:1, 0:64], b_rs[:],
                    start=True, stop=True),
                    waits=[n1, pX_guard[0] or (None, 0)], inc=s_pe)
                n3 = sc.step("vector", lambda e: e.tensor_copy(
                    b_bc[:], pX[0:64, 0:PAN]), waits=[n2], inc=s_v)
                n4 = sc.step("vector", lambda e, p=p, rbase=rbase, q=q: e.tensor_tensor(
                    attn[p][rbase:rbase + 64, q * PAN:(q + 1) * PAN],
                    pM2[0:64, 0:PAN], b_bc[:], ALU.mult),
                    waits=[n3], inc=s_v)
                norm_done[(hh, q)] = n4
                pX_guard[0] = n3
            if hh + 2 < NH:
                vld[hh + 2] = sc.step("sync", lambda e, hh=hh: e.dma_start(
                    out=b_v[hh % 2][:].rearrange("p (c w) -> p c w", w=66)[:, :, 0:64],
                    in_=V_s[hh + 2].rearrange("(c p) n -> p c n", p=NP)),
                    waits=[pv_tok[(hh, 7, 31)]], inc=s_vld[hh % 2], inc_k=16)
        if p + 2 < NPAIR:
            kld[p + 2] = sc.step("sync", lambda e, p=p: e.dma_start(
                out=b_k[p % 2][:], in_=KT_s[(p + 2) * NP:(p + 3) * NP, :]),
                waits=[smm_end[(2 * p + 1, 7, 31)]], inc=s_kld[p % 2], inc_k=16)

    b_end_pe = pv_tok[(NH - 1, 7, 31)]
    b_end_v = norm_done[(NH - 1, 7)]

    if upto == "B":
        return _finish(nc, sc, sems, debug and dict(
            QT_dbg=QT_dbg, qt=qt, AT_dbg=AT_dbg, attn=attn,
            MB_dbg=MB_dbg, W_dbg=None, nacc=nacc, bias_t=bias_t, w_res=w_res),
            upto)

    wo_ld = sc.step("gpsimd", lambda e: e.dma_start(
        out=wo_sb[:].rearrange("p (r q) -> p r q", r=8),
        in_=wo_d.rearrange("(r p) q -> p r q", p=NP)),
        waits=[b_end_pe], inc=s_wo, inc_k=16)

    CC = {"sum": 0, "m": 2, "msq": 4, "vep": 6, "var": 8, "ln": 10, "r": 12}

    def ccol(name, t):
        return c_rows[:, CC[name] + (t % 2): CC[name] + (t % 2) + 1]

    x1_dma = {}
    h2_dma = {}
    x1_add = {}
    h2T_ev = {}
    rt_w = {}
    proj_tok = {}
    for t in range(32):
        xb = c_x[t % 2]
        tok_x = sc.step("sync", lambda e, t=t, xb=xb: e.dma_start(
            out=xb[:], in_=x_own[t * NP:(t + 1) * NP, :]),
            waits=[b_end_pe, x1_add.get(t - 2, (None, 0))],
            inc=s_cx[t % 2], inc_k=16)
        P = (pM0 if t % 2 == 0 else pM1)[:]
        fns = []
        for n in range(2):
            for k in range(8):
                fns.append(lambda e, n=n, k=k, t=t, P=P: e.matmul(
                    P[:, n * PAN:(n + 1) * PAN],
                    attn[k][:, t * NP:(t + 1) * NP],
                    wo_sb[:, k * KB + n * PAN: k * KB + (n + 1) * PAN],
                    start=(k == 0), stop=False))
            fns.append(lambda e, n=n, P=P: e.matmul(
                P[:, n * PAN:(n + 1) * PAN], ones_row_f[:],
                bo_row[0:1, n * PAN:(n + 1) * PAN], start=False, stop=True))
        tp = sc.step("tensor", fns,
                     waits=[wo_ld, small_ld, b_end_v, x1_add.get(t - 2, (None, 0))],
                     inc=s_pe)
        proj_tok[t] = tp
        x1b = c_x1[t % 2]
        tadd = sc.step("vector", lambda e, x1b=x1b, xb=xb, P=P: e.tensor_add(
            x1b[:], P, xb[:]),
            waits=[tp, tok_x, x1_dma.get(t - 2, (None, 0))], inc=s_v)
        x1_add[t] = tadd
        x1_dma[t] = sc.step("sync", lambda e, x1b=x1b, t=t: e.dma_start(
            out=X1_s[t * NP:(t + 1) * NP, :], in_=x1b[:]),
            waits=[tadd], inc=s_x1[t % 2], inc_k=16)
        v1 = sc.step("vector", lambda e, x1b=x1b, t=t: e.reduce_sum(
            ccol("sum", t), x1b[:], axis=AX.X), waits=[tadd], inc=s_v)
        a1 = sc.step("scalar", lambda e, x1b=x1b, t=t: e.activation(
            c_dump[:], x1b[:], AF.Square, accum_out=ccol("msq", t)),
            waits=[tadd], inc=s_a)
        v2 = sc.step("vector", lambda e, t=t: e.tensor_scalar_mul(
            ccol("m", t), ccol("sum", t), 1.0 / D), waits=[v1], inc=s_v)
        v4 = sc.step("vector", lambda e, t=t: e.tensor_scalar(
            ccol("vep", t), ccol("msq", t), 1.0 / D, EPS, ALU.mult, ALU.add),
            waits=[a1], inc=s_v)
        v3 = sc.step("vector", lambda e, t=t: e.tensor_tensor(
            ccol("var", t), ccol("m", t), ccol("m", t), ALU.mult),
            waits=[v2], inc=s_v)
        v5 = sc.step("vector", lambda e, t=t: e.tensor_sub(
            ccol("vep", t), ccol("vep", t), ccol("var", t)),
            waits=[v3, v4], inc=s_v)
        a2 = sc.step("scalar", lambda e, t=t: e.activation(
            ccol("ln", t), ccol("vep", t), AF.Ln), waits=[v5], inc=s_a)
        a3 = sc.step("scalar", lambda e, t=t: e.activation(
            ccol("r", t), ccol("ln", t), AF.Exp, scale=-0.5), waits=[a2], inc=s_a)
        h2b_ = c_h2f[t % 2]
        v6 = sc.step("vector", lambda e, h2b_=h2b_, x1b=x1b, t=t: e.tensor_scalar(
            h2b_[:], x1b[:], ccol("m", t), ccol("r", t), ALU.subtract, ALU.mult),
            waits=[a3, h2T_ev.get(t - 2, (None, 0))], inc=s_v)
        if t % 2 == 0:
            tps = [pM2[:, c * NP:(c + 1) * NP] for c in range(8)]
        else:
            tps = [pT[:, (c % 4) * NP:((c % 4) + 1) * NP] if c < 4
                   else pX[:, (c % 4) * NP:((c % 4) + 1) * NP] for c in range(8)]
        fns = [
            (lambda e, c=c, h2b_=h2b_, dst=tps[c]: e.transpose(
                dst, h2b_[:, c * NP:(c + 1) * NP], idf_sb[:]))
            for c in range(8)
        ]
        wtp = [v6, h2T_ev.get(t - 2, (None, 0))]
        if t % 2 == 1:
            wtp.append(pX_guard[0] or (None, 0))
            wtp.append(rt_w.get(t - 1, (None, 0)))
        tpt = sc.step("tensor", fns, waits=wtp, inc=s_pe)
        hT32 = c_h2T32[t % 2]
        if t % 2 == 0:
            ev1 = sc.step("vector", lambda e, hT32=hT32: e.tensor_copy(
                hT32[:], pM2[:]), waits=[tpt], inc=s_v)
        else:
            ev1a = sc.step("vector", lambda e, hT32=hT32: e.tensor_copy(
                hT32[:, 0:PAN], pT[:]), waits=[tpt], inc=s_v)
            ev1 = sc.step("vector", lambda e, hT32=hT32: e.tensor_copy(
                hT32[:, PAN:2 * PAN], pX[:]), waits=[tpt], inc=s_v)
            pX_guard[0] = ev1
        hTb = c_h2Tb[t % 2]
        ev2 = sc.step("vector", lambda e, hTb=hTb, hT32=hT32: e.tensor_copy(
            hTb[:], hT32[:]), waits=[ev1, h2_dma.get(t - 2, (None, 0))], inc=s_v)
        h2T_ev[t] = ev2
        h2_dma[t] = sc.step("sync", lambda e, hTb=hTb, t=t: e.dma_start(
            out=H2T_s.rearrange("(j p) u -> p j u", p=NP)[:, :, t * NP:(t + 1) * NP],
            in_=hTb[:].rearrange("p (j u) -> p j u", j=8)),
            waits=[ev2], inc=s_h2[t % 2], inc_k=16)
        fns = [
            (lambda e, jj=jj, hT32=hT32: e.matmul(
                pX[:, 0:NE], hT32[:, jj * NP:(jj + 1) * NP],
                rwT_sb[:, jj * NE:(jj + 1) * NE],
                start=(jj == 0), stop=False))
            for jj in range(8)
        ]
        fns.append(lambda e: e.matmul(pX[:, 0:NE], ones_row_f[:], rb_row[:],
                                      start=False, stop=True))
        wrt = [ev1, small_ld]
        if t % 2 == 0:
            wrt.append(pX_guard[0] or (None, 0))
        trt = sc.step("tensor", fns, waits=wrt, inc=s_pe)
        CL = lambda i: c_rt[:, 8 * i: 8 * i + 8]
        r0 = sc.step("vector", lambda e: e.tensor_copy(CL(0), pX[:, 0:NE]),
                     waits=[trt], inc=s_v)
        pX_guard[0] = r0
        r1 = sc.step("vector", lambda e: e.reduce_max(
            c_rt[:, 56:57], CL(0), axis=AX.X), waits=[r0], inc=s_v)
        r2 = sc.step("vector", lambda e: e.tensor_single_scalar(
            CL(1), CL(0), c_rt[:, 56:57], ALU.is_ge), waits=[r1], inc=s_v)
        r3 = sc.step("vector", lambda e: e.scalar_tensor_tensor(
            CL(2), CL(1), -1e30, CL(0), ALU.mult, ALU.add), waits=[r2], inc=s_v)
        r4 = sc.step("vector", lambda e: e.reduce_max(
            c_rt[:, 57:58], CL(2), axis=AX.X), waits=[r3], inc=s_v)
        r5 = sc.step("vector", lambda e: e.tensor_single_scalar(
            CL(3), CL(2), c_rt[:, 57:58], ALU.is_ge), waits=[r4], inc=s_v)
        r6 = sc.step("vector", lambda e: e.tensor_add(CL(4), CL(1), CL(3)),
                     waits=[r5], inc=s_v)
        r7 = sc.step("scalar", lambda e: e.activation(CL(5), CL(0), AF.Sigmoid),
                     waits=[r0], inc=s_a)
        r8 = sc.step("vector", lambda e: e.tensor_tensor(
            CL(6), CL(5), CL(4), ALU.mult), waits=[r6, r7], inc=s_v)
        r9 = sc.step("vector", lambda e: e.reduce_sum(
            c_rt[:, 58:59], CL(6), axis=AX.X), waits=[r8], inc=s_v)
        r10 = sc.step("vector", lambda e: e.reciprocal(
            c_rt[:, 59:60], c_rt[:, 58:59]), waits=[r9], inc=s_v)
        r11 = sc.step("vector", lambda e, t=t: e.tensor_scalar_mul(
            w_res[:, 8 * t: 8 * t + 8], CL(6), c_rt[:, 59:60]),
            waits=[r10], inc=s_v)
        rt_w[t] = r11

    c_end_pe = (s_pe, s_pe.n)
    c_end_v = (s_v, s_v.n)
    c_x1_dma = dict(x1_dma)
    c_h2_dma = dict(h2_dma)

    if upto == "C":
        return _finish(nc, sc, sems, debug and dict(
            QT_dbg=None, qt=None, AT_dbg=AT_dbg, attn=attn,
            MB_dbg=MB_dbg, W_dbg=W_dbg, nacc=nacc, bias_t=bias_t, w_res=w_res),
            upto)

    acc_ld = {}
    h2b_ld = {}
    y_st = {}
    gu_tok = {}
    prod_tok = {}
    dn_tok = {}
    stt_tok = {}
    wgld = {}
    wuld = {}
    wdld = {}
    down_end = {}

    for b in range(NBLK):
        wacc = [c_end_pe, c_end_v, c_x1_dma[8 * b + 7],
                c_x1_dma[8 * b + 6], y_st.get(b - 1, (None, 0))]
        acc_ld[b] = sc.step("sync", lambda e, b=b: e.dma_start(
            out=d_acc[:].rearrange("p (u d) -> p u d", u=8),
            in_=X1_s[b * BT:(b + 1) * BT, :].rearrange("(u p) d -> p u d", p=NP)),
            waits=wacc, inc=s_acc[b % 2], inc_k=16)
        h2b_ld[b] = sc.step("sync", lambda e, b=b: e.dma_start(
            out=d_h2b[:].rearrange("p (j u) -> p j u", j=8),
            in_=H2T_s[:, b * BT:(b + 1) * BT].rearrange("(j p) u -> p j u", p=NP)),
            waits=[c_h2_dma[8 * b + 7], c_h2_dma[8 * b + 6], c_end_pe,
                   gu_tok.get((b - 1, 8, 15, 1), (None, 0))],
            inc=s_h2b[b % 2], inc_k=16)
        for e_i in range(9):
            shared = (e_i == NE)
            gsrc = wgT_d if shared else gw_d[e_i]
            usrc = wuT_d if shared else uw_d[e_i]
            dsrc = wdT_d if shared else dw_d[e_i]
            def sched_wld(qd, when_tok):
                wgld[(b, e_i, qd)] = sc.step("gpsimd",
                    lambda e, qd=qd, gsrc=gsrc: e.dma_start(
                        out=d_wg[qd % 2][:].rearrange("p (r i) -> p r i", r=8),
                        in_=gsrc[:, qd * PAN:(qd + 1) * PAN].rearrange(
                            "(r p) i -> p r i", p=NP)),
                    waits=[when_tok, c_end_pe], inc=s_wg[qd % 2], inc_k=16)
                wuld[(b, e_i, qd)] = sc.step("gpsimd",
                    lambda e, qd=qd, usrc=usrc: e.dma_start(
                        out=d_wu[qd % 2][:].rearrange("p (r i) -> p r i", r=8),
                        in_=usrc[:, qd * PAN:(qd + 1) * PAN].rearrange(
                            "(r p) i -> p r i", p=NP)),
                    waits=[when_tok, c_end_pe], inc=s_wu[qd % 2], inc_k=16)

            def guard_gu(qd):
                key = (b, e_i, 4 * 0 + qd - 2)
                if qd >= 2:
                    return gu_tok.get((b, e_i, (qd - 2) * 4 + 3, 1), (None, 0))
                pe_i, pb = (e_i - 1, b) if e_i > 0 else (8, b - 1)
                if pb < 0:
                    return (None, 0)
                return gu_tok.get((pb, pe_i, (qd + 2) * 4 + 3, 1), (None, 0))

            sched_wld(0, guard_gu(0))
            sched_wld(1, guard_gu(1))
            for qd in range(4):
                for ici in range(4):
                    ic = 4 * qd + ici
                    for pn in range(2):
                        P = (pM0 if (ic * 2 + pn) % 2 == 0 else pM1)[:]
                        fns = []
                        for k in range(8):
                            fns.append(lambda e, k=k, qd=qd, ici=ici, pn=pn, P=P:
                                e.matmul(P[:, 0:PAN],
                                         d_wg[qd % 2][:, k * PAN + ici * NP:
                                                      k * PAN + (ici + 1) * NP],
                                         d_h2b[:, k * KB + pn * PAN:
                                               k * KB + (pn + 1) * PAN],
                                         start=(k == 0), stop=(k == 7)))
                        for k in range(8):
                            fns.append(lambda e, k=k, qd=qd, ici=ici, pn=pn, P=P:
                                e.matmul(P[:, PAN:2 * PAN],
                                         d_wu[qd % 2][:, k * PAN + ici * NP:
                                                      k * PAN + (ici + 1) * NP],
                                         d_h2b[:, k * KB + pn * PAN:
                                               k * KB + (pn + 1) * PAN],
                                         start=(k == 0), stop=(k == 7)))
                        pk = ic * 2 + pn - 2
                        pguard = (prod_tok.get((b, e_i, pk), (None, 0)) if pk >= 0
                                  else prod_tok.get(
                                      (b, e_i - 1, 32 + pk) if e_i > 0
                                      else (b - 1, 8, 32 + pk), (None, 0)))
                        wmm = [wgld[(b, e_i, qd)], wuld[(b, e_i, qd)], h2b_ld[b],
                               c_end_v, pguard,
                               down_end.get((b, e_i - 1) if e_i > 0 else (b - 1, 8),
                                            (None, 0))]
                        tokm = sc.step("tensor", fns, waits=wmm, inc=s_pe)
                        gu_tok[(b, e_i, ic, pn)] = tokm
                        gslot = d_g[(ic * 2 + pn) % 2]
                        tsi = sc.step("scalar", lambda e, gslot=gslot, P=P:
                            e.activation(gslot[:], P[:, 0:PAN], AF.Silu),
                            waits=[tokm], inc=s_a)
                        tpr = sc.step("vector",
                            lambda e, gslot=gslot, P=P, ic=ic, pn=pn:
                            e.tensor_tensor(
                                d_prod[:, ic * KB + pn * PAN: ic * KB + (pn + 1) * PAN],
                                gslot[:], P[:, PAN:2 * PAN], ALU.mult),
                            waits=[tsi,
                                   down_end.get((b, e_i - 1) if e_i > 0 else (b - 1, 8),
                                                (None, 0))],
                            inc=s_v)
                        prod_tok[(b, e_i, ic * 2 + pn)] = tpr
                if qd + 2 <= 3:
                    sched_wld(qd + 2, gu_tok[(b, e_i, 4 * qd + 3, 1)])
            wdld[(b, e_i)] = sc.step("gpsimd", lambda e, dsrc=dsrc: e.dma_start(
                out=d_wd[:].rearrange("p (c h) -> p c h", c=16),
                in_=dsrc.rearrange("(c p) h -> p c h", p=NP)),
                waits=[dn_tok.get((b, e_i - 1, 7, 1) if e_i > 0 else (b - 1, 8, 7, 1),
                                  (None, 0)), c_end_pe],
                inc=s_wd, inc_k=16)
            prod_all = prod_tok[(b, e_i, 31)]
            for tc in range(8):
                for n in range(2):
                    Pd = (pM2[:, n * PAN:(n + 1) * PAN] if tc % 2 == 0
                          else (pT[:] if n == 0 else pX[:]))
                    fns = [
                        (lambda e, ic=ic, tc=tc, n=n, Pd=Pd: e.matmul(
                            Pd, d_prod[:, ic * KB + tc * NP:ic * KB + (tc + 1) * NP],
                            d_wd[:, ic * KB + n * PAN: ic * KB + (n + 1) * PAN],
                            start=(ic == 0), stop=(ic == 15)))
                        for ic in range(16)
                    ]
                    wdn = [prod_all, wdld[(b, e_i)],
                           stt_tok.get((b, e_i, tc - 2, n), (None, 0))]
                    if tc % 2 == 1 and n == 1:
                        wdn.append(pX_guard[0] or (None, 0))
                    tokd = sc.step("tensor", fns, waits=wdn, inc=s_pe)
                    dn_tok[(b, e_i, tc, n)] = tokd
                    if shared:
                        tstt = sc.step("vector",
                            lambda e, tc=tc, n=n, Pd=Pd: e.scalar_tensor_tensor(
                                d_acc[:, tc * KB + n * PAN: tc * KB + (n + 1) * PAN],
                                Pd, 1.0,
                                d_acc[:, tc * KB + n * PAN: tc * KB + (n + 1) * PAN],
                                ALU.mult, ALU.add),
                            waits=[tokd, acc_ld[b], rt_w[8 * b + tc]], inc=s_v)
                    else:
                        tstt = sc.step("vector",
                            lambda e, tc=tc, n=n, Pd=Pd, b=b, e_i=e_i:
                            e.scalar_tensor_tensor(
                                d_acc[:, tc * KB + n * PAN: tc * KB + (n + 1) * PAN],
                                Pd, w_res[:, 8 * (8 * b + tc) + e_i:
                                          8 * (8 * b + tc) + e_i + 1],
                                d_acc[:, tc * KB + n * PAN: tc * KB + (n + 1) * PAN],
                                ALU.mult, ALU.add),
                            waits=[tokd, acc_ld[b], rt_w[8 * b + tc]], inc=s_v)
                    stt_tok[(b, e_i, tc, n)] = tstt
                    if tc % 2 == 1 and n == 1:
                        pX_guard[0] = tstt
            down_end[(b, e_i)] = dn_tok[(b, e_i, 7, 1)]
        y_st[b] = sc.step("sync", lambda e, b=b: e.dma_start(
            out=y_d[b * BT:(b + 1) * BT, :].rearrange("(u p) d -> p u d", p=NP),
            in_=d_acc[:].rearrange("p (u d) -> p u d", u=8)),
            waits=[stt_tok[(b, 8, 7, 1)]] +
                  [stt_tok[(b, ei, tc, n)] for ei in range(9)
                   for tc in range(8) for n in range(2)],
            inc=s_y, inc_k=16)

    return _finish(nc, sc, sems, debug and dict(
        QT_dbg=None, qt=None, AT_dbg=None, attn=None,
        MB_dbg=None, W_dbg=W_dbg, nacc=nacc, bias_t=bias_t, w_res=w_res),
        upto)


def _finish(nc, sc, sems, dbg, upto):
    s_y = sems["s_y"]
    s_v = sems["s_v"]
    s_a = sems["s_a"]
    s_pe = sems["s_pe"]
    if dbg:
        if dbg.get("qt") is not None:
            for m in range(NPAIR):
                sc.step("sync", lambda e, m=m: e.dma_start(
                    out=dbg["QT_dbg"][m], in_=dbg["qt"][m][:]),
                    waits=[(s_v, s_v.n), (s_pe, s_pe.n), (s_a, s_a.n)],
                    inc=s_y, inc_k=16)
        if dbg.get("attn") is not None:
            for m in range(NPAIR):
                sc.step("sync", lambda e, m=m: e.dma_start(
                    out=dbg["AT_dbg"][m], in_=dbg["attn"][m][:]),
                    waits=[(s_v, s_v.n), (s_pe, s_pe.n), (s_a, s_a.n)],
                    inc=s_y, inc_k=16)
        if dbg.get("MB_dbg") is not None:
            sc.step("sync", lambda e: e.dma_start(
                out=dbg["MB_dbg"], in_=dbg["nacc"][0:1, 0:8]),
                waits=[(s_v, s_v.n), (s_a, s_a.n)], inc=s_y, inc_k=16)
        if dbg.get("W_dbg") is not None:
            sc.step("sync", lambda e: e.dma_start(
                out=dbg["W_dbg"], in_=dbg["w_res"][:]),
                waits=[(s_v, s_v.n)], inc=s_y, inc_k=16)
    fy = s_y.n
    final_waits = [(sm, sm.n) for sm in sems["all_dma"] if sm.n > 0 and sm is not s_y]
    final_waits += [(s_v, s_v.n), (s_a, s_a.n), (s_pe, s_pe.n)]
    sc.step("sync", lambda e: e.wait_ge(s_y.h, fy), waits=final_waits)

    with nc.Block() as block:
        block.sync(lambda e: sc.replay("sync", e))
        block.tensor(lambda e: sc.replay("tensor", e))
        block.vector(lambda e: sc.replay("vector", e))
        block.scalar(lambda e: sc.replay("scalar", e))
        block.gpsimd(lambda e: sc.replay("gpsimd", e))
    return nc


def make_in_maps(inputs, n_cores=8):
    x = np.asarray(inputs["L0_attn_hidden_states"], dtype=np.float32)
    f32 = lambda a: np.ascontiguousarray(np.asarray(a, dtype=np.float32))
    wq = f32(inputs["L0_attn_Wq"])
    wkv = f32(inputs["L0_attn_Wkv"])
    wo = f32(inputs["L0_attn_Wo"])
    bo = f32(inputs["L0_attn_bo"]).reshape(1, D)
    wgT = f32(np.asarray(inputs["L0_ffn_shared_gate"], dtype=np.float32).T)
    wuT = f32(np.asarray(inputs["L0_ffn_shared_up"], dtype=np.float32).T)
    wdT = f32(np.asarray(inputs["L0_ffn_shared_down"], dtype=np.float32).T)
    gw = f32(inputs["L0_ffn_gate_w"])
    uw = f32(inputs["L0_ffn_up_w"])
    dw = f32(inputs["L0_ffn_down_w"])
    rwT = f32(np.asarray(inputs["L0_ffn_router_w"], dtype=np.float32).T)
    rb = f32(inputs["L0_ffn_routing_bias"]).reshape(1, NE)
    idb = np.eye(NP, dtype=ml_dtypes.bfloat16)
    idf = np.eye(NP, dtype=np.float32)
    shared = dict(wq=wq, wkv=wkv, wo=wo, bo=bo, wgT=wgT, wuT=wuT, wdT=wdT,
                  gw=gw, uw=uw, dw=dw, rwT=rwT, rb=rb, idb=idb, idf=idf)
    in_maps = []
    for c in range(n_cores):
        bb, hh = c // 2, c % 2
        m = dict(shared)
        m["x_own"] = np.ascontiguousarray(x[bb, hh * TOK:(hh + 1) * TOK, :])
        m["x_oth"] = np.ascontiguousarray(x[bb, (1 - hh) * TOK:(2 - hh) * TOK, :])
        in_maps.append(m)
    return in_maps


def kernel(**inputs):
    nc = build_program(upto="D", debug=False)
    in_maps = make_in_maps(inputs)
    res = run_bass_kernel_spmd(nc, in_maps, core_ids=list(range(8)))
    out = np.empty((4, SEQ, D), dtype=np.float32)
    for c in range(8):
        bb, hh = c // 2, c % 2
        out[bb, hh * TOK:(hh + 1) * TOK, :] = np.asarray(res.results[c]["y"])
    return out


# version 02
# speedup vs baseline: 1.4777x; 1.4777x over previous
import math
import contextlib
import numpy as np
import ml_dtypes
import concourse.bass as bass
import concourse.mybir as mybir
from concourse.bass_utils import run_bass_kernel_spmd

F32 = mybir.dt.float32
BF16 = mybir.dt.bfloat16
AF = mybir.ActivationFunctionType
ALU = mybir.AluOpType
AX = mybir.AxisListType

NP = 128
D = 1024
TOK = 4096
SEQ = 8192
NH = 16
DH = 64
NPAIR = 8
I2 = 2048
NE = 8
PAN = 512
NBLK = 4
BT = 1024
EPS = 1e-5
KB = 1024


class Sem:
    def __init__(self, handle):
        self.h = handle
        self.n = 0


class Sched:
    ENGS = ("sync", "tensor", "vector", "scalar", "gpsimd")

    def __init__(self):
        self.steps = {e: [] for e in self.ENGS}
        self.last_wait = {e: {} for e in self.ENGS}

    def step(self, eng, fns, waits=(), inc=None, inc_k=1):
        if not isinstance(fns, (list, tuple)):
            fns = [fns]
        lw = self.last_wait[eng]
        eff = []
        for (s, v) in waits:
            if s is None or v <= 0:
                continue
            if lw.get(id(s), -1) >= v:
                continue
            lw[id(s)] = v
            eff.append((s, v))
        tok = None
        if inc is not None:
            inc.n += inc_k
            tok = (inc, inc.n)
        self.steps[eng].append((eff, list(fns), inc, inc_k))
        return tok

    def replay(self, eng_name, eng):
        for waits, fns, inc, inc_k in self.steps[eng_name]:
            for (s, v) in waits:
                eng.wait_ge(s.h, v)
            instr = None
            for f in fns:
                instr = f(eng)
            if inc is not None:
                instr.then_inc(inc.h, inc_k)


def build_program(upto="D", debug=False):
    nc = bass.Bass("TRN2", target_bir_lowering=False)

    dt = nc.dram_tensor
    x_own = dt("x_own", [TOK, D], F32, kind="ExternalInput").ap()
    x_oth = dt("x_oth", [TOK, D], F32, kind="ExternalInput").ap()
    wq_d = dt("wq", [NP, 8 * KB], F32, kind="ExternalInput").ap()
    wkv_d = dt("wkv", [NP, 16 * KB], F32, kind="ExternalInput").ap()
    wo_d = dt("wo", [NP, 8 * KB], F32, kind="ExternalInput").ap()
    bo_d = dt("bo", [1, D], F32, kind="ExternalInput").ap()
    gw_d = dt("gw", [9, 4, NP, 4 * KB], F32, kind="ExternalInput").ap()
    uw_d = dt("uw", [9, 4, NP, 4 * KB], F32, kind="ExternalInput").ap()
    dw_d = dt("dw", [9, NP, 16 * KB], F32, kind="ExternalInput").ap()
    rwT_d = dt("rwT", [D, NE], F32, kind="ExternalInput").ap()
    rb_d = dt("rb", [1, NE], F32, kind="ExternalInput").ap()
    idb_d = dt("idb", [NP, NP], BF16, kind="ExternalInput").ap()
    idf_d = dt("idf", [NP, NP], F32, kind="ExternalInput").ap()

    y_d = dt("y", [TOK, D], F32, kind="ExternalOutput").ap()

    def scratch(name, shape, dtype):
        kind = "ExternalOutput" if debug else "Internal"
        return dt(name, shape, dtype, kind=kind).ap()

    KT_s = scratch("KT_s", [D, SEQ], BF16)
    V_s = scratch("V_s", [NH, SEQ, DH], BF16)
    X1_s = scratch("X1_s", [TOK, D], F32)
    H2T_s = scratch("H2T_s", [D, TOK], BF16)
    if debug:
        QT_dbg = dt("QT_dbg", [NPAIR, NP, TOK], BF16, kind="ExternalOutput").ap()
        AT_dbg = dt("AT_dbg", [NPAIR, NP, TOK], BF16, kind="ExternalOutput").ap()
        MB_dbg = dt("MB_dbg", [1, 8], F32, kind="ExternalOutput").ap()
        W_dbg = dt("W_dbg", [NP, 256], F32, kind="ExternalOutput").ap()

    al = nc.alloc_sbuf_tensor
    ones_col_b = al("ones_col_b", [NP, 1], BF16)
    ones_row_f = al("ones_row_f", [1, NP], F32)
    bias_t = al("bias_t", [NP, 1], F32)
    nacc = al("nacc", [1, 16], F32)
    w_res = al("w_res", [NP, 256], F32)
    bo_row = al("bo_row", [1, D], F32)
    rb_row = al("rb_row", [1, NE], F32)
    rwT_sb = al("rwT_sb", [NP, 8 * NE], F32)
    idb_sb = al("idb_sb", [NP, NP], BF16)
    idf_sb = al("idf_sb", [NP, NP], F32)

    AR_BYTES = 190 * KB
    arena = al("arena", [NP, AR_BYTES], mybir.dt.uint8)
    ar0 = nc.lookup_mloc(arena).addr
    _uid = [0]

    def at(off_kb, shape, dtype, n=None):
        def one(i):
            _uid[0] += 1
            off = ar0 + int(off_kb * KB) + i * shape[1] * mybir.dt.size(dtype)
            off = (off + 31) // 32 * 32
            return nc.alloc_sbuf_tensor_at(f"ar_{_uid[0]}", shape, dtype, offset=off)
        if n is None:
            return one(0)
        return [one(i) for i in range(n)]

    qt = at(0, [NP, TOK], BF16, n=NPAIR)
    a_x = at(64, [NP, D], F32, n=3)
    a_hn = at(76, [NP, D], BF16, n=2)
    a_hnT = at(80, [NP, 4096], BF16, n=2)
    a_kst = at(96, [NP, PAN], BF16, n=2)
    a_vst = at(98, [NP, D], BF16, n=2)
    a_sq = at(102, [NP, PAN], BF16, n=2)
    a_dump = at(104, [NP, D], F32)
    a_rows = at(108, [NP, 64], F32)
    wq_sb = at(108.25, [NP, 8 * KB], BF16)
    wkv_sb = at(124.25, [NP, 16 * KB], BF16)
    attn = at(64, [NP, TOK], BF16, n=NPAIR)
    b_k = at(128, [NP, SEQ], BF16, n=2)
    b_v = at(160, [NP, 64 * 66], BF16, n=2)
    b_pt = at(176.8, [NP, KB], BF16, n=2)
    b_bc = at(180.8, [64, PAN], F32)
    b_rs = at(182.8, [1, PAN], F32)
    wo_sb = at(128, [NP, 8 * KB], BF16)
    c_x = at(144, [NP, D], F32, n=2)
    c_x1 = at(152, [NP, D], F32, n=2)
    c_h2f = at(160, [NP, D], F32, n=2)
    c_h2T32 = at(168, [NP, D], F32, n=2)
    c_h2Tb = at(176, [NP, D], BF16, n=2)
    c_rows = at(180, [NP, 64], F32)
    c_rt = at(180.25, [NP, 64], F32)
    c_dump = at(180.5, [NP, D], F32)
    d_acc = at(0, [NP, 8 * KB], F32)
    d_h2b = at(32, [NP, 8 * KB], BF16)
    d_prod = at(48, [NP, 16 * KB], BF16)
    d_wg = at(80, [NP, 4 * KB], BF16, n=2)
    d_wu = at(96, [NP, 4 * KB], BF16, n=2)
    d_wd = at(112, [NP, 16 * KB], BF16)
    d_g = at(144, [NP, PAN], F32, n=2)

    pM0 = nc.alloc_psum_tensor("pM0", [NP, 1024], F32)
    pM1 = nc.alloc_psum_tensor("pM1", [NP, 1024], F32)
    pM2 = nc.alloc_psum_tensor("pM2", [NP, 1024], F32)
    pT = nc.alloc_psum_tensor("pT", [NP, 512], F32)
    pX = nc.alloc_psum_tensor("pX", [NP, 512], F32)
    PTB0 = pT[:].bitcast(BF16)
    PTB1 = pM2[:, 0:512].bitcast(BF16)

    mk = lambda nm: Sem(nc.alloc_semaphore(name=nm))
    s_pre = mk("s_pre")
    s_xb = [mk(f"s_xb{i}") for i in range(3)]
    s_kst = [mk(f"s_kst{i}") for i in range(2)]
    s_vst = [mk(f"s_vst{i}") for i in range(2)]
    s_kld = [mk(f"s_kld{i}") for i in range(2)]
    s_vld = [mk(f"s_vld{i}") for i in range(2)]
    s_wq, s_wkv, s_wo, s_wd = mk("s_wq"), mk("s_wkv"), mk("s_wo"), mk("s_wd")
    s_cx = [mk(f"s_cx{i}") for i in range(2)]
    s_x1 = [mk(f"s_x1{i}") for i in range(2)]
    s_h2 = [mk(f"s_h2{i}") for i in range(2)]
    s_wg = [mk(f"s_wg{i}") for i in range(2)]
    s_wu = [mk(f"s_wu{i}") for i in range(2)]
    s_acc = [mk(f"s_acc{i}") for i in range(2)]
    s_h2b = [mk(f"s_h2b{i}") for i in range(2)]
    s_y, s_pe, s_v, s_a = mk("s_y"), mk("s_pe"), mk("s_v"), mk("s_a")
    all_dma_sems = ([s_pre] + s_xb + s_kst + s_vst + s_kld + s_vld +
                    [s_wq, s_wkv, s_wo, s_wd] + s_cx + s_x1 + s_h2 +
                    s_wg + s_wu + s_acc + s_h2b + [s_y])
    sems = dict(all_dma=all_dma_sems, s_y=s_y, s_pe=s_pe, s_v=s_v, s_a=s_a)

    sc = Sched()

    sc.step("vector", lambda e: e.memset(ones_col_b[:], 1.0), inc=s_v)
    sc.step("vector", lambda e: e.memset(ones_row_f[:], 1.0), inc=s_v)
    sc.step("vector", lambda e: e.memset(nacc[:], 0.0), inc=s_v)
    ones_ready = (s_v, s_v.n)
    sc.step("sync", lambda e: e.dma_start(out=bo_row[:], in_=bo_d), inc=s_pre, inc_k=16)
    sc.step("sync", lambda e: e.dma_start(out=rb_row[:], in_=rb_d), inc=s_pre, inc_k=16)
    sc.step("sync", lambda e: e.dma_start(
        out=rwT_sb[:].rearrange("p (j e) -> p j e", j=8),
        in_=rwT_d.rearrange("(j p) e -> p j e", p=NP)), inc=s_pre, inc_k=16)
    sc.step("sync", lambda e: e.dma_start(out=idb_sb[:], in_=idb_d), inc=s_pre, inc_k=16)
    sc.step("sync", lambda e: e.dma_start(out=idf_sb[:], in_=idf_d), inc=s_pre, inc_k=16)
    small_ld = (s_pre, s_pre.n)
    sc.step("gpsimd", lambda e: e.dma_start(out=wq_sb[:], in_=wq_d),
            inc=s_wq, inc_k=16)
    wq_ld = (s_wq, s_wq.n)
    sc.step("gpsimd", lambda e: e.dma_start(out=wkv_sb[:], in_=wkv_d),
            inc=s_wkv, inc_k=16)
    wkv_ld = (s_wkv, s_wkv.n)

    RC = {"sum": 0, "m": 2, "msq": 4, "vep": 6, "var": 8, "ln": 10, "r": 12}

    def rcol(name, t):
        return a_rows[:, RC[name] + (t % 2): RC[name] + (t % 2) + 1]

    norm_tok = [None]
    tile_norm_done = {}
    tile_tp_done = {}
    tile_ev_done = {}
    panel_mm_done = {}
    kst_dma = {}
    kst_sq = {}
    vst_dma = {}
    ps_ring = [pM0[:, 0:512], pM0[:, 512:1024], pM1[:, 0:512], pM1[:, 512:1024]]
    ps_tok = [None, None, None, None]
    ps_i = [0]

    def next_ps():
        i = ps_i[0] % 4
        ps_i[0] += 1
        return i

    a_mm_end = [None]
    ASQ_guard = [None, None]
    pX_guard = [None]

    for t in range(64):
        xb = a_x[t % 3]
        src = (x_own if t < 32 else x_oth)
        tsl = (t % 32)
        tok_x = sc.step("sync", lambda e, s=src, q=tsl, xb=xb: e.dma_start(
            out=xb[:], in_=s[q * NP:(q + 1) * NP, :]),
            waits=[tile_norm_done.get(t - 3, (None, 0))],
            inc=s_xb[t % 3], inc_k=16)
        v1 = sc.step("vector", lambda e, xb=xb, t=t: e.reduce_sum(
            rcol("sum", t), xb[:], axis=AX.X), waits=[tok_x], inc=s_v)
        a1 = sc.step("scalar", lambda e, xb=xb, t=t: e.activation(
            a_dump[:], xb[:], AF.Square, accum_out=rcol("msq", t)),
            waits=[tok_x], inc=s_a)
        v2 = sc.step("vector", lambda e, t=t: e.tensor_scalar_mul(
            rcol("m", t), rcol("sum", t), 1.0 / D), waits=[v1], inc=s_v)
        v4 = sc.step("vector", lambda e, t=t: e.tensor_scalar(
            rcol("vep", t), rcol("msq", t), 1.0 / D, EPS, ALU.mult, ALU.add),
            waits=[a1], inc=s_v)
        v3 = sc.step("vector", lambda e, t=t: e.tensor_tensor(
            rcol("var", t), rcol("m", t), rcol("m", t), ALU.mult),
            waits=[v2], inc=s_v)
        v5 = sc.step("vector", lambda e, t=t: e.tensor_sub(
            rcol("vep", t), rcol("vep", t), rcol("var", t)),
            waits=[v3, v4], inc=s_v)
        a2 = sc.step("scalar", lambda e, t=t: e.activation(
            rcol("ln", t), rcol("vep", t), AF.Ln), waits=[v5], inc=s_a)
        a3 = sc.step("scalar", lambda e, t=t: e.activation(
            rcol("r", t), rcol("ln", t), AF.Exp, scale=-0.5),
            waits=[a2], inc=s_a)
        hb = a_hn[t % 2]
        v6 = sc.step("vector", lambda e, xb=xb, hb=hb, t=t: e.tensor_scalar(
            hb[:], xb[:], rcol("m", t), rcol("r", t), ALU.subtract, ALU.mult),
            waits=[a3, tile_tp_done.get(t - 2, (None, 0))], inc=s_v)
        tile_norm_done[t] = v6
        PTB = (PTB0 if t % 2 == 0 else PTB1)
        fns = [
            (lambda e, hb=hb, c=c, PTB=PTB: e.transpose(
                PTB[:, c * NP:(c + 1) * NP], hb[:, c * NP:(c + 1) * NP],
                idb_sb[:]))
            for c in range(8)
        ]
        jprev = panel_mm_done.get(t // 4 - 2, (None, 0))
        p1 = sc.step("tensor", fns, waits=[v6, tile_ev_done.get(t - 2, (None, 0)),
                                           small_ld], inc=s_pe)
        hT = a_hnT[(t // 4) % 2]
        ev = sc.step("vector", lambda e, hT=hT, t=t, PTB=PTB: e.tensor_copy(
            hT[:].rearrange("p (c u) -> p c u", c=8)[:, :, (t % 4) * NP:(t % 4 + 1) * NP],
            PTB[:].rearrange("p (c u) -> p c u", c=8)),
            waits=[p1, jprev], inc=s_v)
        tile_ev_done[t] = ev
        tile_tp_done[t] = p1

        if t % 4 != 3:
            continue
        j = t // 4
        own = j < 8
        hT = a_hnT[j % 2]
        pan_ready = ev
        last_mm = None

        def mm_group(lhs_fn, rhs_fn, extra_waits, n_k=8):
            nonlocal last_mm
            si = next_ps()
            ps = ps_ring[si]
            fns = [
                (lambda e, k=k, si=si: e.matmul(
                    ps_ring[si], lhs_fn(k), rhs_fn(k),
                    start=(k == 0), stop=(k == n_k - 1)))
                for k in range(n_k)
            ]
            w = [pan_ready] + extra_waits
            if ps_tok[si] is not None:
                w.append(ps_tok[si])
            tokm = sc.step("tensor", fns, waits=w, inc=s_pe)
            last_mm = tokm
            return si, tokm

        if own:
            jq = j
            for m in range(8):
                si, tokm = mm_group(
                    lambda k, m=m: wq_sb[:, k * KB + m * NP: k * KB + (m + 1) * NP],
                    lambda k, hT=hT: hT[:, k * PAN:(k + 1) * PAN], [wq_ld])
                tev = sc.step("vector", lambda e, m=m, jq=jq, si=si: e.tensor_copy(
                    qt[m][:, jq * PAN:(jq + 1) * PAN], ps_ring[si]),
                    waits=[tokm], inc=s_v)
                ps_tok[si] = tev
                sqb = a_sq[m % 2]
                tsq = sc.step("scalar", lambda e, m=m, jq=jq, sqb=sqb: e.activation(
                    sqb[:], qt[m][:, jq * PAN:(jq + 1) * PAN], AF.Square),
                    waits=[tev, ASQ_guard[m % 2] or (None, 0)], inc=s_a)
                tnm = sc.step("tensor", lambda e, sqb=sqb: e.matmul(
                    pX[0:1, 0:PAN], ones_col_b[:], sqb[:], start=True, stop=True),
                    waits=[tsq, ones_ready, pX_guard[0] or (None, 0)], inc=s_pe)
                ASQ_guard[m % 2] = tnm
                trm = sc.step("vector", lambda e: e.reduce_max(
                    nacc[0:1, 8:9], pX[0:1, 0:PAN], axis=AX.X),
                    waits=[tnm], inc=s_v)
                tmx = sc.step("vector", lambda e: e.tensor_max(
                    nacc[0:1, 0:1], nacc[0:1, 0:1], nacc[0:1, 8:9]),
                    waits=[trm], inc=s_v)
                pX_guard[0] = trm
                norm_tok[0] = tmx
        for c in range(8):
            si, tokm = mm_group(
                lambda k, c=c: wkv_sb[:, k * 2048 + c * NP: k * 2048 + (c + 1) * NP],
                lambda k, hT=hT: hT[:, k * PAN:(k + 1) * PAN], [wkv_ld])
            kb = a_kst[c % 2]
            tev = sc.step("vector", lambda e, kb=kb, si=si: e.tensor_copy(
                kb[:], ps_ring[si]),
                waits=[tokm, kst_dma.get((j, c - 2), kst_dma.get((j - 1, c + 6), (None, 0))),
                       kst_sq.get((j, c - 2), kst_sq.get((j - 1, c + 6), (None, 0)))],
                inc=s_v)
            ps_tok[si] = tev
            td = sc.step("sync", lambda e, kb=kb, c=c, j=j: e.dma_start(
                out=KT_s[c * NP:(c + 1) * NP, j * PAN:(j + 1) * PAN], in_=kb[:]),
                waits=[tev], inc=s_kst[c % 2], inc_k=16)
            kst_dma[(j, c)] = td
            sqb = a_sq[c % 2]
            tsq = sc.step("scalar", lambda e, kb=kb, sqb=sqb: e.activation(
                sqb[:], kb[:], AF.Square),
                waits=[tev, ASQ_guard[c % 2] or (None, 0)], inc=s_a)
            tnm = sc.step("tensor", lambda e, sqb=sqb: e.matmul(
                pX[0:1, 0:PAN], ones_col_b[:], sqb[:], start=True, stop=True),
                waits=[tsq, pX_guard[0] or (None, 0)], inc=s_pe)
            ASQ_guard[c % 2] = tnm
            last_mm = tnm
            trm = sc.step("vector", lambda e: e.reduce_max(
                nacc[0:1, 9:10], pX[0:1, 0:PAN], axis=AX.X),
                waits=[tnm], inc=s_v)
            tmx = sc.step("vector", lambda e: e.tensor_max(
                nacc[0:1, 1:2], nacc[0:1, 1:2], nacc[0:1, 9:10]),
                waits=[trm], inc=s_v)
            pX_guard[0] = trm
            kst_sq[(j, c)] = tsq
            norm_tok[0] = tmx
        for u in range(4):
            vb = a_vst[u % 2]
            toks = []
            for n in range(2):
                si, tokm = mm_group(
                    lambda k, u=u, hT=hT: hT[:, k * PAN + u * NP: k * PAN + (u + 1) * NP],
                    lambda k, n=n: wkv_sb[:, k * 2048 + KB + n * PAN: k * 2048 + KB + (n + 1) * PAN],
                    [wkv_ld])
                tev = sc.step("vector", lambda e, vb=vb, n=n, si=si: e.tensor_copy(
                    vb[:, n * PAN:(n + 1) * PAN], ps_ring[si]),
                    waits=[tokm, vst_dma.get((j, u - 2), vst_dma.get((j - 1, u + 2), (None, 0)))],
                    inc=s_v)
                ps_tok[si] = tev
                toks.append(tev)
            tt = j * 4 + u
            td = sc.step("sync", lambda e, vb=vb, tt=tt: e.dma_start(
                out=V_s[:, tt * NP:(tt + 1) * NP, :].rearrange("h p n -> p h n"),
                in_=vb[:].rearrange("p (h n) -> p h n", h=NH)),
                waits=toks, inc=s_vst[u % 2], inc_k=16)
            vst_dma[(j, u)] = td
        panel_mm_done[j] = last_mm

    a_scratch_done = [(s_kst[0], s_kst[0].n), (s_kst[1], s_kst[1].n),
                      (s_vst[0], s_vst[0].n), (s_vst[1], s_vst[1].n)]
    a_mm_end = panel_mm_done[15]

    bv1 = sc.step("vector", lambda e: e.tensor_tensor(
        nacc[0:1, 2:3], nacc[0:1, 0:1], nacc[0:1, 1:2], ALU.mult),
        waits=[norm_tok[0]], inc=s_v)
    ba1 = sc.step("scalar", lambda e: e.activation(
        nacc[0:1, 3:4], nacc[0:1, 2:3], AF.Ln), waits=[bv1], inc=s_a)
    ba2 = sc.step("scalar", lambda e: e.activation(
        nacc[0:1, 4:5], nacc[0:1, 3:4], AF.Exp, scale=0.5), waits=[ba1], inc=s_a)
    bv2 = sc.step("vector", lambda e: e.tensor_scalar_mul(
        nacc[0:1, 5:6], nacc[0:1, 4:5], -0.125), waits=[ba2], inc=s_v)
    bt1 = sc.step("tensor", lambda e: e.matmul(
        pX[:, 0:1], ones_row_f[:], nacc[0:1, 5:6], start=True, stop=True),
        waits=[bv2, pX_guard[0] or (None, 0)], inc=s_pe)
    bt2 = sc.step("vector", lambda e: e.tensor_copy(bias_t[:], pX[:, 0:1]),
                  waits=[bt1], inc=s_v)
    bias_ready = bt2
    pX_guard[0] = bt2

    if upto == "A":
        return _finish(nc, sc, sems, debug and dict(
            QT_dbg=QT_dbg, qt=qt, AT_dbg=None, attn=None,
            MB_dbg=MB_dbg, W_dbg=None, nacc=nacc, bias_t=bias_t, w_res=w_res),
            upto)

    kld = {}
    vld = {}
    for p in range(2):
        kld[p] = sc.step("sync", lambda e, p=p: e.dma_start(
            out=b_k[p][:], in_=KT_s[p * NP:(p + 1) * NP, :]),
            waits=a_scratch_done + [a_mm_end], inc=s_kld[p % 2], inc_k=16)
    for hh in range(2):
        vld[hh] = sc.step("sync", lambda e, hh=hh: e.dma_start(
            out=b_v[hh % 2][:].rearrange("p (c w) -> p c w", w=66)[:, :, 0:64],
            in_=V_s[hh].rearrange("(c p) n -> p c n", p=NP)),
            waits=a_scratch_done + [a_mm_end], inc=s_vld[hh % 2], inc_k=16)
    vmemset = {}
    for sl in range(2):
        vmemset[sl] = sc.step("vector", lambda e, sl=sl: e.memset(
            b_v[sl][:].rearrange("p (c w) -> p c w", w=66)[:, :, 64:65], 1.0),
            waits=[a_mm_end], inc=s_v)

    exp_tok = {}
    pv_tok = {}
    smm_end = {}
    norm_done = {}
    for p in range(NPAIR):
        for head in range(2):
            hh = 2 * p + head
            kbuf = b_k[p % 2]
            vbuf = b_v[hh % 2]
            rbase = 64 * head
            for q in range(8):
                def s_fns(g):
                    Sp = (pM0 if g % 2 == 0 else pM1)[:]
                    return [
                        (lambda e, c=c, Sp=Sp, kbuf=kbuf, rbase=rbase, p=p, q=q,
                                cc=cc:
                         e.matmul(Sp[:, cc * PAN:(cc + 1) * PAN],
                                  kbuf[rbase:rbase + 64, c * NP:(c + 1) * NP],
                                  qt[p][rbase:rbase + 64, q * PAN:(q + 1) * PAN],
                                  start=True, stop=True))
                        for cc, c in enumerate((2 * g, 2 * g + 1))
                    ]

                def pv_fns(g):
                    pb = b_pt[g % 2]
                    return [
                        (lambda e, c=c, pb=pb, vbuf=vbuf, g=g, cc=cc: e.matmul(
                            pM2[0:65, 0:PAN],
                            vbuf[:, c * 66:c * 66 + 65],
                            pb[:, cc * PAN:(cc + 1) * PAN],
                            start=(g == 0 and cc == 0), stop=(g == 31 and cc == 1)))
                        for cc, c in enumerate((2 * g, 2 * g + 1))
                    ]

                w0 = [kld[p], vld[hh],
                      norm_done.get((hh, q - 1),
                                    norm_done.get((hh - 1, 7), (None, 0)))]
                if q == 0 and head == 0:
                    w0.append(vmemset[hh % 2])
                smm_end[(hh, q, 0)] = sc.step("tensor", s_fns(0), waits=w0,
                                              inc=s_pe)
                smm_end[(hh, q, 1)] = sc.step("tensor", s_fns(1), waits=[],
                                              inc=s_pe)
                for g in range(32):
                    pb = b_pt[g % 2]
                    Sp = (pM0 if g % 2 == 0 else pM1)[:]
                    te = sc.step("scalar", lambda e, Sp=Sp, pb=pb: e.activation(
                        pb[:], Sp, AF.Exp, bias=bias_t[:, 0:1], scale=0.125),
                        waits=[smm_end[(hh, q, g)], bias_ready,
                               pv_tok.get((hh, q, g - 2), (None, 0))],
                        inc=s_a)
                    exp_tok[(hh, q, g)] = te
                    fns = pv_fns(g) + (s_fns(g + 2) if g + 2 < 32 else [])
                    tokt = sc.step("tensor", fns, waits=[te], inc=s_pe)
                    pv_tok[(hh, q, g)] = tokt
                    if g + 2 < 32:
                        smm_end[(hh, q, g + 2)] = tokt
                n1 = sc.step("vector", lambda e: e.reciprocal(
                    b_rs[:], pM2[64:65, 0:PAN]),
                    waits=[pv_tok[(hh, q, 31)]], inc=s_v)
                n2 = sc.step("tensor", lambda e: e.matmul(
                    pX[0:64, 0:PAN], ones_row_f[0:1, 0:64], b_rs[:],
                    start=True, stop=True),
                    waits=[n1, pX_guard[0] or (None, 0)], inc=s_pe)
                n3 = sc.step("vector", lambda e: e.tensor_copy(
                    b_bc[:], pX[0:64, 0:PAN]), waits=[n2], inc=s_v)
                n4 = sc.step("vector", lambda e, p=p, rbase=rbase, q=q: e.tensor_tensor(
                    attn[p][rbase:rbase + 64, q * PAN:(q + 1) * PAN],
                    pM2[0:64, 0:PAN], b_bc[:], ALU.mult),
                    waits=[n3], inc=s_v)
                norm_done[(hh, q)] = n4
                pX_guard[0] = n3
            if hh + 2 < NH:
                vld[hh + 2] = sc.step("sync", lambda e, hh=hh: e.dma_start(
                    out=b_v[hh % 2][:].rearrange("p (c w) -> p c w", w=66)[:, :, 0:64],
                    in_=V_s[hh + 2].rearrange("(c p) n -> p c n", p=NP)),
                    waits=[pv_tok[(hh, 7, 31)]], inc=s_vld[hh % 2], inc_k=16)
        if p + 2 < NPAIR:
            kld[p + 2] = sc.step("sync", lambda e, p=p: e.dma_start(
                out=b_k[p % 2][:], in_=KT_s[(p + 2) * NP:(p + 3) * NP, :]),
                waits=[smm_end[(2 * p + 1, 7, 31)]], inc=s_kld[p % 2], inc_k=16)

    b_end_pe = pv_tok[(NH - 1, 7, 31)]
    b_end_v = norm_done[(NH - 1, 7)]

    if upto == "B":
        return _finish(nc, sc, sems, debug and dict(
            QT_dbg=QT_dbg, qt=qt, AT_dbg=AT_dbg, attn=attn,
            MB_dbg=MB_dbg, W_dbg=None, nacc=nacc, bias_t=bias_t, w_res=w_res),
            upto)

    wo_ld = sc.step("gpsimd", lambda e: e.dma_start(out=wo_sb[:], in_=wo_d),
        waits=[b_end_pe], inc=s_wo, inc_k=16)

    CC = {"sum": 0, "m": 2, "msq": 4, "vep": 6, "var": 8, "ln": 10, "r": 12}

    def ccol(name, t):
        return c_rows[:, CC[name] + (t % 2): CC[name] + (t % 2) + 1]

    x1_dma = {}
    h2_dma = {}
    x1_add = {}
    h2T_ev = {}
    rt_w = {}
    proj_tok = {}
    for t in range(32):
        xb = c_x[t % 2]
        tok_x = sc.step("sync", lambda e, t=t, xb=xb: e.dma_start(
            out=xb[:], in_=x_own[t * NP:(t + 1) * NP, :]),
            waits=[b_end_pe, x1_add.get(t - 2, (None, 0))],
            inc=s_cx[t % 2], inc_k=16)
        P = (pM0 if t % 2 == 0 else pM1)[:]
        fns = []
        for n in range(2):
            for k in range(8):
                fns.append(lambda e, n=n, k=k, t=t, P=P: e.matmul(
                    P[:, n * PAN:(n + 1) * PAN],
                    attn[k][:, t * NP:(t + 1) * NP],
                    wo_sb[:, k * KB + n * PAN: k * KB + (n + 1) * PAN],
                    start=(k == 0), stop=False))
            fns.append(lambda e, n=n, P=P: e.matmul(
                P[:, n * PAN:(n + 1) * PAN], ones_row_f[:],
                bo_row[0:1, n * PAN:(n + 1) * PAN], start=False, stop=True))
        tp = sc.step("tensor", fns,
                     waits=[wo_ld, small_ld, b_end_v, x1_add.get(t - 2, (None, 0))],
                     inc=s_pe)
        proj_tok[t] = tp
        x1b = c_x1[t % 2]
        tadd = sc.step("vector", lambda e, x1b=x1b, xb=xb, P=P: e.tensor_add(
            x1b[:], P, xb[:]),
            waits=[tp, tok_x, x1_dma.get(t - 2, (None, 0))], inc=s_v)
        x1_add[t] = tadd
        x1_dma[t] = sc.step("sync", lambda e, x1b=x1b, t=t: e.dma_start(
            out=X1_s[t * NP:(t + 1) * NP, :], in_=x1b[:]),
            waits=[tadd], inc=s_x1[t % 2], inc_k=16)
        v1 = sc.step("vector", lambda e, x1b=x1b, t=t: e.reduce_sum(
            ccol("sum", t), x1b[:], axis=AX.X), waits=[tadd], inc=s_v)
        a1 = sc.step("scalar", lambda e, x1b=x1b, t=t: e.activation(
            c_dump[:], x1b[:], AF.Square, accum_out=ccol("msq", t)),
            waits=[tadd], inc=s_a)
        v2 = sc.step("vector", lambda e, t=t: e.tensor_scalar_mul(
            ccol("m", t), ccol("sum", t), 1.0 / D), waits=[v1], inc=s_v)
        v4 = sc.step("vector", lambda e, t=t: e.tensor_scalar(
            ccol("vep", t), ccol("msq", t), 1.0 / D, EPS, ALU.mult, ALU.add),
            waits=[a1], inc=s_v)
        v3 = sc.step("vector", lambda e, t=t: e.tensor_tensor(
            ccol("var", t), ccol("m", t), ccol("m", t), ALU.mult),
            waits=[v2], inc=s_v)
        v5 = sc.step("vector", lambda e, t=t: e.tensor_sub(
            ccol("vep", t), ccol("vep", t), ccol("var", t)),
            waits=[v3, v4], inc=s_v)
        a2 = sc.step("scalar", lambda e, t=t: e.activation(
            ccol("ln", t), ccol("vep", t), AF.Ln), waits=[v5], inc=s_a)
        a3 = sc.step("scalar", lambda e, t=t: e.activation(
            ccol("r", t), ccol("ln", t), AF.Exp, scale=-0.5), waits=[a2], inc=s_a)
        h2b_ = c_h2f[t % 2]
        v6 = sc.step("vector", lambda e, h2b_=h2b_, x1b=x1b, t=t: e.tensor_scalar(
            h2b_[:], x1b[:], ccol("m", t), ccol("r", t), ALU.subtract, ALU.mult),
            waits=[a3, h2T_ev.get(t - 2, (None, 0))], inc=s_v)
        if t % 2 == 0:
            tps = [pM2[:, c * NP:(c + 1) * NP] for c in range(8)]
        else:
            tps = [pT[:, (c % 4) * NP:((c % 4) + 1) * NP] if c < 4
                   else pX[:, (c % 4) * NP:((c % 4) + 1) * NP] for c in range(8)]
        fns = [
            (lambda e, c=c, h2b_=h2b_, dst=tps[c]: e.transpose(
                dst, h2b_[:, c * NP:(c + 1) * NP], idf_sb[:]))
            for c in range(8)
        ]
        wtp = [v6, h2T_ev.get(t - 2, (None, 0))]
        if t % 2 == 1:
            wtp.append(pX_guard[0] or (None, 0))
            wtp.append(rt_w.get(t - 1, (None, 0)))
        tpt = sc.step("tensor", fns, waits=wtp, inc=s_pe)
        hT32 = c_h2T32[t % 2]
        if t % 2 == 0:
            ev1 = sc.step("vector", lambda e, hT32=hT32: e.tensor_copy(
                hT32[:], pM2[:]), waits=[tpt], inc=s_v)
        else:
            ev1a = sc.step("vector", lambda e, hT32=hT32: e.tensor_copy(
                hT32[:, 0:PAN], pT[:]), waits=[tpt], inc=s_v)
            ev1 = sc.step("vector", lambda e, hT32=hT32: e.tensor_copy(
                hT32[:, PAN:2 * PAN], pX[:]), waits=[tpt], inc=s_v)
            pX_guard[0] = ev1
        hTb = c_h2Tb[t % 2]
        ev2 = sc.step("vector", lambda e, hTb=hTb, hT32=hT32: e.tensor_copy(
            hTb[:], hT32[:]), waits=[ev1, h2_dma.get(t - 2, (None, 0))], inc=s_v)
        h2T_ev[t] = ev2
        h2_dma[t] = sc.step("sync", lambda e, hTb=hTb, t=t: e.dma_start(
            out=H2T_s.rearrange("(j p) u -> p j u", p=NP)[:, :, t * NP:(t + 1) * NP],
            in_=hTb[:].rearrange("p (j u) -> p j u", j=8)),
            waits=[ev2], inc=s_h2[t % 2], inc_k=16)
        fns = [
            (lambda e, jj=jj, hT32=hT32: e.matmul(
                pX[:, 0:NE], hT32[:, jj * NP:(jj + 1) * NP],
                rwT_sb[:, jj * NE:(jj + 1) * NE],
                start=(jj == 0), stop=False))
            for jj in range(8)
        ]
        fns.append(lambda e: e.matmul(pX[:, 0:NE], ones_row_f[:], rb_row[:],
                                      start=False, stop=True))
        wrt = [ev1, small_ld]
        if t % 2 == 0:
            wrt.append(pX_guard[0] or (None, 0))
        trt = sc.step("tensor", fns, waits=wrt, inc=s_pe)
        CL = lambda i: c_rt[:, 8 * i: 8 * i + 8]
        r0 = sc.step("vector", lambda e: e.tensor_copy(CL(0), pX[:, 0:NE]),
                     waits=[trt], inc=s_v)
        pX_guard[0] = r0
        r1 = sc.step("vector", lambda e: e.reduce_max(
            c_rt[:, 56:57], CL(0), axis=AX.X), waits=[r0], inc=s_v)
        r2 = sc.step("vector", lambda e: e.tensor_single_scalar(
            CL(1), CL(0), c_rt[:, 56:57], ALU.is_ge), waits=[r1], inc=s_v)
        r3 = sc.step("vector", lambda e: e.scalar_tensor_tensor(
            CL(2), CL(1), -1e30, CL(0), ALU.mult, ALU.add), waits=[r2], inc=s_v)
        r4 = sc.step("vector", lambda e: e.reduce_max(
            c_rt[:, 57:58], CL(2), axis=AX.X), waits=[r3], inc=s_v)
        r5 = sc.step("vector", lambda e: e.tensor_single_scalar(
            CL(3), CL(2), c_rt[:, 57:58], ALU.is_ge), waits=[r4], inc=s_v)
        r6 = sc.step("vector", lambda e: e.tensor_add(CL(4), CL(1), CL(3)),
                     waits=[r5], inc=s_v)
        r7 = sc.step("scalar", lambda e: e.activation(CL(5), CL(0), AF.Sigmoid),
                     waits=[r0], inc=s_a)
        r8 = sc.step("vector", lambda e: e.tensor_tensor(
            CL(6), CL(5), CL(4), ALU.mult), waits=[r6, r7], inc=s_v)
        r9 = sc.step("vector", lambda e: e.reduce_sum(
            c_rt[:, 58:59], CL(6), axis=AX.X), waits=[r8], inc=s_v)
        r10 = sc.step("vector", lambda e: e.reciprocal(
            c_rt[:, 59:60], c_rt[:, 58:59]), waits=[r9], inc=s_v)
        r11 = sc.step("vector", lambda e, t=t: e.tensor_scalar_mul(
            w_res[:, 8 * t: 8 * t + 8], CL(6), c_rt[:, 59:60]),
            waits=[r10], inc=s_v)
        rt_w[t] = r11

    c_end_pe = (s_pe, s_pe.n)
    c_end_v = (s_v, s_v.n)
    c_x1_dma = dict(x1_dma)
    c_h2_dma = dict(h2_dma)

    if upto == "C":
        return _finish(nc, sc, sems, debug and dict(
            QT_dbg=None, qt=None, AT_dbg=AT_dbg, attn=attn,
            MB_dbg=MB_dbg, W_dbg=W_dbg, nacc=nacc, bias_t=bias_t, w_res=w_res),
            upto)

    acc_ld = {}
    h2b_ld = {}
    y_st = {}
    gu_tok = {}
    prod_tok = {}
    dn_tok = {}
    stt_tok = {}
    wgld = {}
    wuld = {}
    wdld = {}
    down_end = {}

    for b in range(NBLK):
        wacc = [c_end_pe, c_end_v, c_x1_dma[8 * b + 7],
                c_x1_dma[8 * b + 6], y_st.get(b - 1, (None, 0))]
        acc_ld[b] = sc.step("sync", lambda e, b=b: e.dma_start(
            out=d_acc[:].rearrange("p (u d) -> p u d", u=8),
            in_=X1_s[b * BT:(b + 1) * BT, :].rearrange("(u p) d -> p u d", p=NP)),
            waits=wacc, inc=s_acc[b % 2], inc_k=16)
        h2b_ld[b] = sc.step("sync", lambda e, b=b: e.dma_start(
            out=d_h2b[:].rearrange("p (j u) -> p j u", j=8),
            in_=H2T_s[:, b * BT:(b + 1) * BT].rearrange("(j p) u -> p j u", p=NP)),
            waits=[c_h2_dma[8 * b + 7], c_h2_dma[8 * b + 6], c_end_pe,
                   gu_tok.get((b - 1, 8, 15, 1), (None, 0))],
            inc=s_h2b[b % 2], inc_k=16)
        for e_i in range(9):
            shared = (e_i == NE)
            def sched_wld(qd, when_tok):
                wgld[(b, e_i, qd)] = sc.step("gpsimd",
                    lambda e, qd=qd, e_i=e_i: e.dma_start(
                        out=d_wg[qd % 2][:], in_=gw_d[e_i, qd]),
                    waits=[when_tok, c_end_pe], inc=s_wg[qd % 2], inc_k=16)
                wuld[(b, e_i, qd)] = sc.step("gpsimd",
                    lambda e, qd=qd, e_i=e_i: e.dma_start(
                        out=d_wu[qd % 2][:], in_=uw_d[e_i, qd]),
                    waits=[when_tok, c_end_pe], inc=s_wu[qd % 2], inc_k=16)

            def guard_gu(qd):
                key = (b, e_i, 4 * 0 + qd - 2)
                if qd >= 2:
                    return gu_tok.get((b, e_i, (qd - 2) * 4 + 3, 1), (None, 0))
                pe_i, pb = (e_i - 1, b) if e_i > 0 else (8, b - 1)
                if pb < 0:
                    return (None, 0)
                return gu_tok.get((pb, pe_i, (qd + 2) * 4 + 3, 1), (None, 0))

            sched_wld(0, guard_gu(0))
            sched_wld(1, guard_gu(1))
            for qd in range(4):
                for ici in range(4):
                    ic = 4 * qd + ici
                    for pn in range(2):
                        P = (pM0 if (ic * 2 + pn) % 2 == 0 else pM1)[:]
                        fns = []
                        for k in range(8):
                            fns.append(lambda e, k=k, qd=qd, ici=ici, pn=pn, P=P:
                                e.matmul(P[:, 0:PAN],
                                         d_wg[qd % 2][:, k * PAN + ici * NP:
                                                      k * PAN + (ici + 1) * NP],
                                         d_h2b[:, k * KB + pn * PAN:
                                               k * KB + (pn + 1) * PAN],
                                         start=(k == 0), stop=(k == 7)))
                        for k in range(8):
                            fns.append(lambda e, k=k, qd=qd, ici=ici, pn=pn, P=P:
                                e.matmul(P[:, PAN:2 * PAN],
                                         d_wu[qd % 2][:, k * PAN + ici * NP:
                                                      k * PAN + (ici + 1) * NP],
                                         d_h2b[:, k * KB + pn * PAN:
                                               k * KB + (pn + 1) * PAN],
                                         start=(k == 0), stop=(k == 7)))
                        pk = ic * 2 + pn - 2
                        pguard = (prod_tok.get((b, e_i, pk), (None, 0)) if pk >= 0
                                  else prod_tok.get(
                                      (b, e_i - 1, 32 + pk) if e_i > 0
                                      else (b - 1, 8, 32 + pk), (None, 0)))
                        wmm = [wgld[(b, e_i, qd)], wuld[(b, e_i, qd)], h2b_ld[b],
                               c_end_v, pguard,
                               down_end.get((b, e_i - 1) if e_i > 0 else (b - 1, 8),
                                            (None, 0))]
                        tokm = sc.step("tensor", fns, waits=wmm, inc=s_pe)
                        gu_tok[(b, e_i, ic, pn)] = tokm
                        gslot = d_g[(ic * 2 + pn) % 2]
                        tsi = sc.step("scalar", lambda e, gslot=gslot, P=P:
                            e.activation(gslot[:], P[:, 0:PAN], AF.Silu),
                            waits=[tokm], inc=s_a)
                        tpr = sc.step("vector",
                            lambda e, gslot=gslot, P=P, ic=ic, pn=pn:
                            e.tensor_tensor(
                                d_prod[:, ic * KB + pn * PAN: ic * KB + (pn + 1) * PAN],
                                gslot[:], P[:, PAN:2 * PAN], ALU.mult),
                            waits=[tsi,
                                   down_end.get((b, e_i - 1) if e_i > 0 else (b - 1, 8),
                                                (None, 0))],
                            inc=s_v)
                        prod_tok[(b, e_i, ic * 2 + pn)] = tpr
                if qd + 2 <= 3:
                    sched_wld(qd + 2, gu_tok[(b, e_i, 4 * qd + 3, 1)])
            wdld[(b, e_i)] = sc.step("gpsimd", lambda e, e_i=e_i: e.dma_start(
                out=d_wd[:], in_=dw_d[e_i]),
                waits=[dn_tok.get((b, e_i - 1, 7, 1) if e_i > 0 else (b - 1, 8, 7, 1),
                                  (None, 0)), c_end_pe],
                inc=s_wd, inc_k=16)
            prod_all = prod_tok[(b, e_i, 31)]
            for tc in range(8):
                for n in range(2):
                    Pd = (pM2[:, n * PAN:(n + 1) * PAN] if tc % 2 == 0
                          else (pT[:] if n == 0 else pX[:]))
                    fns = [
                        (lambda e, ic=ic, tc=tc, n=n, Pd=Pd: e.matmul(
                            Pd, d_prod[:, ic * KB + tc * NP:ic * KB + (tc + 1) * NP],
                            d_wd[:, ic * KB + n * PAN: ic * KB + (n + 1) * PAN],
                            start=(ic == 0), stop=(ic == 15)))
                        for ic in range(16)
                    ]
                    wdn = [prod_all, wdld[(b, e_i)],
                           stt_tok.get((b, e_i, tc - 2, n), (None, 0))]
                    if tc % 2 == 1 and n == 1:
                        wdn.append(pX_guard[0] or (None, 0))
                    tokd = sc.step("tensor", fns, waits=wdn, inc=s_pe)
                    dn_tok[(b, e_i, tc, n)] = tokd
                    if shared:
                        tstt = sc.step("vector",
                            lambda e, tc=tc, n=n, Pd=Pd: e.scalar_tensor_tensor(
                                d_acc[:, tc * KB + n * PAN: tc * KB + (n + 1) * PAN],
                                Pd, 1.0,
                                d_acc[:, tc * KB + n * PAN: tc * KB + (n + 1) * PAN],
                                ALU.mult, ALU.add),
                            waits=[tokd, acc_ld[b], rt_w[8 * b + tc]], inc=s_v)
                    else:
                        tstt = sc.step("vector",
                            lambda e, tc=tc, n=n, Pd=Pd, b=b, e_i=e_i:
                            e.scalar_tensor_tensor(
                                d_acc[:, tc * KB + n * PAN: tc * KB + (n + 1) * PAN],
                                Pd, w_res[:, 8 * (8 * b + tc) + e_i:
                                          8 * (8 * b + tc) + e_i + 1],
                                d_acc[:, tc * KB + n * PAN: tc * KB + (n + 1) * PAN],
                                ALU.mult, ALU.add),
                            waits=[tokd, acc_ld[b], rt_w[8 * b + tc]], inc=s_v)
                    stt_tok[(b, e_i, tc, n)] = tstt
                    if tc % 2 == 1 and n == 1:
                        pX_guard[0] = tstt
            down_end[(b, e_i)] = dn_tok[(b, e_i, 7, 1)]
        y_st[b] = sc.step("sync", lambda e, b=b: e.dma_start(
            out=y_d[b * BT:(b + 1) * BT, :].rearrange("(u p) d -> p u d", p=NP),
            in_=d_acc[:].rearrange("p (u d) -> p u d", u=8)),
            waits=[stt_tok[(b, 8, 7, 1)]] +
                  [stt_tok[(b, ei, tc, n)] for ei in range(9)
                   for tc in range(8) for n in range(2)],
            inc=s_y, inc_k=16)

    return _finish(nc, sc, sems, debug and dict(
        QT_dbg=None, qt=None, AT_dbg=None, attn=None,
        MB_dbg=None, W_dbg=W_dbg, nacc=nacc, bias_t=bias_t, w_res=w_res),
        upto)


def _finish(nc, sc, sems, dbg, upto):
    s_y = sems["s_y"]
    s_v = sems["s_v"]
    s_a = sems["s_a"]
    s_pe = sems["s_pe"]
    if dbg:
        if dbg.get("qt") is not None:
            for m in range(NPAIR):
                sc.step("sync", lambda e, m=m: e.dma_start(
                    out=dbg["QT_dbg"][m], in_=dbg["qt"][m][:]),
                    waits=[(s_v, s_v.n), (s_pe, s_pe.n), (s_a, s_a.n)],
                    inc=s_y, inc_k=16)
        if dbg.get("attn") is not None:
            for m in range(NPAIR):
                sc.step("sync", lambda e, m=m: e.dma_start(
                    out=dbg["AT_dbg"][m], in_=dbg["attn"][m][:]),
                    waits=[(s_v, s_v.n), (s_pe, s_pe.n), (s_a, s_a.n)],
                    inc=s_y, inc_k=16)
        if dbg.get("MB_dbg") is not None:
            sc.step("sync", lambda e: e.dma_start(
                out=dbg["MB_dbg"], in_=dbg["nacc"][0:1, 0:8]),
                waits=[(s_v, s_v.n), (s_a, s_a.n)], inc=s_y, inc_k=16)
        if dbg.get("W_dbg") is not None:
            sc.step("sync", lambda e: e.dma_start(
                out=dbg["W_dbg"], in_=dbg["w_res"][:]),
                waits=[(s_v, s_v.n)], inc=s_y, inc_k=16)
    fy = s_y.n
    final_waits = [(sm, sm.n) for sm in sems["all_dma"] if sm.n > 0 and sm is not s_y]
    final_waits += [(s_v, s_v.n), (s_a, s_a.n), (s_pe, s_pe.n)]
    sc.step("sync", lambda e: e.wait_ge(s_y.h, fy), waits=final_waits)

    with nc.Block() as block:
        block.sync(lambda e: sc.replay("sync", e))
        block.tensor(lambda e: sc.replay("tensor", e))
        block.vector(lambda e: sc.replay("vector", e))
        block.scalar(lambda e: sc.replay("scalar", e))
        block.gpsimd(lambda e: sc.replay("gpsimd", e))
    return nc


def make_in_maps(inputs, n_cores=8):
    x = np.asarray(inputs["L0_attn_hidden_states"], dtype=np.float32)
    f32 = lambda a: np.ascontiguousarray(np.asarray(a, dtype=np.float32))

    def lay_proj(w):
        w = np.asarray(w, np.float32)
        return np.ascontiguousarray(
            w.reshape(8, NP, w.shape[1]).transpose(1, 0, 2).reshape(NP, -1))

    def lay_gu(w):
        w = np.asarray(w, np.float32).reshape(8, NP, I2).transpose(1, 0, 2)
        return np.ascontiguousarray(
            np.stack([w[:, :, qd * PAN:(qd + 1) * PAN].reshape(NP, -1)
                      for qd in range(4)]))

    def lay_dw(w):
        w = np.asarray(w, np.float32)
        return np.ascontiguousarray(
            w.reshape(16, NP, D).transpose(1, 0, 2).reshape(NP, -1))

    wq = lay_proj(inputs["L0_attn_Wq"])
    wkv = lay_proj(inputs["L0_attn_Wkv"])
    wo = lay_proj(inputs["L0_attn_Wo"])
    bo = f32(inputs["L0_attn_bo"]).reshape(1, D)
    gw_all = np.stack([lay_gu(np.asarray(inputs["L0_ffn_gate_w"], np.float32)[e])
                       for e in range(NE)] +
                      [lay_gu(np.asarray(inputs["L0_ffn_shared_gate"], np.float32).T)])
    uw_all = np.stack([lay_gu(np.asarray(inputs["L0_ffn_up_w"], np.float32)[e])
                       for e in range(NE)] +
                      [lay_gu(np.asarray(inputs["L0_ffn_shared_up"], np.float32).T)])
    dw_all = np.stack([lay_dw(np.asarray(inputs["L0_ffn_down_w"], np.float32)[e])
                       for e in range(NE)] +
                      [lay_dw(np.asarray(inputs["L0_ffn_shared_down"], np.float32).T)])
    rwT = f32(np.asarray(inputs["L0_ffn_router_w"], dtype=np.float32).T)
    rb = f32(inputs["L0_ffn_routing_bias"]).reshape(1, NE)
    idb = np.eye(NP, dtype=ml_dtypes.bfloat16)
    idf = np.eye(NP, dtype=np.float32)
    shared = dict(wq=wq, wkv=wkv, wo=wo, bo=bo,
                  gw=gw_all, uw=uw_all, dw=dw_all, rwT=rwT, rb=rb,
                  idb=idb, idf=idf)
    in_maps = []
    for c in range(n_cores):
        bb, hh = c // 2, c % 2
        m = dict(shared)
        m["x_own"] = np.ascontiguousarray(x[bb, hh * TOK:(hh + 1) * TOK, :])
        m["x_oth"] = np.ascontiguousarray(x[bb, (1 - hh) * TOK:(2 - hh) * TOK, :])
        in_maps.append(m)
    return in_maps


def kernel(**inputs):
    nc = build_program(upto="D", debug=False)
    in_maps = make_in_maps(inputs)
    res = run_bass_kernel_spmd(nc, in_maps, core_ids=list(range(8)))
    out = np.empty((4, SEQ, D), dtype=np.float32)
    for c in range(8):
        bb, hh = c // 2, c % 2
        out[bb, hh * TOK:(hh + 1) * TOK, :] = np.asarray(res.results[c]["y"])
    return out


# version 04
# speedup vs baseline: 1.6253x; 1.0998x over previous
import math
import contextlib
import numpy as np
import ml_dtypes
import concourse.bass as bass
import concourse.mybir as mybir
from concourse.bass_utils import run_bass_kernel_spmd

F32 = mybir.dt.float32
BF16 = mybir.dt.bfloat16
AF = mybir.ActivationFunctionType
ALU = mybir.AluOpType
AX = mybir.AxisListType

NP = 128
D = 1024
TOK = 4096
SEQ = 8192
NH = 16
DH = 64
NPAIR = 8
I2 = 2048
NE = 8
PAN = 512
NBLK = 4
BT = 1024
EPS = 1e-5
KB = 1024


class Sem:
    def __init__(self, handle):
        self.h = handle
        self.n = 0


class Sched:
    ENGS = ("sync", "tensor", "vector", "scalar", "gpsimd")

    def __init__(self):
        self.steps = {e: [] for e in self.ENGS}
        self.last_wait = {e: {} for e in self.ENGS}

    def step(self, eng, fns, waits=(), inc=None, inc_k=1):
        if not isinstance(fns, (list, tuple)):
            fns = [fns]
        lw = self.last_wait[eng]
        eff = []
        for (s, v) in waits:
            if s is None or v <= 0:
                continue
            if lw.get(id(s), -1) >= v:
                continue
            lw[id(s)] = v
            eff.append((s, v))
        tok = None
        if inc is not None:
            inc.n += inc_k
            tok = (inc, inc.n)
        self.steps[eng].append((eff, list(fns), inc, inc_k))
        return tok

    def replay(self, eng_name, eng):
        for waits, fns, inc, inc_k in self.steps[eng_name]:
            for (s, v) in waits:
                eng.wait_ge(s.h, v)
            instr = None
            for f in fns:
                instr = f(eng)
            if inc is not None:
                instr.then_inc(inc.h, inc_k)


def build_program(upto="D", debug=False):
    nc = bass.Bass("TRN2", target_bir_lowering=False)

    dt = nc.dram_tensor
    x_own = dt("x_own", [TOK, D], F32, kind="ExternalInput").ap()
    x_oth = dt("x_oth", [TOK, D], F32, kind="ExternalInput").ap()
    wq_d = dt("wq", [NP, 8 * KB], F32, kind="ExternalInput").ap()
    wkv_d = dt("wkv", [NP, 16 * KB], F32, kind="ExternalInput").ap()
    wo_d = dt("wo", [NP, 8 * KB], F32, kind="ExternalInput").ap()
    bo_d = dt("bo", [1, D], F32, kind="ExternalInput").ap()
    gw_d = dt("gw", [9, 4, NP, 4 * KB], F32, kind="ExternalInput").ap()
    uw_d = dt("uw", [9, 4, NP, 4 * KB], F32, kind="ExternalInput").ap()
    dw_d = dt("dw", [9, NP, 16 * KB], F32, kind="ExternalInput").ap()
    rwT_d = dt("rwT", [D, NE], F32, kind="ExternalInput").ap()
    rb_d = dt("rb", [1, NE], F32, kind="ExternalInput").ap()
    idb_d = dt("idb", [NP, NP], BF16, kind="ExternalInput").ap()
    idf_d = dt("idf", [NP, NP], F32, kind="ExternalInput").ap()

    y_d = dt("y", [TOK, D], F32, kind="ExternalOutput").ap()

    def scratch(name, shape, dtype):
        kind = "ExternalOutput" if debug else "Internal"
        return dt(name, shape, dtype, kind=kind).ap()

    gwb = dt("gwb", [9, 4, NP, 4 * KB], BF16, kind="Internal").ap()
    uwb = dt("uwb", [9, 4, NP, 4 * KB], BF16, kind="Internal").ap()
    dwb = dt("dwb", [9, NP, 16 * KB], BF16, kind="Internal").ap()
    KT_s = scratch("KT_s", [D, SEQ], BF16)
    V_s = scratch("V_s", [NH, SEQ, DH], BF16)
    X1_s = scratch("X1_s", [TOK, D], F32)
    H2T_s = scratch("H2T_s", [D, TOK], BF16)
    if debug:
        QT_dbg = dt("QT_dbg", [NPAIR, NP, TOK], BF16, kind="ExternalOutput").ap()
        AT_dbg = dt("AT_dbg", [NPAIR, NP, TOK], BF16, kind="ExternalOutput").ap()
        MB_dbg = dt("MB_dbg", [1, 8], F32, kind="ExternalOutput").ap()
        W_dbg = dt("W_dbg", [NP, 256], F32, kind="ExternalOutput").ap()

    al = nc.alloc_sbuf_tensor
    ones_col_b = al("ones_col_b", [NP, 1], BF16)
    ones_row_f = al("ones_row_f", [1, NP], F32)
    bias_t = al("bias_t", [NP, 1], F32)
    nacc = al("nacc", [1, 16], F32)
    w_res = al("w_res", [NP, 256], F32)
    bo_row = al("bo_row", [1, D], F32)
    rb_row = al("rb_row", [1, NE], F32)
    rwT_sb = al("rwT_sb", [NP, 8 * NE], F32)
    idb_sb = al("idb_sb", [NP, NP], BF16)
    idf_sb = al("idf_sb", [NP, NP], F32)

    AR_BYTES = 189 * KB
    arena = al("arena", [NP, AR_BYTES], mybir.dt.uint8)
    ar0 = nc.lookup_mloc(arena).addr
    _uid = [0]

    def at(off_kb, shape, dtype, n=None):
        def one(i):
            _uid[0] += 1
            off = ar0 + int(off_kb * KB) + i * shape[1] * mybir.dt.size(dtype)
            off = (off + 31) // 32 * 32
            return nc.alloc_sbuf_tensor_at(f"ar_{_uid[0]}", shape, dtype, offset=off)
        if n is None:
            return one(0)
        return [one(i) for i in range(n)]

    qt = at(0, [NP, TOK], BF16, n=NPAIR)
    a_x = at(64, [NP, D], F32, n=3)
    a_hn = at(76, [NP, D], BF16, n=2)
    a_hnT = at(80, [NP, 4096], BF16, n=2)
    a_kst = at(96, [NP, PAN], BF16, n=2)
    a_vst = at(98, [NP, D], BF16, n=2)
    a_sq = at(102, [NP, PAN], BF16, n=2)
    a_dump = at(104, [NP, D], F32)
    a_rows = at(108, [NP, 64], F32)
    wq_sb = at(108.25, [NP, 8 * KB], BF16)
    wkv_sb = at(124.25, [NP, 16 * KB], BF16)
    attn = at(64, [NP, TOK], BF16, n=NPAIR)
    b_k = at(128, [NP, SEQ], BF16, n=2)
    b_v = at(160, [NP, 64 * 66], BF16, n=2)
    b_pt = at(176.8, [NP, KB], BF16, n=4)
    b_bc = at(184.9, [64, PAN], F32)
    b_rs = at(186.9, [1, PAN], F32)
    wo_sb = at(128, [NP, 8 * KB], BF16)
    c_x = at(144, [NP, D], F32, n=2)
    c_x1 = at(152, [NP, D], F32, n=2)
    c_h2f = at(160, [NP, D], F32, n=2)
    c_h2T32 = at(168, [NP, D], F32, n=2)
    c_h2Tb = at(176, [NP, D], BF16, n=2)
    c_rows = at(180, [NP, 64], F32)
    c_rt = at(180.25, [NP, 64], F32)
    c_dump = at(180.5, [NP, D], F32)
    d_acc = at(0, [NP, 8 * KB], F32)
    d_h2b = at(32, [NP, 8 * KB], BF16)
    d_prod = at(48, [NP, 16 * KB], BF16)
    d_wg = at(80, [NP, 4 * KB], BF16, n=2)
    d_wu = at(96, [NP, 4 * KB], BF16, n=2)
    d_wd = at(112, [NP, 16 * KB], BF16)
    d_g = at(144, [NP, PAN], F32, n=2)

    pM0 = nc.alloc_psum_tensor("pM0", [NP, 1024], F32)
    pM1 = nc.alloc_psum_tensor("pM1", [NP, 1024], F32)
    pM2 = nc.alloc_psum_tensor("pM2", [NP, 1024], F32)
    pT = nc.alloc_psum_tensor("pT", [NP, 512], F32)
    pX = nc.alloc_psum_tensor("pX", [NP, 512], F32)
    PTB0 = pT[:].bitcast(BF16)
    PTB1 = pM2[:, 0:512].bitcast(BF16)

    mk = lambda nm: Sem(nc.alloc_semaphore(name=nm))
    s_pre = mk("s_pre")
    s_xb = [mk(f"s_xb{i}") for i in range(3)]
    s_kst = [mk(f"s_kst{i}") for i in range(2)]
    s_vst = [mk(f"s_vst{i}") for i in range(2)]
    s_kld = [mk(f"s_kld{i}") for i in range(2)]
    s_vld = [mk(f"s_vld{i}") for i in range(2)]
    s_wq, s_wkv, s_wo, s_wd = mk("s_wq"), mk("s_wkv"), mk("s_wo"), mk("s_wd")
    s_wcast = mk("s_wcast")
    s_cx = [mk(f"s_cx{i}") for i in range(2)]
    s_x1 = [mk(f"s_x1{i}") for i in range(2)]
    s_h2 = [mk(f"s_h2{i}") for i in range(2)]
    s_wg = [mk(f"s_wg{i}") for i in range(2)]
    s_wu = [mk(f"s_wu{i}") for i in range(2)]
    s_acc = [mk(f"s_acc{i}") for i in range(2)]
    s_h2b = [mk(f"s_h2b{i}") for i in range(2)]
    s_y, s_pe, s_v, s_a = mk("s_y"), mk("s_pe"), mk("s_v"), mk("s_a")
    all_dma_sems = ([s_pre] + s_xb + s_kst + s_vst + s_kld + s_vld +
                    [s_wq, s_wkv, s_wo, s_wd, s_wcast] + s_cx + s_x1 + s_h2 +
                    s_wg + s_wu + s_acc + s_h2b + [s_y])
    sems = dict(all_dma=all_dma_sems, s_y=s_y, s_pe=s_pe, s_v=s_v, s_a=s_a)

    sc = Sched()

    sc.step("vector", lambda e: e.memset(ones_col_b[:], 1.0), inc=s_v)
    sc.step("vector", lambda e: e.memset(ones_row_f[:], 1.0), inc=s_v)
    sc.step("vector", lambda e: e.memset(nacc[:], 0.0), inc=s_v)
    ones_ready = (s_v, s_v.n)
    sc.step("sync", lambda e: e.dma_start(out=bo_row[:], in_=bo_d), inc=s_pre, inc_k=16)
    sc.step("sync", lambda e: e.dma_start(out=rb_row[:], in_=rb_d), inc=s_pre, inc_k=16)
    sc.step("sync", lambda e: e.dma_start(
        out=rwT_sb[:].rearrange("p (j e) -> p j e", j=8),
        in_=rwT_d.rearrange("(j p) e -> p j e", p=NP)), inc=s_pre, inc_k=16)
    sc.step("sync", lambda e: e.dma_start(out=idb_sb[:], in_=idb_d), inc=s_pre, inc_k=16)
    sc.step("sync", lambda e: e.dma_start(out=idf_sb[:], in_=idf_d), inc=s_pre, inc_k=16)
    small_ld = (s_pre, s_pre.n)
    sc.step("gpsimd", lambda e: e.dma_start(out=wq_sb[:], in_=wq_d),
            inc=s_wq, inc_k=16)
    wq_ld = (s_wq, s_wq.n)
    sc.step("gpsimd", lambda e: e.dma_start(out=wkv_sb[:], in_=wkv_d),
            inc=s_wkv, inc_k=16)
    wkv_ld = (s_wkv, s_wkv.n)
    if upto == "D":
        for e_i in range(9):
            for qd in range(4):
                sc.step("gpsimd", lambda e, e_i=e_i, qd=qd: e.dma_start(
                    out=gwb[e_i, qd], in_=gw_d[e_i, qd]), inc=s_wcast, inc_k=16)
                sc.step("gpsimd", lambda e, e_i=e_i, qd=qd: e.dma_start(
                    out=uwb[e_i, qd], in_=uw_d[e_i, qd]), inc=s_wcast, inc_k=16)
            sc.step("gpsimd", lambda e, e_i=e_i: e.dma_start(
                out=dwb[e_i], in_=dw_d[e_i]), inc=s_wcast, inc_k=16)
    wcast_done = (s_wcast, s_wcast.n)

    RC = {"sum": 0, "m": 2, "msq": 4, "vep": 6, "var": 8, "ln": 10, "r": 12}

    def rcol(name, t):
        return a_rows[:, RC[name] + (t % 2): RC[name] + (t % 2) + 1]

    norm_tok = [None]
    tile_norm_done = {}
    tile_tp_done = {}
    tile_ev_done = {}
    panel_mm_done = {}
    kst_dma = {}
    kst_sq = {}
    vst_dma = {}
    ps_ring = [pM0[:, 0:512], pM0[:, 512:1024], pM1[:, 0:512], pM1[:, 512:1024]]
    ps_tok = [None, None, None, None]
    ps_i = [0]

    def next_ps():
        i = ps_i[0] % 4
        ps_i[0] += 1
        return i

    a_mm_end = [None]
    ASQ_guard = [None, None]
    pX_guard = [None]

    for t in range(64):
        xb = a_x[t % 3]
        src = (x_own if t < 32 else x_oth)
        tsl = (t % 32)
        tok_x = sc.step("sync", lambda e, s=src, q=tsl, xb=xb: e.dma_start(
            out=xb[:], in_=s[q * NP:(q + 1) * NP, :]),
            waits=[tile_norm_done.get(t - 3, (None, 0))],
            inc=s_xb[t % 3], inc_k=16)
        v1 = sc.step("vector", lambda e, xb=xb, t=t: e.reduce_sum(
            rcol("sum", t), xb[:], axis=AX.X), waits=[tok_x], inc=s_v)
        a1 = sc.step("scalar", lambda e, xb=xb, t=t: e.activation(
            a_dump[:], xb[:], AF.Square, accum_out=rcol("msq", t)),
            waits=[tok_x], inc=s_a)
        v2 = sc.step("vector", lambda e, t=t: e.tensor_scalar_mul(
            rcol("m", t), rcol("sum", t), 1.0 / D), waits=[v1], inc=s_v)
        v4 = sc.step("vector", lambda e, t=t: e.tensor_scalar(
            rcol("vep", t), rcol("msq", t), 1.0 / D, EPS, ALU.mult, ALU.add),
            waits=[a1], inc=s_v)
        v3 = sc.step("vector", lambda e, t=t: e.tensor_tensor(
            rcol("var", t), rcol("m", t), rcol("m", t), ALU.mult),
            waits=[v2], inc=s_v)
        v5 = sc.step("vector", lambda e, t=t: e.tensor_sub(
            rcol("vep", t), rcol("vep", t), rcol("var", t)),
            waits=[v3, v4], inc=s_v)
        a2 = sc.step("scalar", lambda e, t=t: e.activation(
            rcol("ln", t), rcol("vep", t), AF.Ln), waits=[v5], inc=s_a)
        a3 = sc.step("scalar", lambda e, t=t: e.activation(
            rcol("r", t), rcol("ln", t), AF.Exp, scale=-0.5),
            waits=[a2], inc=s_a)
        hb = a_hn[t % 2]
        v6 = sc.step("vector", lambda e, xb=xb, hb=hb, t=t: e.tensor_scalar(
            hb[:], xb[:], rcol("m", t), rcol("r", t), ALU.subtract, ALU.mult),
            waits=[a3, tile_tp_done.get(t - 2, (None, 0))], inc=s_v)
        tile_norm_done[t] = v6
        PTB = (PTB0 if t % 2 == 0 else PTB1)
        fns = [
            (lambda e, hb=hb, c=c, PTB=PTB: e.transpose(
                PTB[:, c * NP:(c + 1) * NP], hb[:, c * NP:(c + 1) * NP],
                idb_sb[:]))
            for c in range(8)
        ]
        jprev = panel_mm_done.get(t // 4 - 2, (None, 0))
        p1 = sc.step("tensor", fns, waits=[v6, tile_ev_done.get(t - 2, (None, 0)),
                                           small_ld], inc=s_pe)
        hT = a_hnT[(t // 4) % 2]
        ev = sc.step("vector", lambda e, hT=hT, t=t, PTB=PTB: e.tensor_copy(
            hT[:].rearrange("p (c u) -> p c u", c=8)[:, :, (t % 4) * NP:(t % 4 + 1) * NP],
            PTB[:].rearrange("p (c u) -> p c u", c=8)),
            waits=[p1, jprev], inc=s_v)
        tile_ev_done[t] = ev
        tile_tp_done[t] = p1

        if t % 4 != 3:
            continue
        j = t // 4
        own = j < 8
        hT = a_hnT[j % 2]
        pan_ready = ev
        last_mm = None

        def mm_group(lhs_fn, rhs_fn, extra_waits, n_k=8):
            nonlocal last_mm
            si = next_ps()
            ps = ps_ring[si]
            fns = [
                (lambda e, k=k, si=si: e.matmul(
                    ps_ring[si], lhs_fn(k), rhs_fn(k),
                    start=(k == 0), stop=(k == n_k - 1)))
                for k in range(n_k)
            ]
            w = [pan_ready] + extra_waits
            if ps_tok[si] is not None:
                w.append(ps_tok[si])
            tokm = sc.step("tensor", fns, waits=w, inc=s_pe)
            last_mm = tokm
            return si, tokm

        if own:
            jq = j
            for m in range(8):
                si, tokm = mm_group(
                    lambda k, m=m: wq_sb[:, k * KB + m * NP: k * KB + (m + 1) * NP],
                    lambda k, hT=hT: hT[:, k * PAN:(k + 1) * PAN], [wq_ld])
                tev = sc.step("vector", lambda e, m=m, jq=jq, si=si: e.tensor_copy(
                    qt[m][:, jq * PAN:(jq + 1) * PAN], ps_ring[si]),
                    waits=[tokm], inc=s_v)
                ps_tok[si] = tev
                sqb = a_sq[m % 2]
                tsq = sc.step("scalar", lambda e, m=m, jq=jq, sqb=sqb: e.activation(
                    sqb[:], qt[m][:, jq * PAN:(jq + 1) * PAN], AF.Square),
                    waits=[tev, ASQ_guard[m % 2] or (None, 0)], inc=s_a)
                tnm = sc.step("tensor", lambda e, sqb=sqb: e.matmul(
                    pX[0:1, 0:PAN], ones_col_b[:], sqb[:], start=True, stop=True),
                    waits=[tsq, ones_ready, pX_guard[0] or (None, 0)], inc=s_pe)
                ASQ_guard[m % 2] = tnm
                trm = sc.step("vector", lambda e: e.reduce_max(
                    nacc[0:1, 8:9], pX[0:1, 0:PAN], axis=AX.X),
                    waits=[tnm], inc=s_v)
                tmx = sc.step("vector", lambda e: e.tensor_max(
                    nacc[0:1, 0:1], nacc[0:1, 0:1], nacc[0:1, 8:9]),
                    waits=[trm], inc=s_v)
                pX_guard[0] = trm
                norm_tok[0] = tmx
        for c in range(8):
            si, tokm = mm_group(
                lambda k, c=c: wkv_sb[:, k * 2048 + c * NP: k * 2048 + (c + 1) * NP],
                lambda k, hT=hT: hT[:, k * PAN:(k + 1) * PAN], [wkv_ld])
            kb = a_kst[c % 2]
            tev = sc.step("vector", lambda e, kb=kb, si=si: e.tensor_copy(
                kb[:], ps_ring[si]),
                waits=[tokm, kst_dma.get((j, c - 2), kst_dma.get((j - 1, c + 6), (None, 0))),
                       kst_sq.get((j, c - 2), kst_sq.get((j - 1, c + 6), (None, 0)))],
                inc=s_v)
            ps_tok[si] = tev
            td = sc.step("sync", lambda e, kb=kb, c=c, j=j: e.dma_start(
                out=KT_s[c * NP:(c + 1) * NP, j * PAN:(j + 1) * PAN], in_=kb[:]),
                waits=[tev], inc=s_kst[c % 2], inc_k=16)
            kst_dma[(j, c)] = td
            sqb = a_sq[c % 2]
            tsq = sc.step("scalar", lambda e, kb=kb, sqb=sqb: e.activation(
                sqb[:], kb[:], AF.Square),
                waits=[tev, ASQ_guard[c % 2] or (None, 0)], inc=s_a)
            tnm = sc.step("tensor", lambda e, sqb=sqb: e.matmul(
                pX[0:1, 0:PAN], ones_col_b[:], sqb[:], start=True, stop=True),
                waits=[tsq, pX_guard[0] or (None, 0)], inc=s_pe)
            ASQ_guard[c % 2] = tnm
            last_mm = tnm
            trm = sc.step("vector", lambda e: e.reduce_max(
                nacc[0:1, 9:10], pX[0:1, 0:PAN], axis=AX.X),
                waits=[tnm], inc=s_v)
            tmx = sc.step("vector", lambda e: e.tensor_max(
                nacc[0:1, 1:2], nacc[0:1, 1:2], nacc[0:1, 9:10]),
                waits=[trm], inc=s_v)
            pX_guard[0] = trm
            kst_sq[(j, c)] = tsq
            norm_tok[0] = tmx
        for u in range(4):
            vb = a_vst[u % 2]
            toks = []
            for n in range(2):
                si, tokm = mm_group(
                    lambda k, u=u, hT=hT: hT[:, k * PAN + u * NP: k * PAN + (u + 1) * NP],
                    lambda k, n=n: wkv_sb[:, k * 2048 + KB + n * PAN: k * 2048 + KB + (n + 1) * PAN],
                    [wkv_ld])
                tev = sc.step("vector", lambda e, vb=vb, n=n, si=si: e.tensor_copy(
                    vb[:, n * PAN:(n + 1) * PAN], ps_ring[si]),
                    waits=[tokm, vst_dma.get((j, u - 2), vst_dma.get((j - 1, u + 2), (None, 0)))],
                    inc=s_v)
                ps_tok[si] = tev
                toks.append(tev)
            tt = j * 4 + u
            td = sc.step("sync", lambda e, vb=vb, tt=tt: e.dma_start(
                out=V_s[:, tt * NP:(tt + 1) * NP, :].rearrange("h p n -> p h n"),
                in_=vb[:].rearrange("p (h n) -> p h n", h=NH)),
                waits=toks, inc=s_vst[u % 2], inc_k=16)
            vst_dma[(j, u)] = td
        panel_mm_done[j] = last_mm

    a_scratch_done = [(s_kst[0], s_kst[0].n), (s_kst[1], s_kst[1].n),
                      (s_vst[0], s_vst[0].n), (s_vst[1], s_vst[1].n)]
    a_mm_end = panel_mm_done[15]

    bv1 = sc.step("vector", lambda e: e.tensor_tensor(
        nacc[0:1, 2:3], nacc[0:1, 0:1], nacc[0:1, 1:2], ALU.mult),
        waits=[norm_tok[0]], inc=s_v)
    ba1 = sc.step("scalar", lambda e: e.activation(
        nacc[0:1, 3:4], nacc[0:1, 2:3], AF.Ln), waits=[bv1], inc=s_a)
    ba2 = sc.step("scalar", lambda e: e.activation(
        nacc[0:1, 4:5], nacc[0:1, 3:4], AF.Exp, scale=0.5), waits=[ba1], inc=s_a)
    bv2 = sc.step("vector", lambda e: e.tensor_scalar_mul(
        nacc[0:1, 5:6], nacc[0:1, 4:5], -0.125), waits=[ba2], inc=s_v)
    bt1 = sc.step("tensor", lambda e: e.matmul(
        pX[:, 0:1], ones_row_f[:], nacc[0:1, 5:6], start=True, stop=True),
        waits=[bv2, pX_guard[0] or (None, 0)], inc=s_pe)
    bt2 = sc.step("vector", lambda e: e.tensor_copy(bias_t[:], pX[:, 0:1]),
                  waits=[bt1], inc=s_v)
    bias_ready = bt2
    pX_guard[0] = bt2

    if upto == "A":
        return _finish(nc, sc, sems, debug and dict(
            QT_dbg=QT_dbg, qt=qt, AT_dbg=None, attn=None,
            MB_dbg=MB_dbg, W_dbg=None, nacc=nacc, bias_t=bias_t, w_res=w_res),
            upto)

    kld = {}
    vld = {}
    for p in range(2):
        kld[p] = sc.step("sync", lambda e, p=p: e.dma_start(
            out=b_k[p][:], in_=KT_s[p * NP:(p + 1) * NP, :]),
            waits=a_scratch_done + [a_mm_end], inc=s_kld[p % 2], inc_k=16)
    for hh in range(2):
        vld[hh] = sc.step("sync", lambda e, hh=hh: e.dma_start(
            out=b_v[hh % 2][:].rearrange("p (c w) -> p c w", w=66)[:, :, 0:64],
            in_=V_s[hh].rearrange("(c p) n -> p c n", p=NP)),
            waits=a_scratch_done + [a_mm_end], inc=s_vld[hh % 2], inc_k=16)
    vmemset = {}
    for sl in range(2):
        vmemset[sl] = sc.step("vector", lambda e, sl=sl: e.memset(
            b_v[sl][:].rearrange("p (c w) -> p c w", w=66)[:, :, 64:65], 1.0),
            waits=[a_mm_end], inc=s_v)

    exp_tok = {}
    pv_tok = {}
    smm_end = {}
    norm_done = {}
    pair_end = {}
    for p in range(NPAIR):
        kbuf = b_k[p % 2]

        def sfns(head, g, p=p, kbuf=kbuf):
            Sp = (pM0 if head == 0 else pM1)[:]
            rbase = 64 * head
            q = cur_q[0]
            return [
                (lambda e, c=c, Sp=Sp, kbuf=kbuf, rbase=rbase, p=p, q=q, cc=cc:
                 e.matmul(Sp[:, cc * PAN:(cc + 1) * PAN],
                          kbuf[rbase:rbase + 64, c * NP:(c + 1) * NP],
                          qt[p][rbase:rbase + 64, q * PAN:(q + 1) * PAN],
                          start=True, stop=True))
                for cc, c in enumerate((2 * g, 2 * g + 1))
            ]

        def pvfns(head, g):
            pb = b_pt[2 * head + g % 2]
            vbuf = b_v[head]
            ocol = head * PAN
            return [
                (lambda e, c=c, pb=pb, vbuf=vbuf, g=g, cc=cc, ocol=ocol:
                 e.matmul(pM2[0:65, ocol:ocol + PAN],
                          vbuf[:, c * 66:c * 66 + 65],
                          pb[:, cc * PAN:(cc + 1) * PAN],
                          start=(g == 0 and cc == 0), stop=(g == 31 and cc == 1)))
                for cc, c in enumerate((2 * g, 2 * g + 1))
            ]

        cur_q = [0]
        for q in range(8):
            cur_q[0] = q
            w0 = [kld[p], vld[2 * p], vld[2 * p + 1],
                  norm_done.get((2 * p, q - 1),
                                norm_done.get((2 * p - 1, 7), (None, 0))),
                  norm_done.get((2 * p + 1, q - 1), (None, 0))]
            if p == 0 and q == 0:
                w0 += [vmemset[0], vmemset[1]]
            stok = {}
            stok[0] = sc.step("tensor", sfns(0, 0) + sfns(1, 0), waits=w0,
                              inc=s_pe)
            pvt = {}
            for g in range(32):
                pbA = b_pt[g % 2]
                pbB = b_pt[2 + g % 2]
                teA = sc.step("scalar", lambda e, pbA=pbA: e.activation(
                    pbA[:], pM0[:], AF.Exp, bias=bias_t[:, 0:1], scale=0.125),
                    waits=[stok[g], bias_ready], inc=s_a)
                teB = sc.step("scalar", lambda e, pbB=pbB: e.activation(
                    pbB[:], pM1[:], AF.Exp, bias=bias_t[:, 0:1], scale=0.125),
                    waits=[], inc=s_a)
                if g + 1 < 32:
                    stok[g + 1] = sc.step("tensor", sfns(0, g + 1) + sfns(1, g + 1),
                                          waits=[teB], inc=s_pe)
                    pvt[g] = sc.step("tensor", pvfns(0, g) + pvfns(1, g),
                                     waits=[], inc=s_pe)
                else:
                    pvt[g] = sc.step("tensor", pvfns(0, g) + pvfns(1, g),
                                     waits=[teB], inc=s_pe)
            for head in range(2):
                hh = 2 * p + head
                ocol = head * PAN
                rbase = 64 * head
                n1 = sc.step("vector", lambda e, ocol=ocol: e.reciprocal(
                    b_rs[:], pM2[64:65, ocol:ocol + PAN]),
                    waits=[pvt[31]], inc=s_v)
                n2 = sc.step("tensor", lambda e: e.matmul(
                    pX[0:64, 0:PAN], ones_row_f[0:1, 0:64], b_rs[:],
                    start=True, stop=True),
                    waits=[n1, pX_guard[0] or (None, 0)], inc=s_pe)
                n3 = sc.step("vector", lambda e: e.tensor_copy(
                    b_bc[:], pX[0:64, 0:PAN]), waits=[n2], inc=s_v)
                n4 = sc.step("vector",
                    lambda e, p=p, rbase=rbase, q=q, ocol=ocol: e.tensor_tensor(
                        attn[p][rbase:rbase + 64, q * PAN:(q + 1) * PAN],
                        pM2[0:64, ocol:ocol + PAN], b_bc[:], ALU.mult),
                    waits=[n3], inc=s_v)
                norm_done[(hh, q)] = n4
                pX_guard[0] = n3
        pair_end[p] = pvt[31]
        if p + 1 < NPAIR:
            for head in range(2):
                vld[2 * (p + 1) + head] = sc.step("sync",
                    lambda e, p=p, head=head: e.dma_start(
                        out=b_v[head][:].rearrange(
                            "p (c w) -> p c w", w=66)[:, :, 0:64],
                        in_=V_s[2 * (p + 1) + head].rearrange(
                            "(c p) n -> p c n", p=NP)),
                    waits=[pair_end[p]], inc=s_vld[head], inc_k=16)
        if p + 2 < NPAIR:
            kld[p + 2] = sc.step("sync", lambda e, p=p: e.dma_start(
                out=b_k[p % 2][:], in_=KT_s[(p + 2) * NP:(p + 3) * NP, :]),
                waits=[pair_end[p]], inc=s_kld[p % 2], inc_k=16)

    b_end_pe = pair_end[NPAIR - 1]
    b_end_v = norm_done[(NH - 1, 7)]

    if upto == "B":
        return _finish(nc, sc, sems, debug and dict(
            QT_dbg=QT_dbg, qt=qt, AT_dbg=AT_dbg, attn=attn,
            MB_dbg=MB_dbg, W_dbg=None, nacc=nacc, bias_t=bias_t, w_res=w_res),
            upto)

    wo_ld = sc.step("gpsimd", lambda e: e.dma_start(out=wo_sb[:], in_=wo_d),
        waits=[b_end_pe], inc=s_wo, inc_k=16)

    CC = {"sum": 0, "m": 2, "msq": 4, "vep": 6, "var": 8, "ln": 10, "r": 12}

    def ccol(name, t):
        return c_rows[:, CC[name] + (t % 2): CC[name] + (t % 2) + 1]

    x1_dma = {}
    h2_dma = {}
    x1_add = {}
    h2T_ev = {}
    rt_w = {}
    proj_tok = {}
    for t in range(32):
        xb = c_x[t % 2]
        tok_x = sc.step("sync", lambda e, t=t, xb=xb: e.dma_start(
            out=xb[:], in_=x_own[t * NP:(t + 1) * NP, :]),
            waits=[b_end_pe, x1_add.get(t - 2, (None, 0))],
            inc=s_cx[t % 2], inc_k=16)
        P = (pM0 if t % 2 == 0 else pM1)[:]
        fns = []
        for n in range(2):
            for k in range(8):
                fns.append(lambda e, n=n, k=k, t=t, P=P: e.matmul(
                    P[:, n * PAN:(n + 1) * PAN],
                    attn[k][:, t * NP:(t + 1) * NP],
                    wo_sb[:, k * KB + n * PAN: k * KB + (n + 1) * PAN],
                    start=(k == 0), stop=False))
            fns.append(lambda e, n=n, P=P: e.matmul(
                P[:, n * PAN:(n + 1) * PAN], ones_row_f[:],
                bo_row[0:1, n * PAN:(n + 1) * PAN], start=False, stop=True))
        tp = sc.step("tensor", fns,
                     waits=[wo_ld, small_ld, b_end_v, x1_add.get(t - 2, (None, 0))],
                     inc=s_pe)
        proj_tok[t] = tp
        x1b = c_x1[t % 2]
        tadd = sc.step("vector", lambda e, x1b=x1b, xb=xb, P=P: e.tensor_add(
            x1b[:], P, xb[:]),
            waits=[tp, tok_x, x1_dma.get(t - 2, (None, 0))], inc=s_v)
        x1_add[t] = tadd
        x1_dma[t] = sc.step("sync", lambda e, x1b=x1b, t=t: e.dma_start(
            out=X1_s[t * NP:(t + 1) * NP, :], in_=x1b[:]),
            waits=[tadd], inc=s_x1[t % 2], inc_k=16)
        v1 = sc.step("vector", lambda e, x1b=x1b, t=t: e.reduce_sum(
            ccol("sum", t), x1b[:], axis=AX.X), waits=[tadd], inc=s_v)
        a1 = sc.step("scalar", lambda e, x1b=x1b, t=t: e.activation(
            c_dump[:], x1b[:], AF.Square, accum_out=ccol("msq", t)),
            waits=[tadd], inc=s_a)
        v2 = sc.step("vector", lambda e, t=t: e.tensor_scalar_mul(
            ccol("m", t), ccol("sum", t), 1.0 / D), waits=[v1], inc=s_v)
        v4 = sc.step("vector", lambda e, t=t: e.tensor_scalar(
            ccol("vep", t), ccol("msq", t), 1.0 / D, EPS, ALU.mult, ALU.add),
            waits=[a1], inc=s_v)
        v3 = sc.step("vector", lambda e, t=t: e.tensor_tensor(
            ccol("var", t), ccol("m", t), ccol("m", t), ALU.mult),
            waits=[v2], inc=s_v)
        v5 = sc.step("vector", lambda e, t=t: e.tensor_sub(
            ccol("vep", t), ccol("vep", t), ccol("var", t)),
            waits=[v3, v4], inc=s_v)
        a2 = sc.step("scalar", lambda e, t=t: e.activation(
            ccol("ln", t), ccol("vep", t), AF.Ln), waits=[v5], inc=s_a)
        a3 = sc.step("scalar", lambda e, t=t: e.activation(
            ccol("r", t), ccol("ln", t), AF.Exp, scale=-0.5), waits=[a2], inc=s_a)
        h2b_ = c_h2f[t % 2]
        v6 = sc.step("vector", lambda e, h2b_=h2b_, x1b=x1b, t=t: e.tensor_scalar(
            h2b_[:], x1b[:], ccol("m", t), ccol("r", t), ALU.subtract, ALU.mult),
            waits=[a3, h2T_ev.get(t - 2, (None, 0))], inc=s_v)
        if t % 2 == 0:
            tps = [pM2[:, c * NP:(c + 1) * NP] for c in range(8)]
        else:
            tps = [pT[:, (c % 4) * NP:((c % 4) + 1) * NP] if c < 4
                   else pX[:, (c % 4) * NP:((c % 4) + 1) * NP] for c in range(8)]
        fns = [
            (lambda e, c=c, h2b_=h2b_, dst=tps[c]: e.transpose(
                dst, h2b_[:, c * NP:(c + 1) * NP], idf_sb[:]))
            for c in range(8)
        ]
        wtp = [v6, h2T_ev.get(t - 2, (None, 0))]
        if t % 2 == 1:
            wtp.append(pX_guard[0] or (None, 0))
            wtp.append(rt_w.get(t - 1, (None, 0)))
        tpt = sc.step("tensor", fns, waits=wtp, inc=s_pe)
        hT32 = c_h2T32[t % 2]
        if t % 2 == 0:
            ev1 = sc.step("vector", lambda e, hT32=hT32: e.tensor_copy(
                hT32[:], pM2[:]), waits=[tpt], inc=s_v)
        else:
            ev1a = sc.step("vector", lambda e, hT32=hT32: e.tensor_copy(
                hT32[:, 0:PAN], pT[:]), waits=[tpt], inc=s_v)
            ev1 = sc.step("vector", lambda e, hT32=hT32: e.tensor_copy(
                hT32[:, PAN:2 * PAN], pX[:]), waits=[tpt], inc=s_v)
            pX_guard[0] = ev1
        hTb = c_h2Tb[t % 2]
        ev2 = sc.step("vector", lambda e, hTb=hTb, hT32=hT32: e.tensor_copy(
            hTb[:], hT32[:]), waits=[ev1, h2_dma.get(t - 2, (None, 0))], inc=s_v)
        h2T_ev[t] = ev2
        h2_dma[t] = sc.step("sync", lambda e, hTb=hTb, t=t: e.dma_start(
            out=H2T_s.rearrange("(j p) u -> p j u", p=NP)[:, :, t * NP:(t + 1) * NP],
            in_=hTb[:].rearrange("p (j u) -> p j u", j=8)),
            waits=[ev2], inc=s_h2[t % 2], inc_k=16)
        fns = [
            (lambda e, jj=jj, hT32=hT32: e.matmul(
                pX[:, 0:NE], hT32[:, jj * NP:(jj + 1) * NP],
                rwT_sb[:, jj * NE:(jj + 1) * NE],
                start=(jj == 0), stop=False))
            for jj in range(8)
        ]
        fns.append(lambda e: e.matmul(pX[:, 0:NE], ones_row_f[:], rb_row[:],
                                      start=False, stop=True))
        wrt = [ev1, small_ld]
        if t % 2 == 0:
            wrt.append(pX_guard[0] or (None, 0))
        trt = sc.step("tensor", fns, waits=wrt, inc=s_pe)
        CL = lambda i: c_rt[:, 8 * i: 8 * i + 8]
        r0 = sc.step("vector", lambda e: e.tensor_copy(CL(0), pX[:, 0:NE]),
                     waits=[trt], inc=s_v)
        pX_guard[0] = r0
        r1 = sc.step("vector", lambda e: e.reduce_max(
            c_rt[:, 56:57], CL(0), axis=AX.X), waits=[r0], inc=s_v)
        r2 = sc.step("vector", lambda e: e.tensor_single_scalar(
            CL(1), CL(0), c_rt[:, 56:57], ALU.is_ge), waits=[r1], inc=s_v)
        r3 = sc.step("vector", lambda e: e.scalar_tensor_tensor(
            CL(2), CL(1), -1e30, CL(0), ALU.mult, ALU.add), waits=[r2], inc=s_v)
        r4 = sc.step("vector", lambda e: e.reduce_max(
            c_rt[:, 57:58], CL(2), axis=AX.X), waits=[r3], inc=s_v)
        r5 = sc.step("vector", lambda e: e.tensor_single_scalar(
            CL(3), CL(2), c_rt[:, 57:58], ALU.is_ge), waits=[r4], inc=s_v)
        r6 = sc.step("vector", lambda e: e.tensor_add(CL(4), CL(1), CL(3)),
                     waits=[r5], inc=s_v)
        r7 = sc.step("scalar", lambda e: e.activation(CL(5), CL(0), AF.Sigmoid),
                     waits=[r0], inc=s_a)
        r8 = sc.step("vector", lambda e: e.tensor_tensor(
            CL(6), CL(5), CL(4), ALU.mult), waits=[r6, r7], inc=s_v)
        r9 = sc.step("vector", lambda e: e.reduce_sum(
            c_rt[:, 58:59], CL(6), axis=AX.X), waits=[r8], inc=s_v)
        r10 = sc.step("vector", lambda e: e.reciprocal(
            c_rt[:, 59:60], c_rt[:, 58:59]), waits=[r9], inc=s_v)
        r11 = sc.step("vector", lambda e, t=t: e.tensor_scalar_mul(
            w_res[:, 8 * t: 8 * t + 8], CL(6), c_rt[:, 59:60]),
            waits=[r10], inc=s_v)
        rt_w[t] = r11

    c_end_pe = (s_pe, s_pe.n)
    c_end_v = (s_v, s_v.n)
    c_x1_dma = dict(x1_dma)
    c_h2_dma = dict(h2_dma)

    if upto == "C":
        return _finish(nc, sc, sems, debug and dict(
            QT_dbg=None, qt=None, AT_dbg=AT_dbg, attn=attn,
            MB_dbg=MB_dbg, W_dbg=W_dbg, nacc=nacc, bias_t=bias_t, w_res=w_res),
            upto)

    acc_ld = {}
    h2b_ld = {}
    y_st = {}
    gu_tok = {}
    prod_tok = {}
    dn_tok = {}
    stt_tok = {}
    wgld = {}
    wuld = {}
    wdld = {}
    down_end = {}

    for b in range(NBLK):
        wacc = [c_end_pe, c_end_v, c_x1_dma[8 * b + 7],
                c_x1_dma[8 * b + 6], y_st.get(b - 1, (None, 0))]
        acc_ld[b] = sc.step("sync", lambda e, b=b: e.dma_start(
            out=d_acc[:].rearrange("p (u d) -> p u d", u=8),
            in_=X1_s[b * BT:(b + 1) * BT, :].rearrange("(u p) d -> p u d", p=NP)),
            waits=wacc, inc=s_acc[b % 2], inc_k=16)
        h2b_ld[b] = sc.step("sync", lambda e, b=b: e.dma_start(
            out=d_h2b[:].rearrange("p (j u) -> p j u", j=8),
            in_=H2T_s[:, b * BT:(b + 1) * BT].rearrange("(j p) u -> p j u", p=NP)),
            waits=[c_h2_dma[8 * b + 7], c_h2_dma[8 * b + 6], c_end_pe,
                   gu_tok.get((b - 1, 8, 15, 1), (None, 0))],
            inc=s_h2b[b % 2], inc_k=16)
        for e_i in range(9):
            shared = (e_i == NE)
            def sched_wld(qd, when_tok):
                wgld[(b, e_i, qd)] = sc.step("sync",
                    lambda e, qd=qd, e_i=e_i: e.dma_start(
                        out=d_wg[qd % 2][:], in_=gwb[e_i, qd]),
                    waits=[when_tok, c_end_pe, wcast_done],
                    inc=s_wg[qd % 2], inc_k=16)
                wuld[(b, e_i, qd)] = sc.step("sync",
                    lambda e, qd=qd, e_i=e_i: e.dma_start(
                        out=d_wu[qd % 2][:], in_=uwb[e_i, qd]),
                    waits=[when_tok, c_end_pe, wcast_done],
                    inc=s_wu[qd % 2], inc_k=16)

            def guard_gu(qd):
                key = (b, e_i, 4 * 0 + qd - 2)
                if qd >= 2:
                    return gu_tok.get((b, e_i, (qd - 2) * 4 + 3, 1), (None, 0))
                pe_i, pb = (e_i - 1, b) if e_i > 0 else (8, b - 1)
                if pb < 0:
                    return (None, 0)
                return gu_tok.get((pb, pe_i, (qd + 2) * 4 + 3, 1), (None, 0))

            sched_wld(0, guard_gu(0))
            sched_wld(1, guard_gu(1))
            for qd in range(4):
                for ici in range(4):
                    ic = 4 * qd + ici
                    for pn in range(2):
                        P = (pM0 if (ic * 2 + pn) % 2 == 0 else pM1)[:]
                        fns = []
                        for k in range(8):
                            fns.append(lambda e, k=k, qd=qd, ici=ici, pn=pn, P=P:
                                e.matmul(P[:, 0:PAN],
                                         d_wg[qd % 2][:, k * PAN + ici * NP:
                                                      k * PAN + (ici + 1) * NP],
                                         d_h2b[:, k * KB + pn * PAN:
                                               k * KB + (pn + 1) * PAN],
                                         start=(k == 0), stop=(k == 7)))
                        for k in range(8):
                            fns.append(lambda e, k=k, qd=qd, ici=ici, pn=pn, P=P:
                                e.matmul(P[:, PAN:2 * PAN],
                                         d_wu[qd % 2][:, k * PAN + ici * NP:
                                                      k * PAN + (ici + 1) * NP],
                                         d_h2b[:, k * KB + pn * PAN:
                                               k * KB + (pn + 1) * PAN],
                                         start=(k == 0), stop=(k == 7)))
                        pk = ic * 2 + pn - 2
                        pguard = (prod_tok.get((b, e_i, pk), (None, 0)) if pk >= 0
                                  else prod_tok.get(
                                      (b, e_i - 1, 32 + pk) if e_i > 0
                                      else (b - 1, 8, 32 + pk), (None, 0)))
                        wmm = [wgld[(b, e_i, qd)], wuld[(b, e_i, qd)], h2b_ld[b],
                               c_end_v, pguard,
                               down_end.get((b, e_i - 1) if e_i > 0 else (b - 1, 8),
                                            (None, 0))]
                        tokm = sc.step("tensor", fns, waits=wmm, inc=s_pe)
                        gu_tok[(b, e_i, ic, pn)] = tokm
                        gslot = d_g[(ic * 2 + pn) % 2]
                        tsi = sc.step("scalar", lambda e, gslot=gslot, P=P:
                            e.activation(gslot[:], P[:, 0:PAN], AF.Silu),
                            waits=[tokm], inc=s_a)
                        tpr = sc.step("vector",
                            lambda e, gslot=gslot, P=P, ic=ic, pn=pn:
                            e.tensor_tensor(
                                d_prod[:, ic * KB + pn * PAN: ic * KB + (pn + 1) * PAN],
                                gslot[:], P[:, PAN:2 * PAN], ALU.mult),
                            waits=[tsi,
                                   down_end.get((b, e_i - 1) if e_i > 0 else (b - 1, 8),
                                                (None, 0))],
                            inc=s_v)
                        prod_tok[(b, e_i, ic * 2 + pn)] = tpr
                if qd + 2 <= 3:
                    sched_wld(qd + 2, gu_tok[(b, e_i, 4 * qd + 3, 1)])
            wdld[(b, e_i)] = sc.step("scalar", lambda e, e_i=e_i: e.dma_start(
                out=d_wd[:], in_=dwb[e_i]),
                waits=[dn_tok.get((b, e_i - 1, 7, 1) if e_i > 0 else (b - 1, 8, 7, 1),
                                  (None, 0)), c_end_pe, wcast_done],
                inc=s_wd, inc_k=16)
            prod_all = prod_tok[(b, e_i, 31)]
            for tc in range(8):
                for n in range(2):
                    Pd = (pM2[:, n * PAN:(n + 1) * PAN] if tc % 2 == 0
                          else (pT[:] if n == 0 else pX[:]))
                    fns = [
                        (lambda e, ic=ic, tc=tc, n=n, Pd=Pd: e.matmul(
                            Pd, d_prod[:, ic * KB + tc * NP:ic * KB + (tc + 1) * NP],
                            d_wd[:, ic * KB + n * PAN: ic * KB + (n + 1) * PAN],
                            start=(ic == 0), stop=(ic == 15)))
                        for ic in range(16)
                    ]
                    wdn = [prod_all, wdld[(b, e_i)],
                           stt_tok.get((b, e_i, tc - 2, n), (None, 0))]
                    if tc % 2 == 1 and n == 1:
                        wdn.append(pX_guard[0] or (None, 0))
                    tokd = sc.step("tensor", fns, waits=wdn, inc=s_pe)
                    dn_tok[(b, e_i, tc, n)] = tokd
                    if shared:
                        tstt = sc.step("vector",
                            lambda e, tc=tc, n=n, Pd=Pd: e.scalar_tensor_tensor(
                                d_acc[:, tc * KB + n * PAN: tc * KB + (n + 1) * PAN],
                                Pd, 1.0,
                                d_acc[:, tc * KB + n * PAN: tc * KB + (n + 1) * PAN],
                                ALU.mult, ALU.add),
                            waits=[tokd, acc_ld[b], rt_w[8 * b + tc]], inc=s_v)
                    else:
                        tstt = sc.step("vector",
                            lambda e, tc=tc, n=n, Pd=Pd, b=b, e_i=e_i:
                            e.scalar_tensor_tensor(
                                d_acc[:, tc * KB + n * PAN: tc * KB + (n + 1) * PAN],
                                Pd, w_res[:, 8 * (8 * b + tc) + e_i:
                                          8 * (8 * b + tc) + e_i + 1],
                                d_acc[:, tc * KB + n * PAN: tc * KB + (n + 1) * PAN],
                                ALU.mult, ALU.add),
                            waits=[tokd, acc_ld[b], rt_w[8 * b + tc]], inc=s_v)
                    stt_tok[(b, e_i, tc, n)] = tstt
                    if tc % 2 == 1 and n == 1:
                        pX_guard[0] = tstt
            down_end[(b, e_i)] = dn_tok[(b, e_i, 7, 1)]
        y_st[b] = sc.step("sync", lambda e, b=b: e.dma_start(
            out=y_d[b * BT:(b + 1) * BT, :].rearrange("(u p) d -> p u d", p=NP),
            in_=d_acc[:].rearrange("p (u d) -> p u d", u=8)),
            waits=[stt_tok[(b, 8, 7, 1)]] +
                  [stt_tok[(b, ei, tc, n)] for ei in range(9)
                   for tc in range(8) for n in range(2)],
            inc=s_y, inc_k=16)

    return _finish(nc, sc, sems, debug and dict(
        QT_dbg=None, qt=None, AT_dbg=None, attn=None,
        MB_dbg=None, W_dbg=W_dbg, nacc=nacc, bias_t=bias_t, w_res=w_res),
        upto)


def _finish(nc, sc, sems, dbg, upto):
    s_y = sems["s_y"]
    s_v = sems["s_v"]
    s_a = sems["s_a"]
    s_pe = sems["s_pe"]
    if dbg:
        if dbg.get("qt") is not None:
            for m in range(NPAIR):
                sc.step("sync", lambda e, m=m: e.dma_start(
                    out=dbg["QT_dbg"][m], in_=dbg["qt"][m][:]),
                    waits=[(s_v, s_v.n), (s_pe, s_pe.n), (s_a, s_a.n)],
                    inc=s_y, inc_k=16)
        if dbg.get("attn") is not None:
            for m in range(NPAIR):
                sc.step("sync", lambda e, m=m: e.dma_start(
                    out=dbg["AT_dbg"][m], in_=dbg["attn"][m][:]),
                    waits=[(s_v, s_v.n), (s_pe, s_pe.n), (s_a, s_a.n)],
                    inc=s_y, inc_k=16)
        if dbg.get("MB_dbg") is not None:
            sc.step("sync", lambda e: e.dma_start(
                out=dbg["MB_dbg"], in_=dbg["nacc"][0:1, 0:8]),
                waits=[(s_v, s_v.n), (s_a, s_a.n)], inc=s_y, inc_k=16)
        if dbg.get("W_dbg") is not None:
            sc.step("sync", lambda e: e.dma_start(
                out=dbg["W_dbg"], in_=dbg["w_res"][:]),
                waits=[(s_v, s_v.n)], inc=s_y, inc_k=16)
    fy = s_y.n
    final_waits = [(sm, sm.n) for sm in sems["all_dma"] if sm.n > 0 and sm is not s_y]
    final_waits += [(s_v, s_v.n), (s_a, s_a.n), (s_pe, s_pe.n)]
    sc.step("sync", lambda e: e.wait_ge(s_y.h, fy), waits=final_waits)

    with nc.Block() as block:
        block.sync(lambda e: sc.replay("sync", e))
        block.tensor(lambda e: sc.replay("tensor", e))
        block.vector(lambda e: sc.replay("vector", e))
        block.scalar(lambda e: sc.replay("scalar", e))
        block.gpsimd(lambda e: sc.replay("gpsimd", e))
    return nc


def make_in_maps(inputs, n_cores=8):
    x = np.asarray(inputs["L0_attn_hidden_states"], dtype=np.float32)
    f32 = lambda a: np.ascontiguousarray(np.asarray(a, dtype=np.float32))

    def lay_proj(w):
        w = np.asarray(w, np.float32)
        return np.ascontiguousarray(
            w.reshape(8, NP, w.shape[1]).transpose(1, 0, 2).reshape(NP, -1))

    def lay_gu(w):
        w = np.asarray(w, np.float32).reshape(8, NP, I2).transpose(1, 0, 2)
        return np.ascontiguousarray(
            np.stack([w[:, :, qd * PAN:(qd + 1) * PAN].reshape(NP, -1)
                      for qd in range(4)]))

    def lay_dw(w):
        w = np.asarray(w, np.float32)
        return np.ascontiguousarray(
            w.reshape(16, NP, D).transpose(1, 0, 2).reshape(NP, -1))

    wq = lay_proj(inputs["L0_attn_Wq"])
    wkv = lay_proj(inputs["L0_attn_Wkv"])
    wo = lay_proj(inputs["L0_attn_Wo"])
    bo = f32(inputs["L0_attn_bo"]).reshape(1, D)
    gw_all = np.stack([lay_gu(np.asarray(inputs["L0_ffn_gate_w"], np.float32)[e])
                       for e in range(NE)] +
                      [lay_gu(np.asarray(inputs["L0_ffn_shared_gate"], np.float32).T)])
    uw_all = np.stack([lay_gu(np.asarray(inputs["L0_ffn_up_w"], np.float32)[e])
                       for e in range(NE)] +
                      [lay_gu(np.asarray(inputs["L0_ffn_shared_up"], np.float32).T)])
    dw_all = np.stack([lay_dw(np.asarray(inputs["L0_ffn_down_w"], np.float32)[e])
                       for e in range(NE)] +
                      [lay_dw(np.asarray(inputs["L0_ffn_shared_down"], np.float32).T)])
    rwT = f32(np.asarray(inputs["L0_ffn_router_w"], dtype=np.float32).T)
    rb = f32(inputs["L0_ffn_routing_bias"]).reshape(1, NE)
    idb = np.eye(NP, dtype=ml_dtypes.bfloat16)
    idf = np.eye(NP, dtype=np.float32)
    shared = dict(wq=wq, wkv=wkv, wo=wo, bo=bo,
                  gw=gw_all, uw=uw_all, dw=dw_all, rwT=rwT, rb=rb,
                  idb=idb, idf=idf)
    in_maps = []
    for c in range(n_cores):
        bb, hh = c // 2, c % 2
        m = dict(shared)
        m["x_own"] = np.ascontiguousarray(x[bb, hh * TOK:(hh + 1) * TOK, :])
        m["x_oth"] = np.ascontiguousarray(x[bb, (1 - hh) * TOK:(2 - hh) * TOK, :])
        in_maps.append(m)
    return in_maps


def kernel(**inputs):
    nc = build_program(upto="D", debug=False)
    in_maps = make_in_maps(inputs)
    res = run_bass_kernel_spmd(nc, in_maps, core_ids=list(range(8)))
    out = np.empty((4, SEQ, D), dtype=np.float32)
    for c in range(8):
        bb, hh = c // 2, c % 2
        out[bb, hh * TOK:(hh + 1) * TOK, :] = np.asarray(res.results[c]["y"])
    return out


# version 05
# speedup vs baseline: 1.8029x; 1.1093x over previous
import math
import contextlib
import numpy as np
import ml_dtypes
import concourse.bass as bass
import concourse.mybir as mybir
from concourse.bass_utils import run_bass_kernel_spmd

F32 = mybir.dt.float32
BF16 = mybir.dt.bfloat16
AF = mybir.ActivationFunctionType
ALU = mybir.AluOpType
AX = mybir.AxisListType

NP = 128
D = 1024
TOK = 4096
SEQ = 8192
NH = 16
DH = 64
NPAIR = 8
I2 = 2048
NE = 8
PAN = 512
NBLK = 4
BT = 1024
EPS = 1e-5
KB = 1024


class Sem:
    def __init__(self, handle):
        self.h = handle
        self.n = 0


class Sched:
    ENGS = ("sync", "tensor", "vector", "scalar", "gpsimd")

    def __init__(self):
        self.steps = {e: [] for e in self.ENGS}
        self.last_wait = {e: {} for e in self.ENGS}

    def step(self, eng, fns, waits=(), inc=None, inc_k=1):
        if not isinstance(fns, (list, tuple)):
            fns = [fns]
        lw = self.last_wait[eng]
        eff = []
        for (s, v) in waits:
            if s is None or v <= 0:
                continue
            if lw.get(id(s), -1) >= v:
                continue
            lw[id(s)] = v
            eff.append((s, v))
        tok = None
        if inc is not None:
            inc.n += inc_k
            tok = (inc, inc.n)
        self.steps[eng].append((eff, list(fns), inc, inc_k))
        return tok

    def replay(self, eng_name, eng):
        for waits, fns, inc, inc_k in self.steps[eng_name]:
            for (s, v) in waits:
                eng.wait_ge(s.h, v)
            instr = None
            for f in fns:
                instr = f(eng)
            if inc is not None:
                instr.then_inc(inc.h, inc_k)


def build_program(upto="D", debug=False):
    nc = bass.Bass("TRN2", target_bir_lowering=False)

    dt = nc.dram_tensor
    x_own = dt("x_own", [TOK, D], F32, kind="ExternalInput").ap()
    x_oth = dt("x_oth", [TOK, D], F32, kind="ExternalInput").ap()
    wq_d = dt("wq", [NP, 8 * KB], F32, kind="ExternalInput").ap()
    wkv_d = dt("wkv", [NP, 16 * KB], F32, kind="ExternalInput").ap()
    wo_d = dt("wo", [NP, 8 * KB], F32, kind="ExternalInput").ap()
    bo_d = dt("bo", [1, D], F32, kind="ExternalInput").ap()
    gw_d = dt("gw", [9, 4, NP, 4 * KB], F32, kind="ExternalInput").ap()
    uw_d = dt("uw", [9, 4, NP, 4 * KB], F32, kind="ExternalInput").ap()
    dw_d = dt("dw", [9, NP, 16 * KB], F32, kind="ExternalInput").ap()
    rwT_d = dt("rwT", [D, NE], F32, kind="ExternalInput").ap()
    rb_d = dt("rb", [1, NE], F32, kind="ExternalInput").ap()
    idb_d = dt("idb", [NP, NP], BF16, kind="ExternalInput").ap()
    idf_d = dt("idf", [NP, NP], F32, kind="ExternalInput").ap()

    y_d = dt("y", [TOK, D], F32, kind="ExternalOutput").ap()

    def scratch(name, shape, dtype):
        kind = "ExternalOutput" if debug else "Internal"
        return dt(name, shape, dtype, kind=kind).ap()

    gwb = dt("gwb", [9, 4, NP, 4 * KB], BF16, kind="Internal").ap()
    uwb = dt("uwb", [9, 4, NP, 4 * KB], BF16, kind="Internal").ap()
    dwb = dt("dwb", [9, NP, 16 * KB], BF16, kind="Internal").ap()
    KT_s = scratch("KT_s", [D, SEQ], BF16)
    V_s = scratch("V_s", [NH, SEQ, DH], BF16)
    X1_s = scratch("X1_s", [TOK, D], F32)
    H2T_s = scratch("H2T_s", [D, TOK], BF16)
    if debug:
        QT_dbg = dt("QT_dbg", [NPAIR, NP, TOK], BF16, kind="ExternalOutput").ap()
        AT_dbg = dt("AT_dbg", [NPAIR, NP, TOK], BF16, kind="ExternalOutput").ap()
        MB_dbg = dt("MB_dbg", [1, 8], F32, kind="ExternalOutput").ap()
        W_dbg = dt("W_dbg", [NP, 256], F32, kind="ExternalOutput").ap()

    al = nc.alloc_sbuf_tensor
    ones_col_b = al("ones_col_b", [NP, 1], BF16)
    ones_row_f = al("ones_row_f", [1, NP], F32)
    bias_t = al("bias_t", [NP, 1], F32)
    nacc = al("nacc", [1, 16], F32)
    w_res = al("w_res", [NP, 256], F32)
    bo_row = al("bo_row", [1, D], F32)
    rb_row = al("rb_row", [1, NE], F32)
    rwT_sb = al("rwT_sb", [NP, 8 * NE], F32)
    idb_sb = al("idb_sb", [NP, NP], BF16)
    idf_sb = al("idf_sb", [NP, NP], F32)

    AR_BYTES = 189 * KB
    arena = al("arena", [NP, AR_BYTES], mybir.dt.uint8)
    ar0 = nc.lookup_mloc(arena).addr
    _uid = [0]

    def at(off_kb, shape, dtype, n=None):
        def one(i):
            _uid[0] += 1
            off = ar0 + int(off_kb * KB) + i * shape[1] * mybir.dt.size(dtype)
            off = (off + 31) // 32 * 32
            return nc.alloc_sbuf_tensor_at(f"ar_{_uid[0]}", shape, dtype, offset=off)
        if n is None:
            return one(0)
        return [one(i) for i in range(n)]

    qt = at(0, [NP, TOK], BF16, n=NPAIR)
    a_x = at(64, [NP, D], F32, n=3)
    a_hn = at(76, [NP, D], BF16, n=2)
    a_hnT = at(80, [NP, 4096], BF16, n=2)
    a_kst = at(96, [NP, PAN], BF16, n=2)
    a_vst = at(98, [NP, D], BF16, n=2)
    a_sq = at(102, [NP, PAN], BF16, n=2)
    a_dump = at(104, [NP, D], F32)
    a_rows = at(108, [NP, 64], F32)
    wq_sb = at(108.25, [NP, 8 * KB], BF16)
    wkv_sb = at(124.25, [NP, 16 * KB], BF16)
    attn = at(64, [NP, TOK], BF16, n=NPAIR)
    b_k = at(128, [NP, SEQ], BF16, n=2)
    b_v = at(160, [NP, 64 * 66], BF16, n=2)
    b_pt = at(176.8, [NP, KB], BF16, n=4)
    b_bc = at(184.9, [64, PAN], F32)
    b_rs = at(186.9, [1, PAN], F32)
    wo_sb = at(128, [NP, 8 * KB], BF16)
    c_x = at(144, [NP, D], F32, n=2)
    c_x1 = at(152, [NP, D], F32, n=2)
    c_h2f = at(160, [NP, D], F32, n=2)
    c_h2T32 = at(168, [NP, D], F32, n=2)
    c_h2Tb = at(176, [NP, D], BF16, n=2)
    c_rows = at(180, [NP, 64], F32)
    c_rt = at(180.25, [NP, 64], F32)
    c_dump = at(180.5, [NP, D], F32)
    d_acc = at(0, [NP, 8 * KB], F32)
    d_h2b = at(32, [NP, 8 * KB], BF16)
    d_prod = at(48, [NP, 16 * KB], BF16, n=2)
    d_wg = at(112, [NP, 4 * KB], BF16, n=2)
    d_wu = at(128, [NP, 4 * KB], BF16, n=2)
    d_wd = at(144, [NP, 16 * KB], BF16)
    d_g = at(176, [NP, PAN], F32, n=2)

    pSA = nc.alloc_psum_tensor("pSA", [NP, 2048], F32)
    pSB = nc.alloc_psum_tensor("pSB", [NP, 1024], F32)
    pO = nc.alloc_psum_tensor("pO", [NP, 1024], F32)
    PTB0 = pSB[:, 0:512].bitcast(BF16)
    PTB1 = pSB[:, 512:1024].bitcast(BF16)

    mk = lambda nm: Sem(nc.alloc_semaphore(name=nm))
    s_pre = mk("s_pre")
    s_xb = [mk(f"s_xb{i}") for i in range(3)]
    s_kst = [mk(f"s_kst{i}") for i in range(2)]
    s_vst = [mk(f"s_vst{i}") for i in range(2)]
    s_kld = [mk(f"s_kld{i}") for i in range(2)]
    s_vld = [mk(f"s_vld{i}") for i in range(2)]
    s_wq, s_wkv, s_wo, s_wd = mk("s_wq"), mk("s_wkv"), mk("s_wo"), mk("s_wd")
    s_wcast = mk("s_wcast")
    s_cx = [mk(f"s_cx{i}") for i in range(2)]
    s_x1 = [mk(f"s_x1{i}") for i in range(2)]
    s_h2 = [mk(f"s_h2{i}") for i in range(2)]
    s_wg = [mk(f"s_wg{i}") for i in range(2)]
    s_wu = [mk(f"s_wu{i}") for i in range(2)]
    s_acc = [mk(f"s_acc{i}") for i in range(2)]
    s_h2b = [mk(f"s_h2b{i}") for i in range(2)]
    s_y, s_pe, s_v, s_a = mk("s_y"), mk("s_pe"), mk("s_v"), mk("s_a")
    all_dma_sems = ([s_pre] + s_xb + s_kst + s_vst + s_kld + s_vld +
                    [s_wq, s_wkv, s_wo, s_wd, s_wcast] + s_cx + s_x1 + s_h2 +
                    s_wg + s_wu + s_acc + s_h2b + [s_y])
    sems = dict(all_dma=all_dma_sems, s_y=s_y, s_pe=s_pe, s_v=s_v, s_a=s_a)

    sc = Sched()

    sc.step("vector", lambda e: e.memset(ones_col_b[:], 1.0), inc=s_v)
    sc.step("vector", lambda e: e.memset(ones_row_f[:], 1.0), inc=s_v)
    sc.step("vector", lambda e: e.memset(nacc[:], 0.0), inc=s_v)
    ones_ready = (s_v, s_v.n)
    sc.step("sync", lambda e: e.dma_start(out=bo_row[:], in_=bo_d), inc=s_pre, inc_k=16)
    sc.step("sync", lambda e: e.dma_start(out=rb_row[:], in_=rb_d), inc=s_pre, inc_k=16)
    sc.step("sync", lambda e: e.dma_start(
        out=rwT_sb[:].rearrange("p (j e) -> p j e", j=8),
        in_=rwT_d.rearrange("(j p) e -> p j e", p=NP)), inc=s_pre, inc_k=16)
    sc.step("sync", lambda e: e.dma_start(out=idb_sb[:], in_=idb_d), inc=s_pre, inc_k=16)
    sc.step("sync", lambda e: e.dma_start(out=idf_sb[:], in_=idf_d), inc=s_pre, inc_k=16)
    small_ld = (s_pre, s_pre.n)
    sc.step("gpsimd", lambda e: e.dma_start(out=wq_sb[:], in_=wq_d),
            inc=s_wq, inc_k=16)
    wq_ld = (s_wq, s_wq.n)
    sc.step("gpsimd", lambda e: e.dma_start(out=wkv_sb[:], in_=wkv_d),
            inc=s_wkv, inc_k=16)
    wkv_ld = (s_wkv, s_wkv.n)
    if upto == "D":
        for e_i in range(9):
            for qd in range(4):
                sc.step("gpsimd", lambda e, e_i=e_i, qd=qd: e.dma_start(
                    out=gwb[e_i, qd], in_=gw_d[e_i, qd]), inc=s_wcast, inc_k=16)
                sc.step("gpsimd", lambda e, e_i=e_i, qd=qd: e.dma_start(
                    out=uwb[e_i, qd], in_=uw_d[e_i, qd]), inc=s_wcast, inc_k=16)
            sc.step("gpsimd", lambda e, e_i=e_i: e.dma_start(
                out=dwb[e_i], in_=dw_d[e_i]), inc=s_wcast, inc_k=16)
    wcast_done = (s_wcast, s_wcast.n)

    RC = {"sum": 0, "m": 2, "msq": 4, "vep": 6, "var": 8, "ln": 10, "r": 12}

    def rcol(name, t):
        return a_rows[:, RC[name] + (t % 2): RC[name] + (t % 2) + 1]

    norm_tok = [None]
    tile_norm_done = {}
    tile_tp_done = {}
    tile_ev_done = {}
    panel_mm_done = {}
    kst_dma = {}
    kst_sq = {}
    vst_dma = {}
    ps_ring = [pSA[:, 0:512], pSA[:, 512:1024], pSA[:, 1024:1536],
               pSA[:, 1536:2048]]
    ps_tok = [None, None, None, None]
    ps_i = [0]

    def next_ps():
        i = ps_i[0] % 4
        ps_i[0] += 1
        return i

    a_mm_end = [None]
    ASQ_guard = [None, None]
    pX_guard = [None]

    for t in range(64):
        xb = a_x[t % 3]
        src = (x_own if t < 32 else x_oth)
        tsl = (t % 32)
        tok_x = sc.step("sync", lambda e, s=src, q=tsl, xb=xb: e.dma_start(
            out=xb[:], in_=s[q * NP:(q + 1) * NP, :]),
            waits=[tile_norm_done.get(t - 3, (None, 0))],
            inc=s_xb[t % 3], inc_k=16)
        v1 = sc.step("vector", lambda e, xb=xb, t=t: e.reduce_sum(
            rcol("sum", t), xb[:], axis=AX.X), waits=[tok_x], inc=s_v)
        a1 = sc.step("scalar", lambda e, xb=xb, t=t: e.activation(
            a_dump[:], xb[:], AF.Square, accum_out=rcol("msq", t)),
            waits=[tok_x], inc=s_a)
        v2 = sc.step("vector", lambda e, t=t: e.tensor_scalar_mul(
            rcol("m", t), rcol("sum", t), 1.0 / D), waits=[v1], inc=s_v)
        v4 = sc.step("vector", lambda e, t=t: e.tensor_scalar(
            rcol("vep", t), rcol("msq", t), 1.0 / D, EPS, ALU.mult, ALU.add),
            waits=[a1], inc=s_v)
        v3 = sc.step("vector", lambda e, t=t: e.tensor_tensor(
            rcol("var", t), rcol("m", t), rcol("m", t), ALU.mult),
            waits=[v2], inc=s_v)
        v5 = sc.step("vector", lambda e, t=t: e.tensor_sub(
            rcol("vep", t), rcol("vep", t), rcol("var", t)),
            waits=[v3, v4], inc=s_v)
        a2 = sc.step("scalar", lambda e, t=t: e.activation(
            rcol("ln", t), rcol("vep", t), AF.Ln), waits=[v5], inc=s_a)
        a3 = sc.step("scalar", lambda e, t=t: e.activation(
            rcol("r", t), rcol("ln", t), AF.Exp, scale=-0.5),
            waits=[a2], inc=s_a)
        hb = a_hn[t % 2]
        v6 = sc.step("vector", lambda e, xb=xb, hb=hb, t=t: e.tensor_scalar(
            hb[:], xb[:], rcol("m", t), rcol("r", t), ALU.subtract, ALU.mult),
            waits=[a3, tile_tp_done.get(t - 2, (None, 0))], inc=s_v)
        tile_norm_done[t] = v6
        PTB = (PTB0 if t % 2 == 0 else PTB1)
        fns = [
            (lambda e, hb=hb, c=c, PTB=PTB: e.transpose(
                PTB[:, c * NP:(c + 1) * NP], hb[:, c * NP:(c + 1) * NP],
                idb_sb[:]))
            for c in range(8)
        ]
        jprev = panel_mm_done.get(t // 4 - 2, (None, 0))
        p1 = sc.step("tensor", fns, waits=[v6, tile_ev_done.get(t - 2, (None, 0)),
                                           small_ld], inc=s_pe)
        hT = a_hnT[(t // 4) % 2]
        ev = sc.step("vector", lambda e, hT=hT, t=t, PTB=PTB: e.tensor_copy(
            hT[:].rearrange("p (c u) -> p c u", c=8)[:, :, (t % 4) * NP:(t % 4 + 1) * NP],
            PTB[:].rearrange("p (c u) -> p c u", c=8)),
            waits=[p1, jprev], inc=s_v)
        tile_ev_done[t] = ev
        tile_tp_done[t] = p1

        if t % 4 != 3:
            continue
        j = t // 4
        own = j < 8
        hT = a_hnT[j % 2]
        pan_ready = ev
        last_mm = None

        def mm_group(lhs_fn, rhs_fn, extra_waits, n_k=8):
            nonlocal last_mm
            si = next_ps()
            ps = ps_ring[si]
            fns = [
                (lambda e, k=k, si=si: e.matmul(
                    ps_ring[si], lhs_fn(k), rhs_fn(k),
                    start=(k == 0), stop=(k == n_k - 1)))
                for k in range(n_k)
            ]
            w = [pan_ready] + extra_waits
            if ps_tok[si] is not None:
                w.append(ps_tok[si])
            tokm = sc.step("tensor", fns, waits=w, inc=s_pe)
            last_mm = tokm
            return si, tokm

        if own:
            jq = j
            for m in range(8):
                si, tokm = mm_group(
                    lambda k, m=m: wq_sb[:, k * KB + m * NP: k * KB + (m + 1) * NP],
                    lambda k, hT=hT: hT[:, k * PAN:(k + 1) * PAN], [wq_ld])
                tev = sc.step("vector", lambda e, m=m, jq=jq, si=si: e.tensor_copy(
                    qt[m][:, jq * PAN:(jq + 1) * PAN], ps_ring[si]),
                    waits=[tokm], inc=s_v)
                ps_tok[si] = tev
                sqb = a_sq[m % 2]
                tsq = sc.step("scalar", lambda e, m=m, jq=jq, sqb=sqb: e.activation(
                    sqb[:], qt[m][:, jq * PAN:(jq + 1) * PAN], AF.Square),
                    waits=[tev, ASQ_guard[m % 2] or (None, 0)], inc=s_a)
                tnm = sc.step("tensor", lambda e, sqb=sqb: e.matmul(
                    pO[0:1, 0:PAN], ones_col_b[:], sqb[:], start=True, stop=True),
                    waits=[tsq, ones_ready, pX_guard[0] or (None, 0)], inc=s_pe)
                ASQ_guard[m % 2] = tnm
                trm = sc.step("vector", lambda e: e.reduce_max(
                    nacc[0:1, 8:9], pO[0:1, 0:PAN], axis=AX.X),
                    waits=[tnm], inc=s_v)
                tmx = sc.step("vector", lambda e: e.tensor_max(
                    nacc[0:1, 0:1], nacc[0:1, 0:1], nacc[0:1, 8:9]),
                    waits=[trm], inc=s_v)
                pX_guard[0] = trm
                norm_tok[0] = tmx
        for c in range(8):
            si, tokm = mm_group(
                lambda k, c=c: wkv_sb[:, k * 2048 + c * NP: k * 2048 + (c + 1) * NP],
                lambda k, hT=hT: hT[:, k * PAN:(k + 1) * PAN], [wkv_ld])
            kb = a_kst[c % 2]
            tev = sc.step("vector", lambda e, kb=kb, si=si: e.tensor_copy(
                kb[:], ps_ring[si]),
                waits=[tokm, kst_dma.get((j, c - 2), kst_dma.get((j - 1, c + 6), (None, 0))),
                       kst_sq.get((j, c - 2), kst_sq.get((j - 1, c + 6), (None, 0)))],
                inc=s_v)
            ps_tok[si] = tev
            td = sc.step("sync", lambda e, kb=kb, c=c, j=j: e.dma_start(
                out=KT_s[c * NP:(c + 1) * NP, j * PAN:(j + 1) * PAN], in_=kb[:]),
                waits=[tev], inc=s_kst[c % 2], inc_k=16)
            kst_dma[(j, c)] = td
            sqb = a_sq[c % 2]
            tsq = sc.step("scalar", lambda e, kb=kb, sqb=sqb: e.activation(
                sqb[:], kb[:], AF.Square),
                waits=[tev, ASQ_guard[c % 2] or (None, 0)], inc=s_a)
            tnm = sc.step("tensor", lambda e, sqb=sqb: e.matmul(
                pO[0:1, 0:PAN], ones_col_b[:], sqb[:], start=True, stop=True),
                waits=[tsq, pX_guard[0] or (None, 0)], inc=s_pe)
            ASQ_guard[c % 2] = tnm
            last_mm = tnm
            trm = sc.step("vector", lambda e: e.reduce_max(
                nacc[0:1, 9:10], pO[0:1, 0:PAN], axis=AX.X),
                waits=[tnm], inc=s_v)
            tmx = sc.step("vector", lambda e: e.tensor_max(
                nacc[0:1, 1:2], nacc[0:1, 1:2], nacc[0:1, 9:10]),
                waits=[trm], inc=s_v)
            pX_guard[0] = trm
            kst_sq[(j, c)] = tsq
            norm_tok[0] = tmx
        for u in range(4):
            vb = a_vst[u % 2]
            toks = []
            for n in range(2):
                si, tokm = mm_group(
                    lambda k, u=u, hT=hT: hT[:, k * PAN + u * NP: k * PAN + (u + 1) * NP],
                    lambda k, n=n: wkv_sb[:, k * 2048 + KB + n * PAN: k * 2048 + KB + (n + 1) * PAN],
                    [wkv_ld])
                tev = sc.step("vector", lambda e, vb=vb, n=n, si=si: e.tensor_copy(
                    vb[:, n * PAN:(n + 1) * PAN], ps_ring[si]),
                    waits=[tokm, vst_dma.get((j, u - 2), vst_dma.get((j - 1, u + 2), (None, 0)))],
                    inc=s_v)
                ps_tok[si] = tev
                toks.append(tev)
            tt = j * 4 + u
            td = sc.step("sync", lambda e, vb=vb, tt=tt: e.dma_start(
                out=V_s[:, tt * NP:(tt + 1) * NP, :].rearrange("h p n -> p h n"),
                in_=vb[:].rearrange("p (h n) -> p h n", h=NH)),
                waits=toks, inc=s_vst[u % 2], inc_k=16)
            vst_dma[(j, u)] = td
        panel_mm_done[j] = last_mm

    a_scratch_done = [(s_kst[0], s_kst[0].n), (s_kst[1], s_kst[1].n),
                      (s_vst[0], s_vst[0].n), (s_vst[1], s_vst[1].n)]
    a_mm_end = panel_mm_done[15]

    bv1 = sc.step("vector", lambda e: e.tensor_tensor(
        nacc[0:1, 2:3], nacc[0:1, 0:1], nacc[0:1, 1:2], ALU.mult),
        waits=[norm_tok[0]], inc=s_v)
    ba1 = sc.step("scalar", lambda e: e.activation(
        nacc[0:1, 3:4], nacc[0:1, 2:3], AF.Ln), waits=[bv1], inc=s_a)
    ba2 = sc.step("scalar", lambda e: e.activation(
        nacc[0:1, 4:5], nacc[0:1, 3:4], AF.Exp, scale=0.5), waits=[ba1], inc=s_a)
    bv2 = sc.step("vector", lambda e: e.tensor_scalar_mul(
        nacc[0:1, 5:6], nacc[0:1, 4:5], -0.125), waits=[ba2], inc=s_v)
    bt1 = sc.step("tensor", lambda e: e.matmul(
        pO[:, 0:1], ones_row_f[:], nacc[0:1, 5:6], start=True, stop=True),
        waits=[bv2, pX_guard[0] or (None, 0)], inc=s_pe)
    bt2 = sc.step("vector", lambda e: e.tensor_copy(bias_t[:], pO[:, 0:1]),
                  waits=[bt1], inc=s_v)
    bias_ready = bt2
    pX_guard[0] = bt2

    if upto == "A":
        return _finish(nc, sc, sems, debug and dict(
            QT_dbg=QT_dbg, qt=qt, AT_dbg=None, attn=None,
            MB_dbg=MB_dbg, W_dbg=None, nacc=nacc, bias_t=bias_t, w_res=w_res),
            upto)

    kld = {}
    vld = {}
    for p in range(2):
        kld[p] = sc.step("sync", lambda e, p=p: e.dma_start(
            out=b_k[p][:], in_=KT_s[p * NP:(p + 1) * NP, :]),
            waits=a_scratch_done + [a_mm_end], inc=s_kld[p % 2], inc_k=16)
    for hh in range(2):
        vld[hh] = sc.step("sync", lambda e, hh=hh: e.dma_start(
            out=b_v[hh % 2][:].rearrange("p (c w) -> p c w", w=66)[:, :, 0:64],
            in_=V_s[hh].rearrange("(c p) n -> p c n", p=NP)),
            waits=a_scratch_done + [a_mm_end], inc=s_vld[hh % 2], inc_k=16)
    vmemset = {}
    for sl in range(2):
        vmemset[sl] = sc.step("vector", lambda e, sl=sl: e.memset(
            b_v[sl][:].rearrange("p (c w) -> p c w", w=66)[:, :, 64:65], 1.0),
            waits=[a_mm_end], inc=s_v)

    exp_tok = {}
    pv_tok = {}
    smm_end = {}
    norm_done = {}
    pair_end = {}
    for p in range(NPAIR):
        kbuf = b_k[p % 2]

        def sfns(head, g, p=p, kbuf=kbuf):
            Sp = (pSA[:, (g % 2) * 1024:(g % 2) * 1024 + 1024] if head == 0
                  else pSB[:])
            rbase = 64 * head
            q = cur_q[0]
            return [
                (lambda e, c=c, Sp=Sp, kbuf=kbuf, rbase=rbase, p=p, q=q, cc=cc:
                 e.matmul(Sp[:, cc * PAN:(cc + 1) * PAN],
                          kbuf[rbase:rbase + 64, c * NP:(c + 1) * NP],
                          qt[p][rbase:rbase + 64, q * PAN:(q + 1) * PAN],
                          start=True, stop=True))
                for cc, c in enumerate((2 * g, 2 * g + 1))
            ]

        def pvfns(head, g):
            pb = b_pt[2 * head + g % 2]
            vbuf = b_v[head]
            ocol = head * PAN
            return [
                (lambda e, c=c, pb=pb, vbuf=vbuf, g=g, cc=cc, ocol=ocol:
                 e.matmul(pO[0:65, ocol:ocol + PAN],
                          vbuf[:, c * 66:c * 66 + 65],
                          pb[:, cc * PAN:(cc + 1) * PAN],
                          start=(g == 0 and cc == 0), stop=(g == 31 and cc == 1)))
                for cc, c in enumerate((2 * g, 2 * g + 1))
            ]

        cur_q = [0]
        for q in range(8):
            cur_q[0] = q
            w0 = [kld[p], vld[2 * p], vld[2 * p + 1],
                  norm_done.get((2 * p, q - 1),
                                norm_done.get((2 * p - 1, 7), (None, 0))),
                  norm_done.get((2 * p + 1, q - 1), (None, 0))]
            if p == 0 and q == 0:
                w0 += [vmemset[0], vmemset[1]]
            stok = {}
            stok[0] = sc.step("tensor", sfns(1, 0) + sfns(0, 0), waits=w0,
                              inc=s_pe)
            teA_tok = {}
            pvt = {}
            for g in range(32):
                pbA = b_pt[g % 2]
                pbB = b_pt[2 + g % 2]
                half = (g % 2) * 1024
                teB = sc.step("scalar", lambda e, pbB=pbB: e.activation(
                    pbB[:], pSB[:], AF.Exp, bias=bias_t[:, 0:1], scale=0.125),
                    waits=[stok[g], bias_ready], inc=s_a)
                teA = sc.step("scalar", lambda e, pbA=pbA, half=half: e.activation(
                    pbA[:], pSA[:, half:half + 1024], AF.Exp,
                    bias=bias_t[:, 0:1], scale=0.125),
                    waits=[], inc=s_a)
                teA_tok[g] = teA
                if g + 1 < 32:
                    stok[g + 1] = sc.step("tensor", sfns(1, g + 1) + sfns(0, g + 1),
                                          waits=[teB, teA_tok.get(g - 1, (None, 0))],
                                          inc=s_pe)
                    pvt[g] = sc.step("tensor", pvfns(0, g) + pvfns(1, g),
                                     waits=[teA], inc=s_pe)
                else:
                    pvt[g] = sc.step("tensor", pvfns(0, g) + pvfns(1, g),
                                     waits=[teA], inc=s_pe)
            for head in range(2):
                hh = 2 * p + head
                ocol = head * PAN
                rbase = 64 * head
                n1 = sc.step("vector", lambda e, ocol=ocol: e.reciprocal(
                    b_rs[:], pO[64:65, ocol:ocol + PAN]),
                    waits=[pvt[31]], inc=s_v)
                n2 = sc.step("tensor", lambda e: e.matmul(
                    pSA[0:64, 0:PAN], ones_row_f[0:1, 0:64], b_rs[:],
                    start=True, stop=True),
                    waits=[n1, pX_guard[0] or (None, 0)], inc=s_pe)
                n3 = sc.step("vector", lambda e: e.tensor_copy(
                    b_bc[:], pSA[0:64, 0:PAN]), waits=[n2], inc=s_v)
                n4 = sc.step("vector",
                    lambda e, p=p, rbase=rbase, q=q, ocol=ocol: e.tensor_tensor(
                        attn[p][rbase:rbase + 64, q * PAN:(q + 1) * PAN],
                        pO[0:64, ocol:ocol + PAN], b_bc[:], ALU.mult),
                    waits=[n3], inc=s_v)
                norm_done[(hh, q)] = n4
                pX_guard[0] = n3
        pair_end[p] = pvt[31]
        if p + 1 < NPAIR:
            for head in range(2):
                vld[2 * (p + 1) + head] = sc.step("sync",
                    lambda e, p=p, head=head: e.dma_start(
                        out=b_v[head][:].rearrange(
                            "p (c w) -> p c w", w=66)[:, :, 0:64],
                        in_=V_s[2 * (p + 1) + head].rearrange(
                            "(c p) n -> p c n", p=NP)),
                    waits=[pair_end[p]], inc=s_vld[head], inc_k=16)
        if p + 2 < NPAIR:
            kld[p + 2] = sc.step("sync", lambda e, p=p: e.dma_start(
                out=b_k[p % 2][:], in_=KT_s[(p + 2) * NP:(p + 3) * NP, :]),
                waits=[pair_end[p]], inc=s_kld[p % 2], inc_k=16)

    b_end_pe = pair_end[NPAIR - 1]
    b_end_v = norm_done[(NH - 1, 7)]

    if upto == "B":
        return _finish(nc, sc, sems, debug and dict(
            QT_dbg=QT_dbg, qt=qt, AT_dbg=AT_dbg, attn=attn,
            MB_dbg=MB_dbg, W_dbg=None, nacc=nacc, bias_t=bias_t, w_res=w_res),
            upto)

    wo_ld = sc.step("gpsimd", lambda e: e.dma_start(out=wo_sb[:], in_=wo_d),
        waits=[b_end_pe], inc=s_wo, inc_k=16)

    CC = {"sum": 0, "m": 2, "msq": 4, "vep": 6, "var": 8, "ln": 10, "r": 12}

    def ccol(name, t):
        return c_rows[:, CC[name] + (t % 2): CC[name] + (t % 2) + 1]

    x1_dma = {}
    h2_dma = {}
    x1_add = {}
    h2T_ev = {}
    rt_w = {}
    rt_r0 = {}
    proj_tok = {}
    for t in range(32):
        xb = c_x[t % 2]
        tok_x = sc.step("sync", lambda e, t=t, xb=xb: e.dma_start(
            out=xb[:], in_=x_own[t * NP:(t + 1) * NP, :]),
            waits=[b_end_pe, x1_add.get(t - 2, (None, 0))],
            inc=s_cx[t % 2], inc_k=16)
        P = (pSA[:, 0:1024] if t % 2 == 0 else pSA[:, 1024:2048])
        fns = []
        for n in range(2):
            for k in range(8):
                fns.append(lambda e, n=n, k=k, t=t, P=P: e.matmul(
                    P[:, n * PAN:(n + 1) * PAN],
                    attn[k][:, t * NP:(t + 1) * NP],
                    wo_sb[:, k * KB + n * PAN: k * KB + (n + 1) * PAN],
                    start=(k == 0), stop=False))
            fns.append(lambda e, n=n, P=P: e.matmul(
                P[:, n * PAN:(n + 1) * PAN], ones_row_f[:],
                bo_row[0:1, n * PAN:(n + 1) * PAN], start=False, stop=True))
        tp = sc.step("tensor", fns,
                     waits=[wo_ld, small_ld, b_end_v, x1_add.get(t - 2, (None, 0)),
                            rt_r0.get(t - 1, (None, 0))],
                     inc=s_pe)
        proj_tok[t] = tp
        x1b = c_x1[t % 2]
        tadd = sc.step("vector", lambda e, x1b=x1b, xb=xb, P=P: e.tensor_add(
            x1b[:], P, xb[:]),
            waits=[tp, tok_x, x1_dma.get(t - 2, (None, 0))], inc=s_v)
        x1_add[t] = tadd
        x1_dma[t] = sc.step("sync", lambda e, x1b=x1b, t=t: e.dma_start(
            out=X1_s[t * NP:(t + 1) * NP, :], in_=x1b[:]),
            waits=[tadd], inc=s_x1[t % 2], inc_k=16)
        v1 = sc.step("vector", lambda e, x1b=x1b, t=t: e.reduce_sum(
            ccol("sum", t), x1b[:], axis=AX.X), waits=[tadd], inc=s_v)
        a1 = sc.step("scalar", lambda e, x1b=x1b, t=t: e.activation(
            c_dump[:], x1b[:], AF.Square, accum_out=ccol("msq", t)),
            waits=[tadd], inc=s_a)
        v2 = sc.step("vector", lambda e, t=t: e.tensor_scalar_mul(
            ccol("m", t), ccol("sum", t), 1.0 / D), waits=[v1], inc=s_v)
        v4 = sc.step("vector", lambda e, t=t: e.tensor_scalar(
            ccol("vep", t), ccol("msq", t), 1.0 / D, EPS, ALU.mult, ALU.add),
            waits=[a1], inc=s_v)
        v3 = sc.step("vector", lambda e, t=t: e.tensor_tensor(
            ccol("var", t), ccol("m", t), ccol("m", t), ALU.mult),
            waits=[v2], inc=s_v)
        v5 = sc.step("vector", lambda e, t=t: e.tensor_sub(
            ccol("vep", t), ccol("vep", t), ccol("var", t)),
            waits=[v3, v4], inc=s_v)
        a2 = sc.step("scalar", lambda e, t=t: e.activation(
            ccol("ln", t), ccol("vep", t), AF.Ln), waits=[v5], inc=s_a)
        a3 = sc.step("scalar", lambda e, t=t: e.activation(
            ccol("r", t), ccol("ln", t), AF.Exp, scale=-0.5), waits=[a2], inc=s_a)
        h2b_ = c_h2f[t % 2]
        v6 = sc.step("vector", lambda e, h2b_=h2b_, x1b=x1b, t=t: e.tensor_scalar(
            h2b_[:], x1b[:], ccol("m", t), ccol("r", t), ALU.subtract, ALU.mult),
            waits=[a3, h2T_ev.get(t - 2, (None, 0))], inc=s_v)
        if t % 2 == 0:
            tps = [pSB[:, c * NP:(c + 1) * NP] for c in range(8)]
        else:
            tps = [pO[:, c * NP:(c + 1) * NP] for c in range(8)]
        fns = [
            (lambda e, c=c, h2b_=h2b_, dst=tps[c]: e.transpose(
                dst, h2b_[:, c * NP:(c + 1) * NP], idf_sb[:]))
            for c in range(8)
        ]
        wtp = [v6, h2T_ev.get(t - 2, (None, 0))]
        if t % 2 == 1:
            wtp.append(pX_guard[0] or (None, 0))
        tpt = sc.step("tensor", fns, waits=wtp, inc=s_pe)
        hT32 = c_h2T32[t % 2]
        if t % 2 == 0:
            ev1 = sc.step("vector", lambda e, hT32=hT32: e.tensor_copy(
                hT32[:], pSB[:]), waits=[tpt], inc=s_v)
        else:
            ev1 = sc.step("vector", lambda e, hT32=hT32: e.tensor_copy(
                hT32[:], pO[:]), waits=[tpt], inc=s_v)
        hTb = c_h2Tb[t % 2]
        ev2 = sc.step("vector", lambda e, hTb=hTb, hT32=hT32: e.tensor_copy(
            hTb[:], hT32[:]), waits=[ev1, h2_dma.get(t - 2, (None, 0))], inc=s_v)
        h2T_ev[t] = ev2
        h2_dma[t] = sc.step("sync", lambda e, hTb=hTb, t=t: e.dma_start(
            out=H2T_s.rearrange("(j p) u -> p j u", p=NP)[:, :, t * NP:(t + 1) * NP],
            in_=hTb[:].rearrange("p (j u) -> p j u", j=8)),
            waits=[ev2], inc=s_h2[t % 2], inc_k=16)
        rcol0 = 1024 if t % 2 == 0 else 0
        fns = [
            (lambda e, jj=jj, hT32=hT32, rcol0=rcol0: e.matmul(
                pSA[:, rcol0:rcol0 + NE], hT32[:, jj * NP:(jj + 1) * NP],
                rwT_sb[:, jj * NE:(jj + 1) * NE],
                start=(jj == 0), stop=False))
            for jj in range(8)
        ]
        fns.append(lambda e, rcol0=rcol0: e.matmul(
            pSA[:, rcol0:rcol0 + NE], ones_row_f[:], rb_row[:],
            start=False, stop=True))
        wrt = [ev1, small_ld, x1_add.get(t - 1, (None, 0)),
               pX_guard[0] or (None, 0)]
        trt = sc.step("tensor", fns, waits=wrt, inc=s_pe)
        CL = lambda i: c_rt[:, 8 * i: 8 * i + 8]
        r0 = sc.step("vector", lambda e, rcol0=rcol0: e.tensor_copy(
            CL(0), pSA[:, rcol0:rcol0 + NE]), waits=[trt], inc=s_v)
        rt_r0[t] = r0
        pX_guard[0] = r0
        r1 = sc.step("vector", lambda e: e.reduce_max(
            c_rt[:, 56:57], CL(0), axis=AX.X), waits=[r0], inc=s_v)
        r2 = sc.step("vector", lambda e: e.tensor_single_scalar(
            CL(1), CL(0), c_rt[:, 56:57], ALU.is_ge), waits=[r1], inc=s_v)
        r3 = sc.step("vector", lambda e: e.scalar_tensor_tensor(
            CL(2), CL(1), -1e30, CL(0), ALU.mult, ALU.add), waits=[r2], inc=s_v)
        r4 = sc.step("vector", lambda e: e.reduce_max(
            c_rt[:, 57:58], CL(2), axis=AX.X), waits=[r3], inc=s_v)
        r5 = sc.step("vector", lambda e: e.tensor_single_scalar(
            CL(3), CL(2), c_rt[:, 57:58], ALU.is_ge), waits=[r4], inc=s_v)
        r6 = sc.step("vector", lambda e: e.tensor_add(CL(4), CL(1), CL(3)),
                     waits=[r5], inc=s_v)
        r7 = sc.step("scalar", lambda e: e.activation(CL(5), CL(0), AF.Sigmoid),
                     waits=[r0], inc=s_a)
        r8 = sc.step("vector", lambda e: e.tensor_tensor(
            CL(6), CL(5), CL(4), ALU.mult), waits=[r6, r7], inc=s_v)
        r9 = sc.step("vector", lambda e: e.reduce_sum(
            c_rt[:, 58:59], CL(6), axis=AX.X), waits=[r8], inc=s_v)
        r10 = sc.step("vector", lambda e: e.reciprocal(
            c_rt[:, 59:60], c_rt[:, 58:59]), waits=[r9], inc=s_v)
        r11 = sc.step("vector", lambda e, t=t: e.tensor_scalar_mul(
            w_res[:, 8 * t: 8 * t + 8], CL(6), c_rt[:, 59:60]),
            waits=[r10], inc=s_v)
        rt_w[t] = r11

    c_end_pe = (s_pe, s_pe.n)
    c_end_v = (s_v, s_v.n)
    c_x1_dma = dict(x1_dma)
    c_h2_dma = dict(h2_dma)

    if upto == "C":
        return _finish(nc, sc, sems, debug and dict(
            QT_dbg=None, qt=None, AT_dbg=AT_dbg, attn=attn,
            MB_dbg=MB_dbg, W_dbg=W_dbg, nacc=nacc, bias_t=bias_t, w_res=w_res),
            upto)

    acc_ld = {}
    h2b_ld = {}
    y_st = {}
    gu_tok = {}
    prod_tok = {}
    dn_tok = {}
    stt_tok = {}
    wgld = {}
    wuld = {}
    wdld = {}
    down_end = {}

    for b in range(NBLK):
        wacc = [c_end_pe, c_end_v, c_x1_dma[8 * b + 7],
                c_x1_dma[8 * b + 6], y_st.get(b - 1, (None, 0))]
        acc_ld[b] = sc.step("sync", lambda e, b=b: e.dma_start(
            out=d_acc[:].rearrange("p (u d) -> p u d", u=8),
            in_=X1_s[b * BT:(b + 1) * BT, :].rearrange("(u p) d -> p u d", p=NP)),
            waits=wacc, inc=s_acc[b % 2], inc_k=16)
        h2b_ld[b] = sc.step("sync", lambda e, b=b: e.dma_start(
            out=d_h2b[:].rearrange("p (j u) -> p j u", j=8),
            in_=H2T_s[:, b * BT:(b + 1) * BT].rearrange("(j p) u -> p j u", p=NP)),
            waits=[c_h2_dma[8 * b + 7], c_h2_dma[8 * b + 6], c_end_pe,
                   gu_tok.get((b - 1, 8, 15, 1), (None, 0))],
            inc=s_h2b[b % 2], inc_k=16)
        for e_i in range(9):
            shared = (e_i == NE)
            sidx = b * 9 + e_i
            dprod = d_prod[sidx % 2]
            pidx = sidx - 2
            prodguard = (down_end.get((pidx // 9, pidx % 9), (None, 0))
                         if pidx >= 0 else (None, 0))
            def sched_wld(qd, when_tok):
                wgld[(b, e_i, qd)] = sc.step("sync",
                    lambda e, qd=qd, e_i=e_i: e.dma_start(
                        out=d_wg[qd % 2][:], in_=gwb[e_i, qd]),
                    waits=[when_tok, c_end_pe, wcast_done],
                    inc=s_wg[qd % 2], inc_k=16)
                wuld[(b, e_i, qd)] = sc.step("sync",
                    lambda e, qd=qd, e_i=e_i: e.dma_start(
                        out=d_wu[qd % 2][:], in_=uwb[e_i, qd]),
                    waits=[when_tok, c_end_pe, wcast_done],
                    inc=s_wu[qd % 2], inc_k=16)

            def guard_gu(qd):
                key = (b, e_i, 4 * 0 + qd - 2)
                if qd >= 2:
                    return gu_tok.get((b, e_i, (qd - 2) * 4 + 3, 1), (None, 0))
                pe_i, pb = (e_i - 1, b) if e_i > 0 else (8, b - 1)
                if pb < 0:
                    return (None, 0)
                return gu_tok.get((pb, pe_i, (qd + 2) * 4 + 3, 1), (None, 0))

            sched_wld(0, guard_gu(0))
            sched_wld(1, guard_gu(1))
            for qd in range(4):
                for ici in range(4):
                    ic = 4 * qd + ici
                    for pn in range(2):
                        ph = ((ic * 2 + pn) % 2) * 1024
                        P = pSA[:, ph:ph + 1024]
                        fns = []
                        for k in range(8):
                            fns.append(lambda e, k=k, qd=qd, ici=ici, pn=pn, P=P:
                                e.matmul(P[:, 0:PAN],
                                         d_wg[qd % 2][:, k * PAN + ici * NP:
                                                      k * PAN + (ici + 1) * NP],
                                         d_h2b[:, k * KB + pn * PAN:
                                               k * KB + (pn + 1) * PAN],
                                         start=(k == 0), stop=(k == 7)))
                        for k in range(8):
                            fns.append(lambda e, k=k, qd=qd, ici=ici, pn=pn, P=P:
                                e.matmul(P[:, PAN:2 * PAN],
                                         d_wu[qd % 2][:, k * PAN + ici * NP:
                                                      k * PAN + (ici + 1) * NP],
                                         d_h2b[:, k * KB + pn * PAN:
                                               k * KB + (pn + 1) * PAN],
                                         start=(k == 0), stop=(k == 7)))
                        pk = ic * 2 + pn - 2
                        pguard = (prod_tok.get((b, e_i, pk), (None, 0)) if pk >= 0
                                  else prod_tok.get(
                                      (b, e_i - 1, 32 + pk) if e_i > 0
                                      else (b - 1, 8, 32 + pk), (None, 0)))
                        wmm = [wgld[(b, e_i, qd)], wuld[(b, e_i, qd)], h2b_ld[b],
                               c_end_v, pguard,
                               c_h2_dma[30], c_h2_dma[31],
                               c_x1_dma[30], c_x1_dma[31]]
                        tokm = sc.step("tensor", fns, waits=wmm, inc=s_pe)
                        gu_tok[(b, e_i, ic, pn)] = tokm
                        gslot = d_g[(ic * 2 + pn) % 2]
                        tsi = sc.step("scalar", lambda e, gslot=gslot, P=P:
                            e.activation(gslot[:], P[:, 0:PAN], AF.Silu),
                            waits=[tokm], inc=s_a)
                        tpr = sc.step("vector",
                            lambda e, gslot=gslot, P=P, ic=ic, pn=pn, dprod=dprod:
                            e.tensor_tensor(
                                dprod[:, ic * KB + pn * PAN: ic * KB + (pn + 1) * PAN],
                                gslot[:], P[:, PAN:2 * PAN], ALU.mult),
                            waits=[tsi, prodguard],
                            inc=s_v)
                        prod_tok[(b, e_i, ic * 2 + pn)] = tpr
                if qd + 2 <= 3:
                    sched_wld(qd + 2, gu_tok[(b, e_i, 4 * qd + 3, 1)])
            wdld[(b, e_i)] = sc.step("scalar", lambda e, e_i=e_i: e.dma_start(
                out=d_wd[:], in_=dwb[e_i]),
                waits=[dn_tok.get((b, e_i - 1, 7, 1) if e_i > 0 else (b - 1, 8, 7, 1),
                                  (None, 0)), c_end_pe, c_end_v, wcast_done,
                       c_h2_dma[30], c_h2_dma[31], c_x1_dma[30], c_x1_dma[31]],
                inc=s_wd, inc_k=16)
            prod_all = prod_tok[(b, e_i, 31)]
            for tc in range(8):
                for n in range(2):
                    Pd = (pSB if tc % 2 == 0 else pO)[:, n * PAN:(n + 1) * PAN]
                    fns = [
                        (lambda e, ic=ic, tc=tc, n=n, Pd=Pd, dprod=dprod: e.matmul(
                            Pd, dprod[:, ic * KB + tc * NP:ic * KB + (tc + 1) * NP],
                            d_wd[:, ic * KB + n * PAN: ic * KB + (n + 1) * PAN],
                            start=(ic == 0), stop=(ic == 15)))
                        for ic in range(16)
                    ]
                    wdn = [prod_all, wdld[(b, e_i)],
                           stt_tok.get((b, e_i, tc - 2, n), (None, 0))]
                    if tc % 2 == 1 and n == 1:
                        wdn.append(pX_guard[0] or (None, 0))
                    tokd = sc.step("tensor", fns, waits=wdn, inc=s_pe)
                    dn_tok[(b, e_i, tc, n)] = tokd
                    if shared:
                        tstt = sc.step("vector",
                            lambda e, tc=tc, n=n, Pd=Pd: e.scalar_tensor_tensor(
                                d_acc[:, tc * KB + n * PAN: tc * KB + (n + 1) * PAN],
                                Pd, 1.0,
                                d_acc[:, tc * KB + n * PAN: tc * KB + (n + 1) * PAN],
                                ALU.mult, ALU.add),
                            waits=[tokd, acc_ld[b], rt_w[8 * b + tc]], inc=s_v)
                    else:
                        tstt = sc.step("vector",
                            lambda e, tc=tc, n=n, Pd=Pd, b=b, e_i=e_i:
                            e.scalar_tensor_tensor(
                                d_acc[:, tc * KB + n * PAN: tc * KB + (n + 1) * PAN],
                                Pd, w_res[:, 8 * (8 * b + tc) + e_i:
                                          8 * (8 * b + tc) + e_i + 1],
                                d_acc[:, tc * KB + n * PAN: tc * KB + (n + 1) * PAN],
                                ALU.mult, ALU.add),
                            waits=[tokd, acc_ld[b], rt_w[8 * b + tc]], inc=s_v)
                    stt_tok[(b, e_i, tc, n)] = tstt
                    if tc % 2 == 1 and n == 1:
                        pX_guard[0] = tstt
            down_end[(b, e_i)] = dn_tok[(b, e_i, 7, 1)]
        y_st[b] = sc.step("sync", lambda e, b=b: e.dma_start(
            out=y_d[b * BT:(b + 1) * BT, :].rearrange("(u p) d -> p u d", p=NP),
            in_=d_acc[:].rearrange("p (u d) -> p u d", u=8)),
            waits=[stt_tok[(b, 8, 7, 1)]] +
                  [stt_tok[(b, ei, tc, n)] for ei in range(9)
                   for tc in range(8) for n in range(2)],
            inc=s_y, inc_k=16)

    return _finish(nc, sc, sems, debug and dict(
        QT_dbg=None, qt=None, AT_dbg=None, attn=None,
        MB_dbg=None, W_dbg=W_dbg, nacc=nacc, bias_t=bias_t, w_res=w_res),
        upto)


def _finish(nc, sc, sems, dbg, upto):
    s_y = sems["s_y"]
    s_v = sems["s_v"]
    s_a = sems["s_a"]
    s_pe = sems["s_pe"]
    if dbg:
        if dbg.get("qt") is not None:
            for m in range(NPAIR):
                sc.step("sync", lambda e, m=m: e.dma_start(
                    out=dbg["QT_dbg"][m], in_=dbg["qt"][m][:]),
                    waits=[(s_v, s_v.n), (s_pe, s_pe.n), (s_a, s_a.n)],
                    inc=s_y, inc_k=16)
        if dbg.get("attn") is not None:
            for m in range(NPAIR):
                sc.step("sync", lambda e, m=m: e.dma_start(
                    out=dbg["AT_dbg"][m], in_=dbg["attn"][m][:]),
                    waits=[(s_v, s_v.n), (s_pe, s_pe.n), (s_a, s_a.n)],
                    inc=s_y, inc_k=16)
        if dbg.get("MB_dbg") is not None:
            sc.step("sync", lambda e: e.dma_start(
                out=dbg["MB_dbg"], in_=dbg["nacc"][0:1, 0:8]),
                waits=[(s_v, s_v.n), (s_a, s_a.n)], inc=s_y, inc_k=16)
        if dbg.get("W_dbg") is not None:
            sc.step("sync", lambda e: e.dma_start(
                out=dbg["W_dbg"], in_=dbg["w_res"][:]),
                waits=[(s_v, s_v.n)], inc=s_y, inc_k=16)
    fy = s_y.n
    final_waits = [(sm, sm.n) for sm in sems["all_dma"] if sm.n > 0 and sm is not s_y]
    final_waits += [(s_v, s_v.n), (s_a, s_a.n), (s_pe, s_pe.n)]
    sc.step("sync", lambda e: e.wait_ge(s_y.h, fy), waits=final_waits)

    with nc.Block() as block:
        block.sync(lambda e: sc.replay("sync", e))
        block.tensor(lambda e: sc.replay("tensor", e))
        block.vector(lambda e: sc.replay("vector", e))
        block.scalar(lambda e: sc.replay("scalar", e))
        block.gpsimd(lambda e: sc.replay("gpsimd", e))
    return nc


def make_in_maps(inputs, n_cores=8):
    x = np.asarray(inputs["L0_attn_hidden_states"], dtype=np.float32)
    f32 = lambda a: np.ascontiguousarray(np.asarray(a, dtype=np.float32))

    def lay_proj(w):
        w = np.asarray(w, np.float32)
        return np.ascontiguousarray(
            w.reshape(8, NP, w.shape[1]).transpose(1, 0, 2).reshape(NP, -1))

    def lay_gu(w):
        w = np.asarray(w, np.float32).reshape(8, NP, I2).transpose(1, 0, 2)
        return np.ascontiguousarray(
            np.stack([w[:, :, qd * PAN:(qd + 1) * PAN].reshape(NP, -1)
                      for qd in range(4)]))

    def lay_dw(w):
        w = np.asarray(w, np.float32)
        return np.ascontiguousarray(
            w.reshape(16, NP, D).transpose(1, 0, 2).reshape(NP, -1))

    wq = lay_proj(inputs["L0_attn_Wq"])
    wkv = lay_proj(inputs["L0_attn_Wkv"])
    wo = lay_proj(inputs["L0_attn_Wo"])
    bo = f32(inputs["L0_attn_bo"]).reshape(1, D)
    gw_all = np.stack([lay_gu(np.asarray(inputs["L0_ffn_gate_w"], np.float32)[e])
                       for e in range(NE)] +
                      [lay_gu(np.asarray(inputs["L0_ffn_shared_gate"], np.float32).T)])
    uw_all = np.stack([lay_gu(np.asarray(inputs["L0_ffn_up_w"], np.float32)[e])
                       for e in range(NE)] +
                      [lay_gu(np.asarray(inputs["L0_ffn_shared_up"], np.float32).T)])
    dw_all = np.stack([lay_dw(np.asarray(inputs["L0_ffn_down_w"], np.float32)[e])
                       for e in range(NE)] +
                      [lay_dw(np.asarray(inputs["L0_ffn_shared_down"], np.float32).T)])
    rwT = f32(np.asarray(inputs["L0_ffn_router_w"], dtype=np.float32).T)
    rb = f32(inputs["L0_ffn_routing_bias"]).reshape(1, NE)
    idb = np.eye(NP, dtype=ml_dtypes.bfloat16)
    idf = np.eye(NP, dtype=np.float32)
    shared = dict(wq=wq, wkv=wkv, wo=wo, bo=bo,
                  gw=gw_all, uw=uw_all, dw=dw_all, rwT=rwT, rb=rb,
                  idb=idb, idf=idf)
    in_maps = []
    for c in range(n_cores):
        bb, hh = c // 2, c % 2
        m = dict(shared)
        m["x_own"] = np.ascontiguousarray(x[bb, hh * TOK:(hh + 1) * TOK, :])
        m["x_oth"] = np.ascontiguousarray(x[bb, (1 - hh) * TOK:(2 - hh) * TOK, :])
        in_maps.append(m)
    return in_maps


def kernel(**inputs):
    nc = build_program(upto="D", debug=False)
    in_maps = make_in_maps(inputs)
    res = run_bass_kernel_spmd(nc, in_maps, core_ids=list(range(8)))
    out = np.empty((4, SEQ, D), dtype=np.float32)
    for c in range(8):
        bb, hh = c // 2, c % 2
        out[bb, hh * TOK:(hh + 1) * TOK, :] = np.asarray(res.results[c]["y"])
    return out


# version 06
# speedup vs baseline: 1.8247x; 1.0121x over previous
import math
import contextlib
import numpy as np
import ml_dtypes
import concourse.bass as bass
import concourse.mybir as mybir
from concourse.bass_utils import run_bass_kernel_spmd

F32 = mybir.dt.float32
BF16 = mybir.dt.bfloat16
AF = mybir.ActivationFunctionType
ALU = mybir.AluOpType
AX = mybir.AxisListType

NP = 128
D = 1024
TOK = 4096
SEQ = 8192
NH = 16
DH = 64
NPAIR = 8
I2 = 2048
NE = 8
PAN = 512
NBLK = 4
BT = 1024
EPS = 1e-5
KB = 1024


class Sem:
    def __init__(self, handle):
        self.h = handle
        self.n = 0


class Sched:
    ENGS = ("sync", "tensor", "vector", "scalar", "gpsimd")

    def __init__(self):
        self.steps = {e: [] for e in self.ENGS}
        self.last_wait = {e: {} for e in self.ENGS}

    def step(self, eng, fns, waits=(), inc=None, inc_k=1):
        if not isinstance(fns, (list, tuple)):
            fns = [fns]
        lw = self.last_wait[eng]
        eff = []
        for (s, v) in waits:
            if s is None or v <= 0:
                continue
            if lw.get(id(s), -1) >= v:
                continue
            lw[id(s)] = v
            eff.append((s, v))
        tok = None
        if inc is not None:
            inc.n += inc_k
            tok = (inc, inc.n)
        self.steps[eng].append((eff, list(fns), inc, inc_k))
        return tok

    def replay(self, eng_name, eng):
        for waits, fns, inc, inc_k in self.steps[eng_name]:
            for (s, v) in waits:
                eng.wait_ge(s.h, v)
            instr = None
            for f in fns:
                instr = f(eng)
            if inc is not None:
                instr.then_inc(inc.h, inc_k)


def build_program(upto="D", debug=False):
    nc = bass.Bass("TRN2", target_bir_lowering=False)

    dt = nc.dram_tensor
    x_own = dt("x_own", [TOK, D], F32, kind="ExternalInput").ap()
    x_oth = dt("x_oth", [TOK, D], F32, kind="ExternalInput").ap()
    wq_d = dt("wq", [NP, 8 * KB], F32, kind="ExternalInput").ap()
    wkv_d = dt("wkv", [NP, 16 * KB], F32, kind="ExternalInput").ap()
    wo_d = dt("wo", [NP, 8 * KB], F32, kind="ExternalInput").ap()
    bo_d = dt("bo", [1, D], F32, kind="ExternalInput").ap()
    gw_d = dt("gw", [9, 4, NP, 4 * KB], F32, kind="ExternalInput").ap()
    uw_d = dt("uw", [9, 4, NP, 4 * KB], F32, kind="ExternalInput").ap()
    dw_d = dt("dw", [9, NP, 16 * KB], F32, kind="ExternalInput").ap()
    rwT_d = dt("rwT", [D, NE], F32, kind="ExternalInput").ap()
    rb_d = dt("rb", [1, NE], F32, kind="ExternalInput").ap()
    idb_d = dt("idb", [NP, NP], BF16, kind="ExternalInput").ap()
    idf_d = dt("idf", [NP, NP], F32, kind="ExternalInput").ap()

    y_d = dt("y", [TOK, D], F32, kind="ExternalOutput").ap()

    def scratch(name, shape, dtype):
        kind = "ExternalOutput" if debug else "Internal"
        return dt(name, shape, dtype, kind=kind).ap()

    gwb = dt("gwb", [9, 4, NP, 4 * KB], BF16, kind="Internal").ap()
    uwb = dt("uwb", [9, 4, NP, 4 * KB], BF16, kind="Internal").ap()
    dwb = dt("dwb", [9, NP, 16 * KB], BF16, kind="Internal").ap()
    KT_s = scratch("KT_s", [D, SEQ], BF16)
    V_s = scratch("V_s", [NH, SEQ, DH], BF16)
    X1_s = scratch("X1_s", [TOK, D], F32)
    H2T_s = scratch("H2T_s", [D, TOK], BF16)
    if debug:
        QT_dbg = dt("QT_dbg", [NPAIR, NP, TOK], BF16, kind="ExternalOutput").ap()
        AT_dbg = dt("AT_dbg", [NPAIR, NP, TOK], BF16, kind="ExternalOutput").ap()
        MB_dbg = dt("MB_dbg", [1, 8], F32, kind="ExternalOutput").ap()
        W_dbg = dt("W_dbg", [NP, 256], F32, kind="ExternalOutput").ap()

    al = nc.alloc_sbuf_tensor
    ones_col_b = al("ones_col_b", [NP, 1], BF16)
    ones_row_f = al("ones_row_f", [1, NP], F32)
    bias_t = al("bias_t", [NP, 1], F32)
    nacc = al("nacc", [1, 16], F32)
    w_res = al("w_res", [NP, 256], F32)
    bo_row = al("bo_row", [1, D], F32)
    rb_row = al("rb_row", [1, NE], F32)
    rwT_sb = al("rwT_sb", [NP, 8 * NE], F32)
    idb_sb = al("idb_sb", [NP, NP], BF16)
    idf_sb = al("idf_sb", [NP, NP], F32)

    AR_BYTES = 189 * KB
    arena = al("arena", [NP, AR_BYTES], mybir.dt.uint8)
    ar0 = nc.lookup_mloc(arena).addr
    _uid = [0]

    def at(off_kb, shape, dtype, n=None):
        def one(i):
            _uid[0] += 1
            off = ar0 + int(off_kb * KB) + i * shape[1] * mybir.dt.size(dtype)
            off = (off + 31) // 32 * 32
            return nc.alloc_sbuf_tensor_at(f"ar_{_uid[0]}", shape, dtype, offset=off)
        if n is None:
            return one(0)
        return [one(i) for i in range(n)]

    qt = at(0, [NP, TOK], BF16, n=NPAIR)
    a_x = at(64, [NP, D], F32, n=3)
    a_hn = at(76, [NP, D], F32, n=2)
    a_hnT = at(84, [NP, 4096], BF16, n=2)
    a_kst = at(100, [NP, PAN], BF16, n=2)
    a_vst = at(102, [NP, D], BF16, n=2)
    a_sq = at(106, [NP, PAN], BF16, n=2)
    a_dump = at(108, [NP, D], F32)
    a_rows = at(112, [NP, 64], F32)
    wq_sb = at(112.25, [NP, 8 * KB], BF16)
    wkv_sb = at(128.25, [NP, 16 * KB], BF16)
    attn = at(64, [NP, TOK], BF16, n=NPAIR)
    b_k = at(128, [NP, SEQ], BF16, n=2)
    b_v = at(160, [NP, 64 * 66], BF16, n=2)
    b_pt = at(176.8, [NP, KB], BF16, n=4)
    b_bc = at(184.9, [64, PAN], F32)
    b_rs = at(186.9, [1, PAN], F32)
    wo_sb = at(128, [NP, 8 * KB], BF16)
    c_x = at(144, [NP, D], F32, n=2)
    c_x1 = at(152, [NP, D], F32, n=2)
    c_h2f = at(160, [NP, D], F32, n=2)
    c_h2T32 = at(168, [NP, D], F32, n=2)
    c_h2Tb = at(176, [NP, D], BF16, n=2)
    c_rows = at(180, [NP, 64], F32)
    c_rt = at(180.25, [NP, 64], F32)
    c_dump = at(180.5, [NP, D], F32)
    d_acc = at(0, [NP, 8 * KB], F32)
    d_h2b = at(32, [NP, 8 * KB], BF16)
    d_prod = at(48, [NP, 16 * KB], BF16, n=2)
    d_wg = at(112, [NP, 4 * KB], BF16, n=2)
    d_wu = at(128, [NP, 4 * KB], BF16, n=2)
    d_wd = at(144, [NP, 16 * KB], BF16)
    d_g = at(176, [NP, PAN], F32, n=2)

    pSA = nc.alloc_psum_tensor("pSA", [NP, 2048], F32)
    pSB = nc.alloc_psum_tensor("pSB", [NP, 1024], F32)
    pO = nc.alloc_psum_tensor("pO", [NP, 1024], F32)
    PTB0 = pSB[:]
    PTB1 = pO[:]

    mk = lambda nm: Sem(nc.alloc_semaphore(name=nm))
    s_pre = mk("s_pre")
    s_xb = [mk(f"s_xb{i}") for i in range(3)]
    s_kst = [mk(f"s_kst{i}") for i in range(2)]
    s_vst = [mk(f"s_vst{i}") for i in range(2)]
    s_kld = [mk(f"s_kld{i}") for i in range(2)]
    s_vld = [mk(f"s_vld{i}") for i in range(2)]
    s_wq, s_wkv, s_wo, s_wd = mk("s_wq"), mk("s_wkv"), mk("s_wo"), mk("s_wd")
    s_wcast = mk("s_wcast")
    s_cx = [mk(f"s_cx{i}") for i in range(2)]
    s_x1 = [mk(f"s_x1{i}") for i in range(2)]
    s_h2 = [mk(f"s_h2{i}") for i in range(2)]
    s_wg = [mk(f"s_wg{i}") for i in range(2)]
    s_wu = [mk(f"s_wu{i}") for i in range(2)]
    s_acc = [mk(f"s_acc{i}") for i in range(2)]
    s_h2b = [mk(f"s_h2b{i}") for i in range(2)]
    s_y, s_pe, s_v, s_a = mk("s_y"), mk("s_pe"), mk("s_v"), mk("s_a")
    all_dma_sems = ([s_pre] + s_xb + s_kst + s_vst + s_kld + s_vld +
                    [s_wq, s_wkv, s_wo, s_wd, s_wcast] + s_cx + s_x1 + s_h2 +
                    s_wg + s_wu + s_acc + s_h2b + [s_y])
    sems = dict(all_dma=all_dma_sems, s_y=s_y, s_pe=s_pe, s_v=s_v, s_a=s_a)

    sc = Sched()

    sc.step("vector", lambda e: e.memset(ones_col_b[:], 1.0), inc=s_v)
    sc.step("vector", lambda e: e.memset(ones_row_f[:], 1.0), inc=s_v)
    sc.step("vector", lambda e: e.memset(nacc[:], 0.0), inc=s_v)
    ones_ready = (s_v, s_v.n)
    sc.step("sync", lambda e: e.dma_start(out=bo_row[:], in_=bo_d), inc=s_pre, inc_k=16)
    sc.step("sync", lambda e: e.dma_start(out=rb_row[:], in_=rb_d), inc=s_pre, inc_k=16)
    sc.step("sync", lambda e: e.dma_start(
        out=rwT_sb[:].rearrange("p (j e) -> p j e", j=8),
        in_=rwT_d.rearrange("(j p) e -> p j e", p=NP)), inc=s_pre, inc_k=16)
    sc.step("sync", lambda e: e.dma_start(out=idb_sb[:], in_=idb_d), inc=s_pre, inc_k=16)
    sc.step("sync", lambda e: e.dma_start(out=idf_sb[:], in_=idf_d), inc=s_pre, inc_k=16)
    small_ld = (s_pre, s_pre.n)
    sc.step("gpsimd", lambda e: e.dma_start(out=wq_sb[:], in_=wq_d),
            inc=s_wq, inc_k=16)
    wq_ld = (s_wq, s_wq.n)
    sc.step("gpsimd", lambda e: e.dma_start(out=wkv_sb[:], in_=wkv_d),
            inc=s_wkv, inc_k=16)
    wkv_ld = (s_wkv, s_wkv.n)
    if upto == "D":
        for e_i in range(9):
            for qd in range(4):
                sc.step("gpsimd", lambda e, e_i=e_i, qd=qd: e.dma_start(
                    out=gwb[e_i, qd], in_=gw_d[e_i, qd]), inc=s_wcast, inc_k=16)
                sc.step("gpsimd", lambda e, e_i=e_i, qd=qd: e.dma_start(
                    out=uwb[e_i, qd], in_=uw_d[e_i, qd]), inc=s_wcast, inc_k=16)
            sc.step("gpsimd", lambda e, e_i=e_i: e.dma_start(
                out=dwb[e_i], in_=dw_d[e_i]), inc=s_wcast, inc_k=16)
    wcast_done = (s_wcast, s_wcast.n)

    RC = {"sum": 0, "m": 2, "msq": 4, "vep": 6, "var": 8, "ln": 10, "r": 12}

    def rcol(name, t):
        return a_rows[:, RC[name] + (t % 2): RC[name] + (t % 2) + 1]

    norm_tok = [None]
    tile_norm_done = {}
    tile_tp_done = {}
    tile_ev_done = {}
    panel_mm_done = {}
    kst_dma = {}
    kst_sq = {}
    vst_dma = {}
    ps_ring = [pSA[:, 0:512], pSA[:, 512:1024], pSA[:, 1024:1536],
               pSA[:, 1536:2048]]
    ps_tok = [None, None, None, None]
    ps_i = [0]

    def next_ps():
        i = ps_i[0] % 4
        ps_i[0] += 1
        return i

    a_mm_end = [None]
    ASQ_guard = [None, None]
    pX_guard = [None]

    for t in range(64):
        xb = a_x[t % 3]
        src = (x_own if t < 32 else x_oth)
        tsl = (t % 32)
        tok_x = sc.step("sync", lambda e, s=src, q=tsl, xb=xb: e.dma_start(
            out=xb[:], in_=s[q * NP:(q + 1) * NP, :]),
            waits=[tile_norm_done.get(t - 3, (None, 0))],
            inc=s_xb[t % 3], inc_k=16)
        v1 = sc.step("vector", lambda e, xb=xb, t=t: e.reduce_sum(
            rcol("sum", t), xb[:], axis=AX.X), waits=[tok_x], inc=s_v)
        a1 = sc.step("scalar", lambda e, xb=xb, t=t: e.activation(
            a_dump[:], xb[:], AF.Square, accum_out=rcol("msq", t)),
            waits=[tok_x], inc=s_a)
        v2 = sc.step("vector", lambda e, t=t: e.tensor_scalar_mul(
            rcol("m", t), rcol("sum", t), 1.0 / D), waits=[v1], inc=s_v)
        v4 = sc.step("vector", lambda e, t=t: e.tensor_scalar(
            rcol("vep", t), rcol("msq", t), 1.0 / D, EPS, ALU.mult, ALU.add),
            waits=[a1], inc=s_v)
        v3 = sc.step("vector", lambda e, t=t: e.tensor_tensor(
            rcol("var", t), rcol("m", t), rcol("m", t), ALU.mult),
            waits=[v2], inc=s_v)
        v5 = sc.step("vector", lambda e, t=t: e.tensor_sub(
            rcol("vep", t), rcol("vep", t), rcol("var", t)),
            waits=[v3, v4], inc=s_v)
        a2 = sc.step("scalar", lambda e, t=t: e.activation(
            rcol("ln", t), rcol("vep", t), AF.Ln), waits=[v5], inc=s_a)
        a3 = sc.step("scalar", lambda e, t=t: e.activation(
            rcol("r", t), rcol("ln", t), AF.Exp, scale=-0.5),
            waits=[a2], inc=s_a)
        hb = a_hn[t % 2]
        v6 = sc.step("vector", lambda e, xb=xb, hb=hb, t=t: e.tensor_scalar(
            hb[:], xb[:], rcol("m", t), rcol("r", t), ALU.subtract, ALU.mult),
            waits=[a3, tile_tp_done.get(t - 2, (None, 0))], inc=s_v)
        tile_norm_done[t] = v6
        PTB = (PTB0 if t % 2 == 0 else PTB1)
        fns = [
            (lambda e, hb=hb, c=c, PTB=PTB: e.transpose(
                PTB[:, c * NP:(c + 1) * NP], hb[:, c * NP:(c + 1) * NP],
                idf_sb[:]))
            for c in range(8)
        ]
        jprev = panel_mm_done.get(t // 4 - 2, (None, 0))
        p1 = sc.step("tensor", fns, waits=[v6, tile_ev_done.get(t - 2, (None, 0)),
                                           small_ld], inc=s_pe)
        hT = a_hnT[(t // 4) % 2]
        ev = sc.step("vector", lambda e, hT=hT, t=t, PTB=PTB: e.tensor_copy(
            hT[:].rearrange("p (c u) -> p c u", c=8)[:, :, (t % 4) * NP:(t % 4 + 1) * NP],
            PTB[:].rearrange("p (c u) -> p c u", c=8)),
            waits=[p1, jprev], inc=s_v)
        tile_ev_done[t] = ev
        tile_tp_done[t] = p1

        if t % 4 != 3:
            continue
        j = t // 4
        own = j < 8
        hT = a_hnT[j % 2]
        pan_ready = ev
        last_mm = None

        def mm_group(lhs_fn, rhs_fn, extra_waits, n_k=8):
            nonlocal last_mm
            si = next_ps()
            ps = ps_ring[si]
            fns = [
                (lambda e, k=k, si=si: e.matmul(
                    ps_ring[si], lhs_fn(k), rhs_fn(k),
                    start=(k == 0), stop=(k == n_k - 1)))
                for k in range(n_k)
            ]
            w = [pan_ready] + extra_waits
            if ps_tok[si] is not None:
                w.append(ps_tok[si])
            tokm = sc.step("tensor", fns, waits=w, inc=s_pe)
            last_mm = tokm
            return si, tokm

        if own:
            jq = j
            for m in range(8):
                si, tokm = mm_group(
                    lambda k, m=m: wq_sb[:, k * KB + m * NP: k * KB + (m + 1) * NP],
                    lambda k, hT=hT: hT[:, k * PAN:(k + 1) * PAN], [wq_ld])
                tev = sc.step("vector", lambda e, m=m, jq=jq, si=si: e.tensor_copy(
                    qt[m][:, jq * PAN:(jq + 1) * PAN], ps_ring[si]),
                    waits=[tokm], inc=s_v)
                ps_tok[si] = tev
                sqb = a_sq[m % 2]
                tsq = sc.step("scalar", lambda e, m=m, jq=jq, sqb=sqb: e.activation(
                    sqb[:], qt[m][:, jq * PAN:(jq + 1) * PAN], AF.Square),
                    waits=[tev, ASQ_guard[m % 2] or (None, 0)], inc=s_a)
                tnm = sc.step("tensor", lambda e, sqb=sqb, si=si: e.matmul(
                    ps_ring[si][0:1, :], ones_col_b[:], sqb[:],
                    start=True, stop=True),
                    waits=[tsq, ones_ready, ps_tok[si] or (None, 0)], inc=s_pe)
                ASQ_guard[m % 2] = tnm
                trm = sc.step("vector", lambda e, si=si: e.reduce_max(
                    nacc[0:1, 8:9], ps_ring[si][0:1, :], axis=AX.X),
                    waits=[tnm], inc=s_v)
                ps_tok[si] = trm
                tmx = sc.step("vector", lambda e: e.tensor_max(
                    nacc[0:1, 0:1], nacc[0:1, 0:1], nacc[0:1, 8:9]),
                    waits=[trm], inc=s_v)
                pX_guard[0] = trm
                norm_tok[0] = tmx
        for c in range(8):
            si, tokm = mm_group(
                lambda k, c=c: wkv_sb[:, k * 2048 + c * NP: k * 2048 + (c + 1) * NP],
                lambda k, hT=hT: hT[:, k * PAN:(k + 1) * PAN], [wkv_ld])
            kb = a_kst[c % 2]
            tev = sc.step("vector", lambda e, kb=kb, si=si: e.tensor_copy(
                kb[:], ps_ring[si]),
                waits=[tokm, kst_dma.get((j, c - 2), kst_dma.get((j - 1, c + 6), (None, 0))),
                       kst_sq.get((j, c - 2), kst_sq.get((j - 1, c + 6), (None, 0)))],
                inc=s_v)
            ps_tok[si] = tev
            td = sc.step("sync", lambda e, kb=kb, c=c, j=j: e.dma_start(
                out=KT_s[c * NP:(c + 1) * NP, j * PAN:(j + 1) * PAN], in_=kb[:]),
                waits=[tev], inc=s_kst[c % 2], inc_k=16)
            kst_dma[(j, c)] = td
            sqb = a_sq[c % 2]
            tsq = sc.step("scalar", lambda e, kb=kb, sqb=sqb: e.activation(
                sqb[:], kb[:], AF.Square),
                waits=[tev, ASQ_guard[c % 2] or (None, 0)], inc=s_a)
            tnm = sc.step("tensor", lambda e, sqb=sqb, si=si: e.matmul(
                ps_ring[si][0:1, :], ones_col_b[:], sqb[:],
                start=True, stop=True),
                waits=[tsq, ps_tok[si] or (None, 0)], inc=s_pe)
            ASQ_guard[c % 2] = tnm
            last_mm = tnm
            trm = sc.step("vector", lambda e, si=si: e.reduce_max(
                nacc[0:1, 9:10], ps_ring[si][0:1, :], axis=AX.X),
                waits=[tnm], inc=s_v)
            ps_tok[si] = trm
            tmx = sc.step("vector", lambda e: e.tensor_max(
                nacc[0:1, 1:2], nacc[0:1, 1:2], nacc[0:1, 9:10]),
                waits=[trm], inc=s_v)
            pX_guard[0] = trm
            kst_sq[(j, c)] = tsq
            norm_tok[0] = tmx
        for u in range(4):
            vb = a_vst[u % 2]
            toks = []
            for n in range(2):
                si, tokm = mm_group(
                    lambda k, u=u, hT=hT: hT[:, k * PAN + u * NP: k * PAN + (u + 1) * NP],
                    lambda k, n=n: wkv_sb[:, k * 2048 + KB + n * PAN: k * 2048 + KB + (n + 1) * PAN],
                    [wkv_ld])
                tev = sc.step("vector", lambda e, vb=vb, n=n, si=si: e.tensor_copy(
                    vb[:, n * PAN:(n + 1) * PAN], ps_ring[si]),
                    waits=[tokm, vst_dma.get((j, u - 2), vst_dma.get((j - 1, u + 2), (None, 0)))],
                    inc=s_v)
                ps_tok[si] = tev
                toks.append(tev)
            tt = j * 4 + u
            td = sc.step("sync", lambda e, vb=vb, tt=tt: e.dma_start(
                out=V_s[:, tt * NP:(tt + 1) * NP, :].rearrange("h p n -> p h n"),
                in_=vb[:].rearrange("p (h n) -> p h n", h=NH)),
                waits=toks, inc=s_vst[u % 2], inc_k=16)
            vst_dma[(j, u)] = td
        panel_mm_done[j] = last_mm

    a_scratch_done = [(s_kst[0], s_kst[0].n), (s_kst[1], s_kst[1].n),
                      (s_vst[0], s_vst[0].n), (s_vst[1], s_vst[1].n)]
    a_mm_end = panel_mm_done[15]

    bv1 = sc.step("vector", lambda e: e.tensor_tensor(
        nacc[0:1, 2:3], nacc[0:1, 0:1], nacc[0:1, 1:2], ALU.mult),
        waits=[norm_tok[0]], inc=s_v)
    ba1 = sc.step("scalar", lambda e: e.activation(
        nacc[0:1, 3:4], nacc[0:1, 2:3], AF.Ln), waits=[bv1], inc=s_a)
    ba2 = sc.step("scalar", lambda e: e.activation(
        nacc[0:1, 4:5], nacc[0:1, 3:4], AF.Exp, scale=0.5), waits=[ba1], inc=s_a)
    bv2 = sc.step("vector", lambda e: e.tensor_scalar_mul(
        nacc[0:1, 5:6], nacc[0:1, 4:5], -0.125), waits=[ba2], inc=s_v)
    bt1 = sc.step("tensor", lambda e: e.matmul(
        pSA[:, 0:1], ones_row_f[:], nacc[0:1, 5:6], start=True, stop=True),
        waits=[bv2, ps_tok[0] or (None, 0)], inc=s_pe)
    bt2 = sc.step("vector", lambda e: e.tensor_copy(bias_t[:], pSA[:, 0:1]),
                  waits=[bt1], inc=s_v)
    bias_ready = bt2
    pX_guard[0] = bt2

    if upto == "A":
        return _finish(nc, sc, sems, debug and dict(
            QT_dbg=QT_dbg, qt=qt, AT_dbg=None, attn=None,
            MB_dbg=MB_dbg, W_dbg=None, nacc=nacc, bias_t=bias_t, w_res=w_res),
            upto)

    kld = {}
    vld = {}
    for p in range(2):
        kld[p] = sc.step("sync", lambda e, p=p: e.dma_start(
            out=b_k[p][:], in_=KT_s[p * NP:(p + 1) * NP, :]),
            waits=a_scratch_done + [a_mm_end], inc=s_kld[p % 2], inc_k=16)
    for hh in range(2):
        vld[hh] = sc.step("sync", lambda e, hh=hh: e.dma_start(
            out=b_v[hh % 2][:].rearrange("p (c w) -> p c w", w=66)[:, :, 0:64],
            in_=V_s[hh].rearrange("(c p) n -> p c n", p=NP)),
            waits=a_scratch_done + [a_mm_end], inc=s_vld[hh % 2], inc_k=16)
    vmemset = {}
    for sl in range(2):
        vmemset[sl] = sc.step("vector", lambda e, sl=sl: e.memset(
            b_v[sl][:].rearrange("p (c w) -> p c w", w=66)[:, :, 64:65], 1.0),
            waits=[a_mm_end], inc=s_v)

    exp_tok = {}
    pv_tok = {}
    smm_end = {}
    norm_done = {}
    pair_end = {}
    for p in range(NPAIR):
        kbuf = b_k[p % 2]

        def sfns(head, g, p=p, kbuf=kbuf):
            Sp = (pSA[:, (g % 2) * 1024:(g % 2) * 1024 + 1024] if head == 0
                  else pSB[:])
            rbase = 64 * head
            q = cur_q[0]
            return [
                (lambda e, c=c, Sp=Sp, kbuf=kbuf, rbase=rbase, p=p, q=q, cc=cc:
                 e.matmul(Sp[:, cc * PAN:(cc + 1) * PAN],
                          kbuf[rbase:rbase + 64, c * NP:(c + 1) * NP],
                          qt[p][rbase:rbase + 64, q * PAN:(q + 1) * PAN],
                          start=True, stop=True))
                for cc, c in enumerate((2 * g, 2 * g + 1))
            ]

        def pvfns(head, g):
            pb = b_pt[2 * head + g % 2]
            vbuf = b_v[head]
            ocol = head * PAN
            return [
                (lambda e, c=c, pb=pb, vbuf=vbuf, g=g, cc=cc, ocol=ocol:
                 e.matmul(pO[0:65, ocol:ocol + PAN],
                          vbuf[:, c * 66:c * 66 + 65],
                          pb[:, cc * PAN:(cc + 1) * PAN],
                          start=(g == 0 and cc == 0), stop=(g == 31 and cc == 1)))
                for cc, c in enumerate((2 * g, 2 * g + 1))
            ]

        cur_q = [0]
        for q in range(8):
            cur_q[0] = q
            w0 = [kld[p], vld[2 * p], vld[2 * p + 1],
                  norm_done.get((2 * p, q - 1),
                                norm_done.get((2 * p - 1, 7), (None, 0))),
                  norm_done.get((2 * p + 1, q - 1), (None, 0))]
            if p == 0 and q == 0:
                w0 += [vmemset[0], vmemset[1]]
            stok = {}
            stok[0] = sc.step("tensor", sfns(1, 0) + sfns(0, 0), waits=w0,
                              inc=s_pe)
            teA_tok = {}
            pvt = {}
            for g in range(32):
                pbA = b_pt[g % 2]
                pbB = b_pt[2 + g % 2]
                half = (g % 2) * 1024
                teB = sc.step("scalar", lambda e, pbB=pbB: e.activation(
                    pbB[:], pSB[:], AF.Exp, bias=bias_t[:, 0:1], scale=0.125),
                    waits=[stok[g], bias_ready], inc=s_a)
                teA = sc.step("scalar", lambda e, pbA=pbA, half=half: e.activation(
                    pbA[:], pSA[:, half:half + 1024], AF.Exp,
                    bias=bias_t[:, 0:1], scale=0.125),
                    waits=[], inc=s_a)
                teA_tok[g] = teA
                if g + 1 < 32:
                    stok[g + 1] = sc.step("tensor", sfns(1, g + 1) + sfns(0, g + 1),
                                          waits=[teB, teA_tok.get(g - 1, (None, 0))],
                                          inc=s_pe)
                    pvt[g] = sc.step("tensor", pvfns(0, g) + pvfns(1, g),
                                     waits=[teA], inc=s_pe)
                else:
                    pvt[g] = sc.step("tensor", pvfns(0, g) + pvfns(1, g),
                                     waits=[teA], inc=s_pe)
            for head in range(2):
                hh = 2 * p + head
                ocol = head * PAN
                rbase = 64 * head
                n1 = sc.step("vector", lambda e, ocol=ocol: e.reciprocal(
                    b_rs[:], pO[64:65, ocol:ocol + PAN]),
                    waits=[pvt[31]], inc=s_v)
                n2 = sc.step("tensor", lambda e: e.matmul(
                    pSA[0:64, 0:PAN], ones_row_f[0:1, 0:64], b_rs[:],
                    start=True, stop=True),
                    waits=[n1, pX_guard[0] or (None, 0)], inc=s_pe)
                n3 = sc.step("vector", lambda e: e.tensor_copy(
                    b_bc[:], pSA[0:64, 0:PAN]), waits=[n2], inc=s_v)
                n4 = sc.step("vector",
                    lambda e, p=p, rbase=rbase, q=q, ocol=ocol: e.tensor_tensor(
                        attn[p][rbase:rbase + 64, q * PAN:(q + 1) * PAN],
                        pO[0:64, ocol:ocol + PAN], b_bc[:], ALU.mult),
                    waits=[n3], inc=s_v)
                norm_done[(hh, q)] = n4
                pX_guard[0] = n3
        pair_end[p] = pvt[31]
        if p + 1 < NPAIR:
            for head in range(2):
                vld[2 * (p + 1) + head] = sc.step("sync",
                    lambda e, p=p, head=head: e.dma_start(
                        out=b_v[head][:].rearrange(
                            "p (c w) -> p c w", w=66)[:, :, 0:64],
                        in_=V_s[2 * (p + 1) + head].rearrange(
                            "(c p) n -> p c n", p=NP)),
                    waits=[pair_end[p]], inc=s_vld[head], inc_k=16)
        if p + 2 < NPAIR:
            kld[p + 2] = sc.step("sync", lambda e, p=p: e.dma_start(
                out=b_k[p % 2][:], in_=KT_s[(p + 2) * NP:(p + 3) * NP, :]),
                waits=[pair_end[p]], inc=s_kld[p % 2], inc_k=16)

    b_end_pe = pair_end[NPAIR - 1]
    b_end_v = norm_done[(NH - 1, 7)]

    if upto == "B":
        return _finish(nc, sc, sems, debug and dict(
            QT_dbg=QT_dbg, qt=qt, AT_dbg=AT_dbg, attn=attn,
            MB_dbg=MB_dbg, W_dbg=None, nacc=nacc, bias_t=bias_t, w_res=w_res),
            upto)

    wo_ld = sc.step("gpsimd", lambda e: e.dma_start(out=wo_sb[:], in_=wo_d),
        waits=[b_end_pe], inc=s_wo, inc_k=16)

    CC = {"sum": 0, "m": 2, "msq": 4, "vep": 6, "var": 8, "ln": 10, "r": 12}

    def ccol(name, t):
        return c_rows[:, CC[name] + (t % 2): CC[name] + (t % 2) + 1]

    x1_dma = {}
    h2_dma = {}
    x1_add = {}
    h2T_ev = {}
    rt_w = {}
    rt_r0 = {}
    proj_tok = {}
    for t in range(32):
        xb = c_x[t % 2]
        tok_x = sc.step("sync", lambda e, t=t, xb=xb: e.dma_start(
            out=xb[:], in_=x_own[t * NP:(t + 1) * NP, :]),
            waits=[b_end_pe, x1_add.get(t - 2, (None, 0))],
            inc=s_cx[t % 2], inc_k=16)
        P = (pSA[:, 0:1024] if t % 2 == 0 else pSA[:, 1024:2048])
        fns = []
        for n in range(2):
            for k in range(8):
                fns.append(lambda e, n=n, k=k, t=t, P=P: e.matmul(
                    P[:, n * PAN:(n + 1) * PAN],
                    attn[k][:, t * NP:(t + 1) * NP],
                    wo_sb[:, k * KB + n * PAN: k * KB + (n + 1) * PAN],
                    start=(k == 0), stop=False))
            fns.append(lambda e, n=n, P=P: e.matmul(
                P[:, n * PAN:(n + 1) * PAN], ones_row_f[:],
                bo_row[0:1, n * PAN:(n + 1) * PAN], start=False, stop=True))
        tp = sc.step("tensor", fns,
                     waits=[wo_ld, small_ld, b_end_v, x1_add.get(t - 2, (None, 0)),
                            rt_r0.get(t - 1, (None, 0))],
                     inc=s_pe)
        proj_tok[t] = tp
        x1b = c_x1[t % 2]
        tadd = sc.step("vector", lambda e, x1b=x1b, xb=xb, P=P: e.tensor_add(
            x1b[:], P, xb[:]),
            waits=[tp, tok_x, x1_dma.get(t - 2, (None, 0))], inc=s_v)
        x1_add[t] = tadd
        x1_dma[t] = sc.step("sync", lambda e, x1b=x1b, t=t: e.dma_start(
            out=X1_s[t * NP:(t + 1) * NP, :], in_=x1b[:]),
            waits=[tadd], inc=s_x1[t % 2], inc_k=16)
        v1 = sc.step("vector", lambda e, x1b=x1b, t=t: e.reduce_sum(
            ccol("sum", t), x1b[:], axis=AX.X), waits=[tadd], inc=s_v)
        a1 = sc.step("scalar", lambda e, x1b=x1b, t=t: e.activation(
            c_dump[:], x1b[:], AF.Square, accum_out=ccol("msq", t)),
            waits=[tadd], inc=s_a)
        v2 = sc.step("vector", lambda e, t=t: e.tensor_scalar_mul(
            ccol("m", t), ccol("sum", t), 1.0 / D), waits=[v1], inc=s_v)
        v4 = sc.step("vector", lambda e, t=t: e.tensor_scalar(
            ccol("vep", t), ccol("msq", t), 1.0 / D, EPS, ALU.mult, ALU.add),
            waits=[a1], inc=s_v)
        v3 = sc.step("vector", lambda e, t=t: e.tensor_tensor(
            ccol("var", t), ccol("m", t), ccol("m", t), ALU.mult),
            waits=[v2], inc=s_v)
        v5 = sc.step("vector", lambda e, t=t: e.tensor_sub(
            ccol("vep", t), ccol("vep", t), ccol("var", t)),
            waits=[v3, v4], inc=s_v)
        a2 = sc.step("scalar", lambda e, t=t: e.activation(
            ccol("ln", t), ccol("vep", t), AF.Ln), waits=[v5], inc=s_a)
        a3 = sc.step("scalar", lambda e, t=t: e.activation(
            ccol("r", t), ccol("ln", t), AF.Exp, scale=-0.5), waits=[a2], inc=s_a)
        h2b_ = c_h2f[t % 2]
        v6 = sc.step("vector", lambda e, h2b_=h2b_, x1b=x1b, t=t: e.tensor_scalar(
            h2b_[:], x1b[:], ccol("m", t), ccol("r", t), ALU.subtract, ALU.mult),
            waits=[a3, h2T_ev.get(t - 2, (None, 0))], inc=s_v)
        if t % 2 == 0:
            tps = [pSB[:, c * NP:(c + 1) * NP] for c in range(8)]
        else:
            tps = [pO[:, c * NP:(c + 1) * NP] for c in range(8)]
        fns = [
            (lambda e, c=c, h2b_=h2b_, dst=tps[c]: e.transpose(
                dst, h2b_[:, c * NP:(c + 1) * NP], idf_sb[:]))
            for c in range(8)
        ]
        wtp = [v6, h2T_ev.get(t - 2, (None, 0))]
        if t % 2 == 1:
            wtp.append(pX_guard[0] or (None, 0))
        tpt = sc.step("tensor", fns, waits=wtp, inc=s_pe)
        hT32 = c_h2T32[t % 2]
        if t % 2 == 0:
            ev1 = sc.step("vector", lambda e, hT32=hT32: e.tensor_copy(
                hT32[:], pSB[:]), waits=[tpt], inc=s_v)
        else:
            ev1 = sc.step("vector", lambda e, hT32=hT32: e.tensor_copy(
                hT32[:], pO[:]), waits=[tpt], inc=s_v)
        hTb = c_h2Tb[t % 2]
        ev2 = sc.step("vector", lambda e, hTb=hTb, hT32=hT32: e.tensor_copy(
            hTb[:], hT32[:]), waits=[ev1, h2_dma.get(t - 2, (None, 0))], inc=s_v)
        h2T_ev[t] = ev2
        h2_dma[t] = sc.step("sync", lambda e, hTb=hTb, t=t: e.dma_start(
            out=H2T_s.rearrange("(j p) u -> p j u", p=NP)[:, :, t * NP:(t + 1) * NP],
            in_=hTb[:].rearrange("p (j u) -> p j u", j=8)),
            waits=[ev2], inc=s_h2[t % 2], inc_k=16)
        rcol0 = 1024 if t % 2 == 0 else 0
        fns = [
            (lambda e, jj=jj, hT32=hT32, rcol0=rcol0: e.matmul(
                pSA[:, rcol0:rcol0 + NE], hT32[:, jj * NP:(jj + 1) * NP],
                rwT_sb[:, jj * NE:(jj + 1) * NE],
                start=(jj == 0), stop=False))
            for jj in range(8)
        ]
        fns.append(lambda e, rcol0=rcol0: e.matmul(
            pSA[:, rcol0:rcol0 + NE], ones_row_f[:], rb_row[:],
            start=False, stop=True))
        wrt = [ev1, small_ld, x1_add.get(t - 1, (None, 0)),
               pX_guard[0] or (None, 0)]
        trt = sc.step("tensor", fns, waits=wrt, inc=s_pe)
        CL = lambda i: c_rt[:, 8 * i: 8 * i + 8]
        r0 = sc.step("vector", lambda e, rcol0=rcol0: e.tensor_copy(
            CL(0), pSA[:, rcol0:rcol0 + NE]), waits=[trt], inc=s_v)
        rt_r0[t] = r0
        pX_guard[0] = r0
        r1 = sc.step("vector", lambda e: e.reduce_max(
            c_rt[:, 56:57], CL(0), axis=AX.X), waits=[r0], inc=s_v)
        r2 = sc.step("vector", lambda e: e.tensor_single_scalar(
            CL(1), CL(0), c_rt[:, 56:57], ALU.is_ge), waits=[r1], inc=s_v)
        r3 = sc.step("vector", lambda e: e.scalar_tensor_tensor(
            CL(2), CL(1), -1e30, CL(0), ALU.mult, ALU.add), waits=[r2], inc=s_v)
        r4 = sc.step("vector", lambda e: e.reduce_max(
            c_rt[:, 57:58], CL(2), axis=AX.X), waits=[r3], inc=s_v)
        r5 = sc.step("vector", lambda e: e.tensor_single_scalar(
            CL(3), CL(2), c_rt[:, 57:58], ALU.is_ge), waits=[r4], inc=s_v)
        r6 = sc.step("vector", lambda e: e.tensor_add(CL(4), CL(1), CL(3)),
                     waits=[r5], inc=s_v)
        r7 = sc.step("scalar", lambda e: e.activation(CL(5), CL(0), AF.Sigmoid),
                     waits=[r0], inc=s_a)
        r8 = sc.step("vector", lambda e: e.tensor_tensor(
            CL(6), CL(5), CL(4), ALU.mult), waits=[r6, r7], inc=s_v)
        r9 = sc.step("vector", lambda e: e.reduce_sum(
            c_rt[:, 58:59], CL(6), axis=AX.X), waits=[r8], inc=s_v)
        r10 = sc.step("vector", lambda e: e.reciprocal(
            c_rt[:, 59:60], c_rt[:, 58:59]), waits=[r9], inc=s_v)
        r11 = sc.step("vector", lambda e, t=t: e.tensor_scalar_mul(
            w_res[:, 8 * t: 8 * t + 8], CL(6), c_rt[:, 59:60]),
            waits=[r10], inc=s_v)
        rt_w[t] = r11

    c_end_pe = (s_pe, s_pe.n)
    c_end_v = (s_v, s_v.n)
    c_x1_dma = dict(x1_dma)
    c_h2_dma = dict(h2_dma)

    if upto == "C":
        return _finish(nc, sc, sems, debug and dict(
            QT_dbg=None, qt=None, AT_dbg=AT_dbg, attn=attn,
            MB_dbg=MB_dbg, W_dbg=W_dbg, nacc=nacc, bias_t=bias_t, w_res=w_res),
            upto)

    acc_ld = {}
    h2b_ld = {}
    y_st = {}
    gu_tok = {}
    prod_tok = {}
    dn_tok = {}
    stt_tok = {}
    wgld = {}
    wuld = {}
    wdld = {}
    down_end = {}

    for b in range(NBLK):
        wacc = [c_end_pe, c_end_v, c_x1_dma[8 * b + 7],
                c_x1_dma[8 * b + 6], y_st.get(b - 1, (None, 0))]
        acc_ld[b] = sc.step("sync", lambda e, b=b: e.dma_start(
            out=d_acc[:].rearrange("p (u d) -> p u d", u=8),
            in_=X1_s[b * BT:(b + 1) * BT, :].rearrange("(u p) d -> p u d", p=NP)),
            waits=wacc, inc=s_acc[b % 2], inc_k=16)
        h2b_ld[b] = sc.step("sync", lambda e, b=b: e.dma_start(
            out=d_h2b[:].rearrange("p (j u) -> p j u", j=8),
            in_=H2T_s[:, b * BT:(b + 1) * BT].rearrange("(j p) u -> p j u", p=NP)),
            waits=[c_h2_dma[8 * b + 7], c_h2_dma[8 * b + 6], c_end_pe,
                   gu_tok.get((b - 1, 8, 15, 1), (None, 0))],
            inc=s_h2b[b % 2], inc_k=16)
        for e_i in range(9):
            shared = (e_i == NE)
            sidx = b * 9 + e_i
            dprod = d_prod[sidx % 2]
            pidx = sidx - 2
            prodguard = (down_end.get((pidx // 9, pidx % 9), (None, 0))
                         if pidx >= 0 else (None, 0))
            def sched_wld(qd, when_tok):
                wgld[(b, e_i, qd)] = sc.step("sync",
                    lambda e, qd=qd, e_i=e_i: e.dma_start(
                        out=d_wg[qd % 2][:], in_=gwb[e_i, qd]),
                    waits=[when_tok, c_end_pe, wcast_done],
                    inc=s_wg[qd % 2], inc_k=16)
                wuld[(b, e_i, qd)] = sc.step("sync",
                    lambda e, qd=qd, e_i=e_i: e.dma_start(
                        out=d_wu[qd % 2][:], in_=uwb[e_i, qd]),
                    waits=[when_tok, c_end_pe, wcast_done],
                    inc=s_wu[qd % 2], inc_k=16)

            def guard_gu(qd):
                key = (b, e_i, 4 * 0 + qd - 2)
                if qd >= 2:
                    return gu_tok.get((b, e_i, (qd - 2) * 4 + 3, 1), (None, 0))
                pe_i, pb = (e_i - 1, b) if e_i > 0 else (8, b - 1)
                if pb < 0:
                    return (None, 0)
                return gu_tok.get((pb, pe_i, (qd + 2) * 4 + 3, 1), (None, 0))

            sched_wld(0, guard_gu(0))
            sched_wld(1, guard_gu(1))
            for qd in range(4):
                for ici in range(4):
                    ic = 4 * qd + ici
                    for pn in range(2):
                        ph = ((ic * 2 + pn) % 2) * 1024
                        P = pSA[:, ph:ph + 1024]
                        fns = []
                        for k in range(8):
                            fns.append(lambda e, k=k, qd=qd, ici=ici, pn=pn, P=P:
                                e.matmul(P[:, 0:PAN],
                                         d_wg[qd % 2][:, k * PAN + ici * NP:
                                                      k * PAN + (ici + 1) * NP],
                                         d_h2b[:, k * KB + pn * PAN:
                                               k * KB + (pn + 1) * PAN],
                                         start=(k == 0), stop=(k == 7)))
                        for k in range(8):
                            fns.append(lambda e, k=k, qd=qd, ici=ici, pn=pn, P=P:
                                e.matmul(P[:, PAN:2 * PAN],
                                         d_wu[qd % 2][:, k * PAN + ici * NP:
                                                      k * PAN + (ici + 1) * NP],
                                         d_h2b[:, k * KB + pn * PAN:
                                               k * KB + (pn + 1) * PAN],
                                         start=(k == 0), stop=(k == 7)))
                        pk = ic * 2 + pn - 2
                        pguard = (prod_tok.get((b, e_i, pk), (None, 0)) if pk >= 0
                                  else prod_tok.get(
                                      (b, e_i - 1, 32 + pk) if e_i > 0
                                      else (b - 1, 8, 32 + pk), (None, 0)))
                        wmm = [wgld[(b, e_i, qd)], wuld[(b, e_i, qd)], h2b_ld[b],
                               c_end_v, pguard,
                               c_h2_dma[30], c_h2_dma[31],
                               c_x1_dma[30], c_x1_dma[31]]
                        tokm = sc.step("tensor", fns, waits=wmm, inc=s_pe)
                        gu_tok[(b, e_i, ic, pn)] = tokm
                        gslot = d_g[(ic * 2 + pn) % 2]
                        tsi = sc.step("scalar", lambda e, gslot=gslot, P=P:
                            e.activation(gslot[:], P[:, 0:PAN], AF.Silu),
                            waits=[tokm], inc=s_a)
                        tpr = sc.step("vector",
                            lambda e, gslot=gslot, P=P, ic=ic, pn=pn, dprod=dprod:
                            e.tensor_tensor(
                                dprod[:, ic * KB + pn * PAN: ic * KB + (pn + 1) * PAN],
                                gslot[:], P[:, PAN:2 * PAN], ALU.mult),
                            waits=[tsi, prodguard],
                            inc=s_v)
                        prod_tok[(b, e_i, ic * 2 + pn)] = tpr
                if qd + 2 <= 3:
                    sched_wld(qd + 2, gu_tok[(b, e_i, 4 * qd + 3, 1)])
            wdld[(b, e_i)] = sc.step("scalar", lambda e, e_i=e_i: e.dma_start(
                out=d_wd[:], in_=dwb[e_i]),
                waits=[dn_tok.get((b, e_i - 1, 7, 1) if e_i > 0 else (b - 1, 8, 7, 1),
                                  (None, 0)), c_end_pe, c_end_v, wcast_done,
                       c_h2_dma[30], c_h2_dma[31], c_x1_dma[30], c_x1_dma[31]],
                inc=s_wd, inc_k=16)
            prod_all = prod_tok[(b, e_i, 31)]
            for tc in range(8):
                for n in range(2):
                    Pd = (pSB if tc % 2 == 0 else pO)[:, n * PAN:(n + 1) * PAN]
                    fns = [
                        (lambda e, ic=ic, tc=tc, n=n, Pd=Pd, dprod=dprod: e.matmul(
                            Pd, dprod[:, ic * KB + tc * NP:ic * KB + (tc + 1) * NP],
                            d_wd[:, ic * KB + n * PAN: ic * KB + (n + 1) * PAN],
                            start=(ic == 0), stop=(ic == 15)))
                        for ic in range(16)
                    ]
                    wdn = [prod_all, wdld[(b, e_i)],
                           stt_tok.get((b, e_i, tc - 2, n), (None, 0))]
                    if tc % 2 == 1 and n == 1:
                        wdn.append(pX_guard[0] or (None, 0))
                    tokd = sc.step("tensor", fns, waits=wdn, inc=s_pe)
                    dn_tok[(b, e_i, tc, n)] = tokd
                    if shared:
                        tstt = sc.step("vector",
                            lambda e, tc=tc, n=n, Pd=Pd: e.scalar_tensor_tensor(
                                d_acc[:, tc * KB + n * PAN: tc * KB + (n + 1) * PAN],
                                Pd, 1.0,
                                d_acc[:, tc * KB + n * PAN: tc * KB + (n + 1) * PAN],
                                ALU.mult, ALU.add),
                            waits=[tokd, acc_ld[b], rt_w[8 * b + tc]], inc=s_v)
                    else:
                        tstt = sc.step("vector",
                            lambda e, tc=tc, n=n, Pd=Pd, b=b, e_i=e_i:
                            e.scalar_tensor_tensor(
                                d_acc[:, tc * KB + n * PAN: tc * KB + (n + 1) * PAN],
                                Pd, w_res[:, 8 * (8 * b + tc) + e_i:
                                          8 * (8 * b + tc) + e_i + 1],
                                d_acc[:, tc * KB + n * PAN: tc * KB + (n + 1) * PAN],
                                ALU.mult, ALU.add),
                            waits=[tokd, acc_ld[b], rt_w[8 * b + tc]], inc=s_v)
                    stt_tok[(b, e_i, tc, n)] = tstt
                    if tc % 2 == 1 and n == 1:
                        pX_guard[0] = tstt
            down_end[(b, e_i)] = dn_tok[(b, e_i, 7, 1)]
        y_st[b] = sc.step("sync", lambda e, b=b: e.dma_start(
            out=y_d[b * BT:(b + 1) * BT, :].rearrange("(u p) d -> p u d", p=NP),
            in_=d_acc[:].rearrange("p (u d) -> p u d", u=8)),
            waits=[stt_tok[(b, 8, 7, 1)]] +
                  [stt_tok[(b, ei, tc, n)] for ei in range(9)
                   for tc in range(8) for n in range(2)],
            inc=s_y, inc_k=16)

    return _finish(nc, sc, sems, debug and dict(
        QT_dbg=None, qt=None, AT_dbg=None, attn=None,
        MB_dbg=None, W_dbg=W_dbg, nacc=nacc, bias_t=bias_t, w_res=w_res),
        upto)


def _finish(nc, sc, sems, dbg, upto):
    s_y = sems["s_y"]
    s_v = sems["s_v"]
    s_a = sems["s_a"]
    s_pe = sems["s_pe"]
    if dbg:
        if dbg.get("qt") is not None:
            for m in range(NPAIR):
                sc.step("sync", lambda e, m=m: e.dma_start(
                    out=dbg["QT_dbg"][m], in_=dbg["qt"][m][:]),
                    waits=[(s_v, s_v.n), (s_pe, s_pe.n), (s_a, s_a.n)],
                    inc=s_y, inc_k=16)
        if dbg.get("attn") is not None:
            for m in range(NPAIR):
                sc.step("sync", lambda e, m=m: e.dma_start(
                    out=dbg["AT_dbg"][m], in_=dbg["attn"][m][:]),
                    waits=[(s_v, s_v.n), (s_pe, s_pe.n), (s_a, s_a.n)],
                    inc=s_y, inc_k=16)
        if dbg.get("MB_dbg") is not None:
            sc.step("sync", lambda e: e.dma_start(
                out=dbg["MB_dbg"], in_=dbg["nacc"][0:1, 0:8]),
                waits=[(s_v, s_v.n), (s_a, s_a.n)], inc=s_y, inc_k=16)
        if dbg.get("W_dbg") is not None:
            sc.step("sync", lambda e: e.dma_start(
                out=dbg["W_dbg"], in_=dbg["w_res"][:]),
                waits=[(s_v, s_v.n)], inc=s_y, inc_k=16)
    fy = s_y.n
    final_waits = [(sm, sm.n) for sm in sems["all_dma"] if sm.n > 0 and sm is not s_y]
    final_waits += [(s_v, s_v.n), (s_a, s_a.n), (s_pe, s_pe.n)]
    sc.step("sync", lambda e: e.wait_ge(s_y.h, fy), waits=final_waits)

    with nc.Block() as block:
        block.sync(lambda e: sc.replay("sync", e))
        block.tensor(lambda e: sc.replay("tensor", e))
        block.vector(lambda e: sc.replay("vector", e))
        block.scalar(lambda e: sc.replay("scalar", e))
        block.gpsimd(lambda e: sc.replay("gpsimd", e))
    return nc


def make_in_maps(inputs, n_cores=8):
    x = np.asarray(inputs["L0_attn_hidden_states"], dtype=np.float32)
    f32 = lambda a: np.ascontiguousarray(np.asarray(a, dtype=np.float32))

    def lay_proj(w):
        w = np.asarray(w, np.float32)
        return np.ascontiguousarray(
            w.reshape(8, NP, w.shape[1]).transpose(1, 0, 2).reshape(NP, -1))

    def lay_gu(w):
        w = np.asarray(w, np.float32).reshape(8, NP, I2).transpose(1, 0, 2)
        return np.ascontiguousarray(
            np.stack([w[:, :, qd * PAN:(qd + 1) * PAN].reshape(NP, -1)
                      for qd in range(4)]))

    def lay_dw(w):
        w = np.asarray(w, np.float32)
        return np.ascontiguousarray(
            w.reshape(16, NP, D).transpose(1, 0, 2).reshape(NP, -1))

    wq = lay_proj(inputs["L0_attn_Wq"])
    wkv = lay_proj(inputs["L0_attn_Wkv"])
    wo = lay_proj(inputs["L0_attn_Wo"])
    bo = f32(inputs["L0_attn_bo"]).reshape(1, D)
    gw_all = np.stack([lay_gu(np.asarray(inputs["L0_ffn_gate_w"], np.float32)[e])
                       for e in range(NE)] +
                      [lay_gu(np.asarray(inputs["L0_ffn_shared_gate"], np.float32).T)])
    uw_all = np.stack([lay_gu(np.asarray(inputs["L0_ffn_up_w"], np.float32)[e])
                       for e in range(NE)] +
                      [lay_gu(np.asarray(inputs["L0_ffn_shared_up"], np.float32).T)])
    dw_all = np.stack([lay_dw(np.asarray(inputs["L0_ffn_down_w"], np.float32)[e])
                       for e in range(NE)] +
                      [lay_dw(np.asarray(inputs["L0_ffn_shared_down"], np.float32).T)])
    rwT = f32(np.asarray(inputs["L0_ffn_router_w"], dtype=np.float32).T)
    rb = f32(inputs["L0_ffn_routing_bias"]).reshape(1, NE)
    idb = np.eye(NP, dtype=ml_dtypes.bfloat16)
    idf = np.eye(NP, dtype=np.float32)
    shared = dict(wq=wq, wkv=wkv, wo=wo, bo=bo,
                  gw=gw_all, uw=uw_all, dw=dw_all, rwT=rwT, rb=rb,
                  idb=idb, idf=idf)
    in_maps = []
    for c in range(n_cores):
        bb, hh = c // 2, c % 2
        m = dict(shared)
        m["x_own"] = np.ascontiguousarray(x[bb, hh * TOK:(hh + 1) * TOK, :])
        m["x_oth"] = np.ascontiguousarray(x[bb, (1 - hh) * TOK:(2 - hh) * TOK, :])
        in_maps.append(m)
    return in_maps


def kernel(**inputs):
    nc = build_program(upto="D", debug=False)
    in_maps = make_in_maps(inputs)
    res = run_bass_kernel_spmd(nc, in_maps, core_ids=list(range(8)))
    out = np.empty((4, SEQ, D), dtype=np.float32)
    for c in range(8):
        bb, hh = c // 2, c % 2
        out[bb, hh * TOK:(hh + 1) * TOK, :] = np.asarray(res.results[c]["y"])
    return out
